# version 25
# baseline (speedup 1.0000x reference)
import contextlib
import numpy as np
import ml_dtypes
import concourse.bass as bass
import concourse.mybir as mybir
from concourse.bass_utils import run_bass_kernel_spmd

F32 = mybir.dt.float32
BF16 = mybir.dt.bfloat16
I32 = mybir.dt.int32
F16 = mybir.dt.float16
AF = mybir.ActivationFunctionType
ALU = mybir.AluOpType
AX = mybir.AxisListType

C = 1024
NH = 16
HD = 64
CQ = 512
KV = 64
FF = 4096
NAV = 128
NE = 8
FE = 2048
LN_EPS = 1e-5
S4 = float(HD ** 0.25)
ANCHOR = 8.0

MAGIC = 2048.0


class Sem:

    def __init__(self, st, nc, name):
        self.h = st.enter_context(nc.semaphore(name))
        self.n = 0

    def inc(self, instr, k=1):
        instr.then_inc(self.h, k)
        self.n += k
        return instr


class Ctx:
    def __init__(self, nc, st, T):
        self.nc = nc
        self.st = st
        self.T = T
        self.NT = T // 128
        self.NS = T // 512


def sb(ctx, name, shape, dtype):
    return ctx.st.enter_context(ctx.nc.sbuf_tensor(name, shape, dtype))


def build(T=4096, phases=("l0a", "f0", "l1a", "f1")):
    nc = bass.Bass("TRN2", target_bir_lowering=False)
    st = contextlib.ExitStack()
    ctx = Ctx(nc, st, T)
    NT, NS = ctx.NT, ctx.NS

    din = {}

    def dram_in(name, shape, dtype=F32):
        din[name] = nc.dram_tensor(name, list(shape), dtype, kind="ExternalInput").ap()
        return din[name]

    x_in = dram_in("x", (T, C))
    out_hbm = nc.dram_tensor("out", [T, C], F32, kind="ExternalOutput").ap()

    dram_in("L0_wq_d", (CQ, C)); dram_in("L0_bq_d", (CQ,))
    dram_in("L0_w_qk", (C, CQ)); dram_in("L0_b_qk", (C,))
    dram_in("L0_wkv_d", (KV, C)); dram_in("L0_bkv_d", (KV,))
    dram_in("L0_wv_u", (C, KV)); dram_in("L0_bv_u", (C,))
    dram_in("L0_w_proj", (C, C)); dram_in("L0_b_proj", (C,))
    dram_in("F0_ln1_g", (C,)); dram_in("F0_ln1_b", (C,))
    dram_in("F0_w1", (FF, C)); dram_in("F0_b1", (FF,))
    dram_in("F0_ln2_g", (FF,)); dram_in("F0_ln2_b", (FF,))
    dram_in("F0_w2", (C, FF)); dram_in("F0_b2", (C,))
    dram_in("L1_Wq", (C, C)); dram_in("L1_bq", (C,))
    dram_in("L1_Wk", (C, C)); dram_in("L1_bk", (C,))
    dram_in("L1_Wv", (C, C)); dram_in("L1_bv", (C,))
    dram_in("L1_Wo", (C, C)); dram_in("L1_bo", (C,))
    dram_in("F1_Wp1", (C, NAV)); dram_in("F1_bp1", (NAV,))
    dram_in("F1_Wp2", (NAV, NE)); dram_in("F1_bp2", (NE,))
    dram_in("F1_W1", (NE, C, FE)); dram_in("F1_b1", (NE, FE))
    dram_in("F1_W2", (NE, FE, C)); dram_in("F1_b2", (NE, C))
    dram_in("F1_Wo", (C, C)); dram_in("F1_bo", (C,))
    dram_in("ident", (128, 128))
    dram_in("ident_bf", (128, 128), BF16)
    dram_in("ident_h", (128, 128), F16)
    dram_in("iota8", (1, 8))

    scr_xln = nc.dram_tensor("scr_xln", [C, T], BF16, kind="Internal").ap()
    scr_q = nc.dram_tensor("scr_q", [C, T], BF16, kind="Internal").ap()
    scr_k = nc.dram_tensor("scr_k", [C, T], BF16, kind="Internal").ap()
    scr_v = nc.dram_tensor("scr_v", [T, C], BF16, kind="Internal").ap()
    scr_wt = nc.dram_tensor("scr_wt", [C, C], BF16, kind="Internal").ap()
    scr_qw1 = nc.dram_tensor("scr_qw1", [C, FF], BF16, kind="Internal").ap()
    scr_qw2 = nc.dram_tensor("scr_qw2", [FF, C], BF16, kind="Internal").ap()
    scr_w1s = nc.dram_tensor("scr_w1s", [C, FE], BF16, kind="Internal").ap()
    scr_w2s = nc.dram_tensor("scr_w2s", [FE, C], BF16, kind="Internal").ap()
    scr_wo = nc.dram_tensor("scr_wo", [C, C], BF16, kind="Internal").ap()

    XT = sb(ctx, "XT", [128, 8, T], F32)
    rows = sb(ctx, "rows", [1, 5, 512], F32)
    ones_f1 = sb(ctx, "ones_f1", [1, 128], F32)
    ones_b1 = sb(ctx, "ones_b1", [1, 128], BF16)
    ones_c128 = sb(ctx, "ones_c128", [128, 1], F32)
    ones_c128b = sb(ctx, "ones_c128b", [128, 1], BF16)
    negones64 = sb(ctx, "negones64", [64, 1], BF16)
    ident_f = sb(ctx, "ident_f", [128, 128], F32)
    ident_h = sb(ctx, "ident_h_sb", [128, 128], F16)
    bias_pool = sb(ctx, "bias_pool", [128, 256], F32)
    BIAS = {}

    PS = ctx.st.enter_context(nc.psum_tensor("PS", [128, 8, 512], F32))
    PS16 = PS.bitcast(F16)

    sd = Sem(st, nc, "sd")
    sa = Sem(st, nc, "sa")
    sg = Sem(st, nc, "sg")
    stt = Sem(st, nc, "stt")
    sv = Sem(st, nc, "sv")
    ss = Sem(st, nc, "ss")
    sc_ = Sem(st, nc, "sc_")
    sq_ = Sem(st, nc, "sq_")
    sw_ = Sem(st, nc, "sw_")

    blk = st.enter_context(nc.Block())
    t_, v_, s_, sy_, g_ = nc.tensor, nc.vector, nc.scalar, nc.sync, nc.gpsimd
    _names = {}

    def sbt(name, shape, dtype):
        k = _names.get(name, 0)
        _names[name] = k + 1
        return nc.sbuf_tensor(name if k == 0 else name + "__" + str(k), shape, dtype)

    def bank(b, n=1, p0=0, p1=128):
        return PS[p0:p1, b:b + n, :]

    def wait_all(eng):
        eng.wait_ge(sc_.h, sc_.n)
        eng.wait_ge(sq_.h, sq_.n)
        eng.wait_ge(sw_.h, sw_.n)
        eng.wait_ge(stt.h, stt.n)
        eng.wait_ge(sv.h, sv.n)
        eng.wait_ge(ss.h, ss.n)
        eng.wait_ge(sd.h, sd.n)
        eng.wait_ge(sa.h, sa.n)
        eng.wait_ge(sg.h, sg.n)

    def barrier():
        for eng in (t_, v_, s_, sy_, g_):
            wait_all(eng)

    sc_.inc(sy_.dma_start(ident_f[:, :], din["ident"]), 16)
    sc_.inc(sy_.dma_start(ident_h[:, :], din["ident_h"]), 16)

    bias_col = [0]

    def load_bias(name, vec_ap, n, queue=sy_, qsem=sc_):
        c0 = bias_col[0]
        k = (n + 127) // 128
        assert n % 128 == 0 or n < 128
        ap = vec_ap.rearrange("(k p) -> p k", p=min(n, 128)) if n >= 128 else vec_ap
        if n >= 128:
            qsem.inc(queue.dma_start(bias_pool[:, c0:c0 + k], ap, allow_slow_non_contiguous=True), 16)
        else:
            qsem.inc(queue.dma_start(bias_pool[0:n, c0:c0 + 1], vec_ap.rearrange("(n o) -> n o", o=1), allow_slow_non_contiguous=True), 16)
        BIAS[name] = (c0, k)
        bias_col[0] += k

    def load_bias64(name, vec_ap, n):
        c0 = bias_col[0]
        k = n // 64
        ap = vec_ap.rearrange("(h d) -> d h", d=64)
        sc_.inc(sy_.dma_start(bias_pool[0:64, c0:c0 + k], ap,
                              allow_slow_non_contiguous=True), 16)
        BIAS[name + "_64"] = (c0, k)
        bias_col[0] += k

    def bias64_slice(name, h):
        c0, k = BIAS[name + "_64"]
        return bias_pool[0:64, c0 + h:c0 + h + 1]

    for nm in ["L0_bq_d", "L0_b_qk", "L0_bkv_d", "L0_bv_u", "L0_b_proj",
               "F0_ln1_g", "F0_ln1_b", "F0_b2",
               "L1_bq", "L1_bk", "L1_bv", "L1_bo", "F1_bo"]:
        n = din[nm].tensor.shape[0]
        load_bias(nm, din[nm], n)
    for nm in ["F0_b1", "F0_ln2_g", "F0_ln2_b"]:
        load_bias(nm, din[nm], FF)
    load_bias64("L0_b_qk", din["L0_b_qk"], C)

    def bias_slice(name, j, p0=0, p1=128):
        c0, k = BIAS[name]
        return bias_pool[p0:p1, c0 + j:c0 + j + 1]

    consts = sb(ctx, "consts", [128, 8], F32)
    cc_eps = sb(ctx, "cc_eps", [128, 1], F32)
    v_.memset(cc_eps[:, :], LN_EPS)
    v_.memset(consts[:, 0:1], -ANCHOR)
    v_.memset(consts[:, 1:2], 0.0)
    v_.memset(ones_f1[:, :], 1.0)
    v_.memset(ones_b1[:, :], 1.0)
    v_.memset(ones_c128[:, :], 1.0)
    v_.memset(ones_c128b[:, :], 1.0)
    v_.memset(negones64[:, :], -1.0)
    v_.memset(rows[:, :, :], 0.0)
    sv.inc(v_.drain())

    v_.wait_ge(sc_.h, sc_.n)
    for nm, scale in [("L1_bq", 1.0 / S4), ("L1_bk", 1.0 / S4), ("L0_b_qk", 0.125),
                      ("F1_bo", 0.3)]:
        c0, k = BIAS[nm]
        c1 = bias_col[0]
        v_.tensor_scalar(bias_pool[:, c1:c1 + k], bias_pool[:, c0:c0 + k], scale, None,
                         op0=ALU.mult)
        BIAS[nm + ("_s3" if nm == "F1_bo" else "_s")] = (c1, k)
        bias_col[0] += k
    c0, k = BIAS["L0_b_qk_64"]
    c1 = bias_col[0]
    v_.tensor_scalar(bias_pool[0:64, c1:c1 + k], bias_pool[0:64, c0:c0 + k], 0.125, None,
                     op0=ALU.mult)
    BIAS["L0_b_qk_s_64"] = (c1, k)
    bias_col[0] += k
    sv.inc(v_.drain())

    with sbt("xin_stage", [128, C], F32) as xin_stage:
      for tc in range(NT):
          sd.inc(sy_.dma_start(xin_stage[:, :], x_in[tc * 128:(tc + 1) * 128, :]), 16)
          t_.wait_ge(sd.h, sd.n)
          t_.wait_ge(sc_.h, sc_.n)
          if tc > 0:
              t_.wait_ge(ss.h, ss.n)
          for cc in range(8):
              b = cc // 4
              off = cc % 4
              tgt = PS[:, b, :].rearrange("p (o q) -> p o q", o=4)[:, off, :]
              ins = t_.transpose(tgt, xin_stage[:, cc * 128:(cc + 1) * 128], ident_f[:, :])
          stt.inc(ins)
          s_.wait_ge(stt.h, stt.n)
          for cc in range(8):
              b = cc // 4
              off = cc % 4
              src = PS[:, b, :].rearrange("p (o q) -> p o q", o=4)[:, off, :]
              ins = s_.copy(XT[:, cc, tc * 128:(tc + 1) * 128], src)
          ss.inc(ins)
          sy_.wait_ge(stt.h, stt.n)


    barrier()

    def ln_and_xln():
        with contextlib.ExitStack() as ph:
            xst = [ph.enter_context(sbt(f"xls_ln{i}", [128, 8, 512], BF16))
                   for i in range(2)]
            xl = [ph.enter_context(sbt(f"xl_ln{i}", [128, 512], F32)) for i in range(4)]
            sq2 = [ph.enter_context(sbt(f"sq_ln{i}", [128, 512], F32)) for i in range(2)]
            rows2 = ph.enter_context(sbt("rows2_ln", [1, 5, 512], F32))
            R = [rows, rows2]
            BK = {0: (6, 7, 4, 5), 1: (2, 3, 0, 1)}
            lnss = [0, 0]
            lnsv = [0, 0]
            lndma = [0, 0]
            sqrd = {0: 0, 1: 0}
            mmN, mmB = {}, {}

            def stA(t):
                tl = slice(t * 512, (t + 1) * 512)
                p = t % 2
                s6, s7, _, _ = BK[p]
                for cc in range(8):
                    v_.wait_ge(stt.h, sqrd[cc % 2])
                    sv.inc(v_.tensor_tensor(sq2[cc % 2][:, :], XT[:, cc, tl],
                                            XT[:, cc, tl], op=ALU.mult))
                    t_.wait_ge(sv.h, sv.n)
                    if cc == 0:
                        t_.wait_ge(ss.h, lnss[p])
                    stt.inc(t_.matmul(bank(s6, 1, 0, 1), ones_c128[:, :], XT[:, cc, tl],
                                      start=(cc == 0), stop=(cc == 7)))
                    stt.inc(t_.matmul(bank(s7, 1, 0, 1), ones_c128[:, :], sq2[cc % 2][:, :],
                                      start=(cc == 0), stop=(cc == 7)))
                    sqrd[cc % 2] = stt.n
                mmN[t] = stt.n

            def stB(t):
                p = t % 2
                s6, s7, b4, b5 = BK[p]
                R_ = R[p]
                v_.wait_ge(stt.h, mmN[t])
                v_.tensor_scalar(R_[:, 0, :], bank(s6, 1, 0, 1), 1.0 / C, None,
                                 op0=ALU.mult)
                sv.inc(v_.drain())
                s_.wait_ge(sv.h, sv.n)
                ss.inc(s_.activation(R_[:, 4, :], R_[:, 0, :], AF.Square))
                ss.inc(s_.activation(R_[:, 3, :], bank(s7, 1, 0, 1), AF.Identity,
                                     scale=1.0 / C, bias=cc_eps[0:1, 0:1]))
                lnss[p] = ss.n
                v_.wait_ge(ss.h, ss.n)
                sv.inc(v_.tensor_tensor(R_[:, 3, :], R_[:, 3, :], R_[:, 4, :],
                                        op=ALU.subtract))
                s_.wait_ge(sv.h, sv.n)
                ss.inc(s_.activation(R_[:, 4, :], R_[:, 3, :], AF.Ln))
                ss.inc(s_.drain())
                ss.inc(s_.activation(R_[:, 1, :], R_[:, 4, :], AF.Exp, scale=-0.5))
                v_.wait_ge(ss.h, ss.n)
                sv.inc(v_.tensor_tensor(R_[:, 2, :], R_[:, 0, :], R_[:, 1, :],
                                        op=ALU.mult))
                sv.inc(v_.drain())
                if t == NS - 1:
                    sv.inc(v_.tensor_copy(consts[0:1, 2:3], R_[0:1, 0, 511:512]))
                    sv.inc(v_.tensor_copy(consts[0:1, 3:4], R_[0:1, 1, 511:512]))
                    sv.inc(v_.drain())
                t_.wait_ge(sv.h, sv.n)
                t_.wait_ge(sv.h, lnsv[p])
                stt.inc(t_.matmul(bank(b4, 1), ones_f1[:, :], R_[:, 1, :],
                                  start=True, stop=True))
                stt.inc(t_.matmul(bank(b5, 1), ones_f1[:, :], R_[:, 2, :],
                                  start=True, stop=True))
                mmB[t] = stt.n

            def stC(t):
                tl = slice(t * 512, (t + 1) * 512)
                p = t % 2
                _, _, b4, b5 = BK[p]
                v_.wait_ge(stt.h, mmB[t])
                v_.wait_ge(sd.h, lndma[p])
                for cc in range(8):
                    sv.inc(v_.tensor_tensor(xl[cc % 4][:, :], XT[:, cc, tl], bank(b4, 1),
                                            op=ALU.mult))
                    if cc % 4 == 3:
                        sv.inc(v_.drain())
                        for cc2 in range(cc - 3, cc + 1):
                            sv.inc(v_.tensor_tensor(xst[p][:, cc2, :], xl[cc2 % 4][:, :],
                                                    bank(b5, 1), op=ALU.subtract))
                        sv.inc(v_.drain())
                lnsv[p] = sv.n
                sy_.wait_ge(sv.h, sv.n)
                sy_.wait_ge(sd.h, sd.n)
                dst = scr_xln.rearrange("(k p) t -> p k t", p=128)[:, :, tl]
                sd.inc(sy_.dma_start(dst, xst[p][:, :, :]), 16)
                lndma[p] = sd.n

            stA(0)
            stB(0)
            for t in range(NS):
                if t + 1 < NS:
                    stA(t + 1)
                stC(t)
                if t + 1 < NS:
                    stB(t + 1)
            barrier()

    _wtp_id = [0]

    def wt_prep(w_ap, rows_n, cols_n, dst_sb, scale=1.0):
        nr, nc_ = (rows_n + 127) // 128, (cols_n + 127) // 128
        _wtp_id[0] += 1
        _wst = contextlib.ExitStack()
        wnat = _wst.enter_context(
            sbt("wnat%d" % _wtp_id[0], [128, cols_n], F32))
        for rr in range(nr):
            h_ = min(128, rows_n - rr * 128)
            sy_.wait_ge(stt.h, stt.n)
            sd.inc(sy_.dma_start(wnat[0:h_, 0:cols_n], w_ap[rr * 128:rr * 128 + h_, :]), 16)
            t_.wait_ge(sd.h, sd.n)
            t_.wait_ge(ss.h, ss.n)
            for jj in range(nc_):
                b = jj % 2
                off = jj // 2 % 2
                w_ = min(128, cols_n - jj * 128)
                tgt = PS[:, b, :].rearrange("p (o q) -> p o q", o=4)[0:w_, off, 0:h_]
                stt.inc(t_.transpose(tgt, wnat[0:h_, jj * 128:jj * 128 + w_],
                                     ident_f[0:h_, 0:h_]))
                if jj % 4 == 3 or jj == nc_ - 1:
                    s_.wait_ge(stt.h, stt.n)
                    for jj2 in range(jj - jj % 4, jj + 1):
                        b2 = jj2 % 2
                        off2 = jj2 // 2 % 2
                        w2 = min(128, cols_n - jj2 * 128)
                        src = PS[:, b2, :].rearrange("p (o q) -> p o q", o=4)[0:w2, off2, 0:h_]
                        ins = s_.activation(dst_sb[0:w2, jj2, rr * 128:rr * 128 + h_], src,
                                            AF.Identity, scale=scale)
                    ss.inc(ins)
                    t_.wait_ge(ss.h, ss.n)
        barrier()
        _wst.close()

    def wt_to_scr(w_name, scale=1.0):
        with contextlib.ExitStack() as ph:
            wt_full = ph.enter_context(sbt("wtf_" + w_name, [128, 8, C], BF16))
            wt_prep(din[w_name], C, C, wt_full, scale=scale)
            barrier()
            sd.inc(sy_.dma_start(scr_wt.rearrange("(k p) c -> p k c", p=128),
                                 wt_full[:, :, :]), 16)
            barrier()

    def epilogue():
        barrier()
        xout_stage = sb(ctx, "xout_stage", [128, C], F32)
        for tc in range(NT):
            t_.wait_ge(sd.h, sd.n)
            for cc in range(8):
                b = cc // 4
                off = cc % 4
                tgt = PS[:, b, :].rearrange("p (o q) -> p o q", o=4)[:, off, :]
                ins = t_.transpose(tgt, XT[:, cc, tc * 128:(tc + 1) * 128], ident_f[:, :])
            stt.inc(ins)
            s_.wait_ge(stt.h, stt.n)
            for cc in range(8):
                b = cc // 4
                off = cc % 4
                src = PS[:, b, :].rearrange("p (o q) -> p o q", o=4)[:, off, :]
                ins = s_.copy(xout_stage[:, cc * 128:(cc + 1) * 128], src)
            ss.inc(ins)
            sy_.wait_ge(ss.h, ss.n)
            sd.inc(sy_.dma_start(out_hbm[tc * 128:(tc + 1) * 128, :], xout_stage[:, :]), 16)
            t_.wait_ge(ss.h, ss.n)
        sy_.wait_ge(sd.h, sd.n)

    xadd_state = {"done": 0}
    wtp = [sb(ctx, "wtp0", [128, C], BF16), sb(ctx, "wtp1", [128, C], BF16)]
    wtp_last = [0, 0]

    def proj_pair(bo_name, pair, first, y_pair, wn=None):
        slab = wtp[pair % 2]
        sy_.wait_ge(stt.h, wtp_last[pair % 2])
        sy_.wait_ge(sw_.h, sw_.n)
        sw_.inc(sy_.dma_start(slab[:, :], scr_wt[pair * 128:(pair + 1) * 128, :]), 16)
        t_.wait_ge(sw_.h, sw_.n)
        nts = y_pair.shape[-1] // 512
        toff = 0 if wn is None else wn * (y_pair.shape[-1])
        t_.wait_ge(sv.h, sv.n)
        t_.wait_ge(ss.h, ss.n)
        add_done = {0: 0, 1: 0}
        for cc in range(8):
            for ts2 in range(nts):
                b = (cc * nts + ts2) % 2
                t_.wait_ge(sv.h, add_done[b])
                stt.inc(t_.matmul(bank(6 + b), slab[:, cc * 128:(cc + 1) * 128],
                                  y_pair[:, ts2 * 512:(ts2 + 1) * 512],
                                  start=True, stop=True))
                v_.wait_ge(stt.h, stt.n)
                tl2 = slice(toff + ts2 * 512, toff + (ts2 + 1) * 512)
                if first:
                    ins = v_.scalar_tensor_tensor(XT[:, cc, tl2], bank(6 + b),
                                                  bias_slice(bo_name, cc), XT[:, cc, tl2],
                                                  op0=ALU.add, op1=ALU.add)
                else:
                    ins = v_.tensor_tensor(XT[:, cc, tl2], XT[:, cc, tl2], bank(6 + b),
                                           op=ALU.add)
                sv.inc(ins)
                add_done[b] = sv.n
        wtp_last[pair % 2] = stt.n
        sv.inc(v_.drain())
        xadd_state["done"] = sv.n

    def mha_qkv():
        with contextlib.ExitStack() as ph:
            wt_q = ph.enter_context(sbt("wt_q", [128, 8, C], BF16))
            wt_k = ph.enter_context(sbt("wt_k", [128, 8, C], BF16))
            stage_qk = ph.enter_context(sbt("stage_qk", [128, 4, 512], BF16))
            xls2 = [ph.enter_context(sbt(f"xls_qkv{i}", [128, 8, 512], BF16))
                    for i in range(2)]
            vstage = ph.enter_context(sbt("vstage", [128, 2, 1024], BF16))
            wt_prep(din["L1_Wq"], C, C, wt_q, scale=1.0 / S4)
            wt_prep(din["L1_Wk"], C, C, wt_k, scale=1.0 / S4)
            wt_v = wt_q
            barrier()
            xdma = [0, 0]
            xread = [0, 0]

            def x_issue(ts):
                buf = ts % 2
                sy_.wait_ge(stt.h, xread[buf])
                sy_.wait_ge(sq_.h, sq_.n)
                src = scr_xln.rearrange("(k p) t -> p k t", p=128)[
                    :, :, ts * 512:(ts + 1) * 512]
                sq_.inc(sy_.dma_start(xls2[buf][:, :, :], src), 16)
                xdma[buf] = sq_.n

            evac_done = [0, 0, 0, 0]
            x_issue(0)
            for tsub in range(NS):
                tl = slice(tsub * 512, (tsub + 1) * 512)
                xls = xls2[tsub % 2]
                if tsub + 1 < NS:
                    x_issue(tsub + 1)
                t_.wait_ge(sq_.h, xdma[tsub % 2])
                s_.wait_ge(sd.h, sd.n)
                for mat, wt, bname, scr in ((0, wt_q, "L1_bq_s", scr_q),
                                            (1, wt_k, "L1_bk_s", scr_k)):
                    s_.wait_ge(sd.h, sd.n)
                    for co in range(8):
                        b = co % 2
                        if co == 4:
                            s_.wait_ge(sd.h, sd.n)
                        t_.wait_ge(ss.h, evac_done[b])
                        for ci in range(8):
                            ins = t_.matmul(bank(b), wt[:, ci, co * 128:(co + 1) * 128],
                                            xls[:, ci, :], start=(ci == 0), stop=(ci == 7))
                        stt.inc(ins)
                        s_.wait_ge(stt.h, stt.n)
                        ss.inc(s_.activation(stage_qk[:, co % 4, :], bank(b), AF.Identity,
                                             bias=bias_slice(bname, co)))
                        evac_done[b] = ss.n
                        if co % 4 == 3:
                            sy_.wait_ge(ss.h, ss.n)
                            h4 = co // 4
                            dstm = scr.rearrange("(k p) t -> p k t",
                                                 p=128)[:, h4 * 4:h4 * 4 + 4, tl]
                            sd.inc(sy_.dma_start(dstm, stage_qk[:, :, :]), 16)
                xread[tsub % 2] = stt.n
            barrier()
            wt_prep(din["L1_Wv"], C, C, wt_v, scale=1.0)
            barrier()
            xdma = [0, 0]
            xread = [0, 0]
            x_issue2 = x_issue
            x_issue2(0)
            for tsub in range(NS):
                tl = slice(tsub * 512, (tsub + 1) * 512)
                xls = xls2[tsub % 2]
                if tsub + 1 < NS:
                    x_issue2(tsub + 1)
                t_.wait_ge(sq_.h, xdma[tsub % 2])
                s_.wait_ge(sd.h, sd.n)
                for tci in range(4):
                    if tci == 2:
                        s_.wait_ge(sd.h, sd.n)
                    for ch in range(2):
                        b = 2 + ch
                        t_.wait_ge(ss.h, evac_done[b])
                        for ci in range(8):
                            ins = t_.matmul(bank(b), xls[:, ci, tci * 128:(tci + 1) * 128],
                                            wt_v[:, ci, ch * 512:(ch + 1) * 512],
                                            start=(ci == 0), stop=(ci == 7))
                        stt.inc(ins)
                        s_.wait_ge(stt.h, stt.n)
                        ss.inc(s_.activation(vstage[:, tci % 2, ch * 512:(ch + 1) * 512],
                                             bank(b), AF.Identity))
                        evac_done[b] = ss.n
                    if tci % 2 == 1:
                        sy_.wait_ge(ss.h, ss.n)
                        t0 = tsub * 512 + (tci - 1) * 128
                        sd.inc(sy_.dma_start(
                            scr_v[t0:t0 + 256, :].rearrange("(a p) c -> p a c", p=128),
                            vstage[:, :, :]), 16)
                xread[tsub % 2] = stt.n
            barrier()

    def mha_attn_core(wo_name, bo_name, bv_name):
        NC_ = ctx.NT
        NW = T // 1024
        wt_to_scr(wo_name)
        with contextlib.ExitStack() as ph:
            kaug = ph.enter_context(sbt("kaug", [65, T], BF16))
            qaug2 = [ph.enter_context(sbt(f"qaug_{i}", [65, T], BF16))
                     for i in range(2)]
            v4 = ph.enter_context(sbt("v4", [128, NC_, 2, 66], BF16))
            e_sb = [ph.enter_context(sbt(f"e_sb{i}", [128, 2048], BF16))
                    for i in range(2)]
            qk_tmp = ph.enter_context(sbt("qk_tmp", [64, 2048], BF16))
            den_row = ph.enter_context(sbt("den_row", [1, 1024], F32))
            u_sb = ph.enter_context(sbt("u_sb", [64, 1024], F32))
            y_pair = ph.enter_context(sbt("y_pair", [128, T], BF16))
            v_.memset(v4[:, :, :, 64:65], 1.0)
            v_.memset(kaug[64:65, :], 1.0)
            sv.inc(v_.drain())
            barrier()
            exp_prev = [0]
            pvd = {0: 0, 1: 0}
            v4_ready = [0]
            ep_state = {0: 0, 1: 0}
            qread = [0, 0]
            anch_mm = [0]
            bdma = [0]
            for h in range(NH):
                qaug = qaug2[h % 2]
                if h % 2 == 0:
                    srcv = scr_v.rearrange("(c p) (g d) -> p c g d", p=128, d=64)
                    sy_.wait_ge(stt.h, stt.n)
                    sy_.wait_ge(sd.h, sd.n)
                    for g2 in range(2):
                        sd.inc(sy_.dma_start(v4[:, :, g2, 0:64],
                                             srcv[:, :, h + g2, :]), 16)
                    v4_ready[0] = sd.n
                sy_.wait_ge(stt.h, qread[(h + 1) % 2])
                sy_.wait_ge(sq_.h, sq_.n)
                sq_.inc(sy_.dma_start(kaug[0:64, :],
                                      scr_k[h * 64:(h + 1) * 64, :]), 16)
                if h == 0:
                    sq_.inc(sy_.dma_start(qaug[0:64, :], scr_q[0:64, :]), 16)
                if h + 1 < NH:
                    sq_.inc(sy_.dma_start(qaug2[(h + 1) % 2][0:64, :],
                                          scr_q[(h + 1) * 64:(h + 2) * 64, :]), 16)
                bdma[0] = sq_.n
                t_.wait_ge(sd.h, v4_ready[0])
                for hf2 in range(max(1, T // 2048)):
                    hl = slice(hf2 * 2048, min((hf2 + 1) * 2048, T))
                    v_.wait_ge(sq_.h, bdma[0])
                    v_.wait_ge(stt.h, anch_mm[0])
                    sv.inc(v_.tensor_tensor(qk_tmp[:, :], qaug[0:64, hl], kaug[0:64, hl],
                                            op=ALU.mult))
                    t_.wait_ge(sv.h, sv.n)
                    if hf2 == 0:
                        t_.wait_ge(ss.h, ss.n)
                    for j4 in range(4):
                        stt.inc(t_.matmul(PS[0:1, 4 + (j4 % 2) * 2 + j4 // 2, :],
                                          negones64[:, :],
                                          qk_tmp[:, j4 * 512:(j4 + 1) * 512],
                                          start=True, stop=True))
                    anch_mm[0] = stt.n
                    s_.wait_ge(stt.h, stt.n)
                    for j4 in range(4):
                        ins = s_.activation(qaug[64:65, hf2 * 2048 + j4 * 512:
                                                 hf2 * 2048 + (j4 + 1) * 512],
                                            PS[0:1, 4 + (j4 % 2) * 2 + j4 // 2, :],
                                            AF.Identity, bias=consts[0:1, 0:1])
                    ss.inc(ins)
                    t_.wait_ge(ss.h, ss.n)
                for w in range(NW):
                    wl = slice(w * 1024, (w + 1) * 1024)

                    def emit_qk(sc):
                        sb_ = (sc % 2) * 2
                        if sc % 2 == 0:
                            t_.wait_ge(ss.h, exd.get(sc // 2 - 1, exp_prev[0]))
                        if sc < 2 and w == 0:
                            t_.wait_ge(sv.h, xadd_state["done"])
                        for half in range(2):
                            ins = t_.matmul(PS[:, sb_ + half, :],
                                            kaug[:, sc * 128:(sc + 1) * 128],
                                            qaug[:, w * 1024 + half * 512:
                                                 w * 1024 + (half + 1) * 512],
                                            start=True, stop=True)
                        stt.inc(ins)
                        qk_done[sc] = stt.n

                    def emit_exp2(k):
                        s_.wait_ge(stt.h, qk_done[2 * k + 1])
                        s_.wait_ge(stt.h, pvd[k % 2])
                        ss.inc(s_.activation(e_sb[k % 2][:, :],
                                             bank(0, 4).rearrange("p a t -> p (a t)"),
                                             AF.Exp))
                        exd[k] = ss.n

                    pb = 4 + (w % 2) * 2

                    def emit_pv(sc):
                        t_.wait_ge(ss.h, exd[sc // 2])
                        if sc == 0:
                            t_.wait_ge(sv.h, ep_state[w % 2])
                        k2 = sc // 2
                        off = (sc % 2) * 1024
                        for half in range(2):
                            ins = t_.matmul(PS[0:65, pb + half, :],
                                            v4[:, sc, h % 2, 0:65],
                                            e_sb[k2 % 2][:, off + half * 512:
                                                         off + (half + 1) * 512],
                                            start=(sc == 0), stop=(sc == NC_ - 1))
                        stt.inc(ins)
                        pvd[k2 % 2] = stt.n

                    qk_done, exd = {}, {}
                    emit_qk(0)
                    emit_qk(1)
                    emit_exp2(0)
                    for k in range(1, NC_ // 2):
                        emit_qk(2 * k)
                        emit_qk(2 * k + 1)
                        emit_exp2(k)
                        emit_pv(2 * k - 2)
                        emit_pv(2 * k - 1)
                    emit_pv(NC_ - 2)
                    emit_pv(NC_ - 1)
                    exp_prev[0] = exd[NC_ // 2 - 1]
                    s_.wait_ge(stt.h, stt.n)
                    ss.inc(s_.copy(den_row[0:1, :],
                                   PS[64:65, pb:pb + 2, :].rearrange("p a t -> p (a t)")))
                    v_.wait_ge(ss.h, ss.n)
                    sv.inc(v_.reciprocal(den_row[0:1, :], den_row[0:1, :]))
                    s_.wait_ge(sv.h, sv.n)
                    ss.inc(s_.copy(u_sb[:, :],
                                   PS[0:64, pb:pb + 2, :].rearrange("p a t -> p (a t)")))
                    t_.wait_ge(ss.h, ss.n)
                    stt.inc(t_.matmul(PS[0:64, pb, :], ones_f1[0:1, 0:64],
                                      den_row[0:1, 0:512], start=True, stop=True))
                    stt.inc(t_.matmul(PS[0:64, pb + 1, :], ones_f1[0:1, 0:64],
                                      den_row[0:1, 512:1024], start=True, stop=True))
                    v_.wait_ge(stt.h, stt.n)
                    sv.inc(v_.tensor_tensor(u_sb[:, :], u_sb[:, :],
                                            PS[0:64, pb:pb + 2, :].rearrange("p a t -> p (a t)"),
                                            op=ALU.mult))
                    ep_state[w % 2] = sv.n
                    s_.wait_ge(sv.h, sv.n)
                    p0 = (h % 2) * 64
                    ss.inc(s_.activation(y_pair[p0:p0 + 64, wl], u_sb[:, :], AF.Identity,
                                         bias=bias_slice(bv_name, h // 2, p0, p0 + 64)))
                qread[h % 2] = stt.n
                if h % 2 == 1:
                    proj_pair(bo_name, h // 2, h == 1, y_pair)
            barrier()

    def mla_attn():
        NC_ = ctx.NT
        NW = T // 1024
        wt_to_scr("L0_w_proj")
        with contextlib.ExitStack() as ph:
            ckvT = ph.enter_context(sbt("ckvT", [65, T], BF16))
            ckv_tok = ph.enter_context(sbt("ckv_tok", [128, NC_, 66], BF16))
            wvuT = ph.enter_context(sbt("wvuT", [65, C], BF16))
            bkv_row = ph.enter_context(sbt("bkv_row", [1, 64], F32))
            tok_tmp = ph.enter_context(sbt("tok_tmp", [128, 64], F32))
            with contextlib.ExitStack() as phw:
                wvuT_tmp = phw.enter_context(sbt("wvuT_tmp", [128, 1, C], BF16))
                wt_prep(din["L0_wv_u"], C, KV, wvuT_tmp, scale=1.0)
                ss.inc(s_.drain())
                ss.inc(s_.copy(wvuT[0:64, :], wvuT_tmp[0:64, 0, :]))
                barrier()
            sg.inc(g_.dma_start(wvuT[64:65, :],
                                din["L0_bv_u"].rearrange("(o n) -> o n", o=1)), 16)
            sc_.inc(sy_.dma_start(bkv_row[0:1, :],
                                  din["L0_bkv_d"].rearrange("(o n) -> o n", o=1)), 16)
            v_.memset(ckv_tok[:, :, 64:65], 1.0)
            v_.memset(ckvT[64:65, :], 1.0)
            sv.inc(v_.drain())
            barrier()
            with contextlib.ExitStack() as ph2:
                wqdT = ph2.enter_context(sbt("wqdT", [128, 8, CQ], BF16))
                wqkT = ph2.enter_context(sbt("wqkT", [128, 4, C], BF16))
                wkvdT = ph2.enter_context(sbt("wkvdT", [128, 8, KV], BF16))
                xls = ph2.enter_context(sbt("xls_mla", [128, 8, 512], BF16))
                cqc = ph2.enter_context(sbt("cqc", [128, 4, 512], BF16))
                stq = ph2.enter_context(sbt("stq", [128, 8, 512], BF16))
                wt_prep(din["L0_wq_d"], CQ, C, wqdT, scale=1.0)
                wt_prep(din["L0_w_qk"], C, CQ, wqkT, scale=0.125)
                wt_prep(din["L0_wkv_d"], KV, C, wkvdT, scale=1.0)
                barrier()
                evac_done = [0, 0, 0, 0]
                t_.wait_ge(sc_.h, sc_.n)
                stt.inc(t_.matmul(PS[:, 5, 0:64], ones_f1[0:1, :], bkv_row[0:1, :],
                                  start=True, stop=True))
                for tsub in range(NS):
                    tl = slice(tsub * 512, (tsub + 1) * 512)
                    sy_.wait_ge(stt.h, stt.n)
                    src = scr_xln.rearrange("(k p) t -> p k t", p=128)[:, :, tl]
                    sd.inc(sy_.dma_start(xls[:, :, :], src), 16)
                    t_.wait_ge(sd.h, sd.n)
                    s_.wait_ge(sd.h, sd.n)
                    for co in range(4):
                        b = co % 2
                        t_.wait_ge(ss.h, evac_done[b])
                        for ci in range(8):
                            ins = t_.matmul(bank(b), wqdT[:, ci, co * 128:(co + 1) * 128],
                                            xls[:, ci, :], start=(ci == 0), stop=(ci == 7))
                        stt.inc(ins)
                        s_.wait_ge(stt.h, stt.n)
                        ss.inc(s_.activation(cqc[:, co, :], bank(b), AF.Identity,
                                             bias=bias_slice("L0_bq_d", co)))
                        evac_done[b] = ss.n
                    t_.wait_ge(ss.h, ss.n)
                    for co in range(8):
                        b = co % 2
                        t_.wait_ge(ss.h, evac_done[b])
                        for ci in range(4):
                            ins = t_.matmul(bank(b), wqkT[:, ci, co * 128:(co + 1) * 128],
                                            cqc[:, ci, :], start=(ci == 0), stop=(ci == 3))
                        stt.inc(ins)
                        s_.wait_ge(stt.h, stt.n)
                        ss.inc(s_.activation(stq[:, co, :], bank(b), AF.Identity,
                                             bias=bias_slice("L0_b_qk_s", co)))
                        evac_done[b] = ss.n
                    sy_.wait_ge(ss.h, ss.n)
                    dstq = scr_q.rearrange("(k p) t -> p k t", p=128)[:, :, tl]
                    sd.inc(sy_.dma_start(dstq, stq[:, :, :]), 16)
                    b = 2
                    t_.wait_ge(ss.h, evac_done[b])
                    for ci in range(8):
                        ins = t_.matmul(bank(b, 1, 0, 64), wkvdT[:, ci, 0:64],
                                        xls[:, ci, :], start=(ci == 0), stop=(ci == 7))
                    stt.inc(ins)
                    s_.wait_ge(stt.h, stt.n)
                    ss.inc(s_.activation(ckvT[0:64, tl], bank(b, 1, 0, 64), AF.Identity,
                                         bias=bias_slice("L0_bkv_d", 0, 0, 64)))
                    evac_done[b] = ss.n
                    for tci in range(4):
                        sc = tsub * 4 + tci
                        b = 3
                        t_.wait_ge(sv.h, sv.n)
                        t_.wait_ge(ss.h, evac_done[b])
                        for ci in range(8):
                            ins = t_.matmul(PS[:, b, 0:64],
                                            xls[:, ci, tci * 128:(tci + 1) * 128],
                                            wkvdT[:, ci, 0:64],
                                            start=(ci == 0), stop=(ci == 7))
                        stt.inc(ins)
                        s_.wait_ge(stt.h, stt.n)
                        s_.wait_ge(sv.h, sv.n)
                        ss.inc(s_.copy(tok_tmp[:, :], PS[:, b, 0:64]))
                        evac_done[b] = ss.n
                        v_.wait_ge(ss.h, ss.n)
                        sv.inc(v_.tensor_tensor(ckv_tok[:, sc, 0:64], tok_tmp[:, :],
                                                PS[:, 5, 0:64], op=ALU.add))
                barrier()
            ep_state = {0: 0, 1: 0}
            ep_ss = {0: 0, 1: 0}
            ph3 = contextlib.ExitStack()
            qaug2 = [ph3.enter_context(sbt(f"qaug0_{i}", [65, T], BF16))
                     for i in range(2)]
            e_sb = [ph3.enter_context(sbt(f"e0_sb{i}", [128, 2048], BF16))
                    for i in range(2)]
            qk_tmp = ph3.enter_context(sbt("qk_tmp0", [64, 2048], BF16))
            den_row = ph3.enter_context(sbt("den_row0", [1, 1024], F32))
            u_sb = ph3.enter_context(sbt("u_sb0", [65, 1024], F32))
            z_bf = ph3.enter_context(sbt("z_bf", [65, 1024], BF16))
            y_pair = ph3.enter_context(sbt("y_pair0", [128, T], BF16))
            exp_prev = [0]
            pvd = {0: 0, 1: 0}
            qread = [0, 0]
            qdma = [0, 0]
            anch_mm = [0]
            for h in range(NH):
                qaug = qaug2[h % 2]
                if h == 0:
                    sy_.wait_ge(sq_.h, sq_.n)
                    sq_.inc(sy_.dma_start(qaug[0:64, :], scr_q[0:64, :]), 16)
                    qdma[0] = sq_.n
                if h + 1 < NH:
                    sy_.wait_ge(stt.h, qread[(h + 1) % 2])
                    sy_.wait_ge(sq_.h, sq_.n)
                    sq_.inc(sy_.dma_start(qaug2[(h + 1) % 2][0:64, :],
                                          scr_q[(h + 1) * 64:(h + 2) * 64, :]), 16)
                    qdma[(h + 1) % 2] = sq_.n
                for hf2 in range(max(1, T // 2048)):
                    hl = slice(hf2 * 2048, min((hf2 + 1) * 2048, T))
                    v_.wait_ge(sq_.h, qdma[h % 2])
                    v_.wait_ge(stt.h, anch_mm[0])
                    sv.inc(v_.tensor_tensor(qk_tmp[:, :], qaug[0:64, hl], ckvT[0:64, hl],
                                            op=ALU.mult))
                    t_.wait_ge(sv.h, sv.n)
                    if hf2 == 0:
                        t_.wait_ge(ss.h, ss.n)
                    for j4 in range(4):
                        stt.inc(t_.matmul(PS[0:1, 4 + (j4 % 2) * 2 + j4 // 2, :],
                                          negones64[:, :],
                                          qk_tmp[:, j4 * 512:(j4 + 1) * 512],
                                          start=True, stop=True))
                    anch_mm[0] = stt.n
                    s_.wait_ge(stt.h, stt.n)
                    for j4 in range(4):
                        ins = s_.activation(qaug[64:65, hf2 * 2048 + j4 * 512:
                                                 hf2 * 2048 + (j4 + 1) * 512],
                                            PS[0:1, 4 + (j4 % 2) * 2 + j4 // 2, :],
                                            AF.Identity, bias=consts[0:1, 0:1])
                    ss.inc(ins)
                    t_.wait_ge(ss.h, ss.n)
                for w in range(NW):
                    wl = slice(w * 1024, (w + 1) * 1024)

                    def emit_qk(sc):
                        sb_ = (sc % 2) * 2
                        if sc % 2 == 0:
                            t_.wait_ge(ss.h, exd.get(sc // 2 - 1, exp_prev[0]))
                        if sc < 2 and w == 0:
                            t_.wait_ge(sv.h, xadd_state["done"])
                        for half in range(2):
                            ins = t_.matmul(PS[:, sb_ + half, :],
                                            ckvT[:, sc * 128:(sc + 1) * 128],
                                            qaug[:, w * 1024 + half * 512:
                                                 w * 1024 + (half + 1) * 512],
                                            start=True, stop=True)
                        stt.inc(ins)
                        qk_done[sc] = stt.n

                    def emit_exp2(k):
                        s_.wait_ge(stt.h, qk_done[2 * k + 1])
                        s_.wait_ge(stt.h, pvd[k % 2])
                        ss.inc(s_.activation(e_sb[k % 2][:, :],
                                             bank(0, 4).rearrange("p a t -> p (a t)"),
                                             AF.Exp))
                        exd[k] = ss.n

                    pb = 4 + (w % 2) * 2

                    def emit_pv(sc):
                        t_.wait_ge(ss.h, exd[sc // 2])
                        if sc == 0:
                            t_.wait_ge(sv.h, ep_state[w % 2])
                            t_.wait_ge(ss.h, ep_ss[w % 2])
                        k2 = sc // 2
                        off = (sc % 2) * 1024
                        for half in range(2):
                            ins = t_.matmul(PS[0:65, pb + half, :],
                                            ckv_tok[:, sc, 0:65],
                                            e_sb[k2 % 2][:, off + half * 512:
                                                         off + (half + 1) * 512],
                                            start=(sc == 0), stop=(sc == NC_ - 1))
                        stt.inc(ins)
                        pvd[k2 % 2] = stt.n

                    qk_done, exd = {}, {}
                    emit_qk(0)
                    emit_qk(1)
                    emit_exp2(0)
                    for k in range(1, NC_ // 2):
                        emit_qk(2 * k)
                        emit_qk(2 * k + 1)
                        emit_exp2(k)
                        emit_pv(2 * k - 2)
                        emit_pv(2 * k - 1)
                    emit_pv(NC_ - 2)
                    emit_pv(NC_ - 1)
                    exp_prev[0] = exd[NC_ // 2 - 1]
                    s_.wait_ge(stt.h, stt.n)
                    ss.inc(s_.copy(den_row[0:1, :],
                                   PS[64:65, pb:pb + 2, :].rearrange("p a t -> p (a t)")))
                    v_.wait_ge(ss.h, ss.n)
                    sv.inc(v_.reciprocal(den_row[0:1, :], den_row[0:1, :]))
                    s_.wait_ge(sv.h, sv.n)
                    ss.inc(s_.copy(u_sb[:, :],
                                   PS[0:65, pb:pb + 2, :].rearrange("p a t -> p (a t)")))
                    t_.wait_ge(ss.h, ss.n)
                    stt.inc(t_.matmul(PS[0:65, pb, :], ones_f1[0:1, 0:65],
                                      den_row[0:1, 0:512], start=True, stop=True))
                    stt.inc(t_.matmul(PS[0:65, pb + 1, :], ones_f1[0:1, 0:65],
                                      den_row[0:1, 512:1024], start=True, stop=True))
                    v_.wait_ge(stt.h, stt.n)
                    sv.inc(v_.tensor_tensor(z_bf[:, :], u_sb[:, :],
                                            PS[0:65, pb:pb + 2, :].rearrange("p a t -> p (a t)"),
                                            op=ALU.mult))
                    ep_state[w % 2] = sv.n
                    t_.wait_ge(sv.h, sv.n)
                    for half in range(2):
                        stt.inc(t_.matmul(PS[0:64, pb + half, :],
                                          wvuT[:, h * 64:(h + 1) * 64],
                                          z_bf[:, half * 512:(half + 1) * 512],
                                          start=True, stop=True))
                    s_.wait_ge(stt.h, stt.n)
                    p0 = (h % 2) * 64
                    ss.inc(s_.copy(y_pair[p0:p0 + 64, wl],
                                   PS[0:64, pb:pb + 2, :].rearrange("p a t -> p (a t)")))
                    ep_ss[w % 2] = ss.n
                qread[h % 2] = stt.n
                if h % 2 == 1:
                    proj_pair("L0_b_proj", h // 2, h == 1, y_pair)
            barrier()
            ph3.close()


    _qwp_id = [0]

    def quant_weight_prep(w_ap, rows_n, cols_n, scr_dst, isw_col):
        nr = rows_n // 128
        _qwp_id[0] += 1
        u = str(_qwp_id[0])
        with contextlib.ExitStack() as ph:
            wch = ph.enter_context(sbt("wch" + u, [128, cols_n], F32))
            qch = ph.enter_context(sbt("qch" + u, [128, cols_n], I32))
            parts = ph.enter_context(sbt("parts" + u, [128, nr + 1], F32))
            sca = ph.enter_context(sbt("sca" + u, [1, 4], F32))
            swb = ph.enter_context(sbt("swb" + u, [128, 2], F32))
            tstg = ph.enter_context(sbt("tstg" + u, [128, cols_n // 128, 128], BF16))
            for rr in range(nr):
                sy_.wait_ge(sv.h, sv.n)
                sd.inc(sy_.dma_start(wch[:, :], w_ap[rr * 128:(rr + 1) * 128, :]), 16)
                v_.wait_ge(sd.h, sd.n)
                sv.inc(v_.reduce_sum(parts[:, rr:rr + 1], wch[:, :], axis=AX.X,
                                     apply_absolute_value=True))
            sv.inc(v_.drain())
            sv.inc(v_.reduce_sum(parts[:, nr:nr + 1], parts[:, 0:nr], axis=AX.X))
            sv.inc(v_.drain())
            sv.inc(v_.tensor_copy(wch[:, 0:1], parts[:, nr:nr + 1]))
            sv.inc(v_.drain())
            t_.wait_ge(sv.h, sv.n)
            stt.inc(t_.matmul(PS[0:1, 6, 0:1], ones_c128[:, :], wch[:, 0:1],
                              start=True, stop=True))
            v_.wait_ge(stt.h, stt.n)
            sv.inc(v_.tensor_scalar(sca[0:1, 0:1], PS[0:1, 6, 0:1],
                                    1.0 / (rows_n * cols_n), 1e-5,
                                    op0=ALU.mult, op1=ALU.max))
            sv.inc(v_.drain())
            sv.inc(v_.reciprocal(sca[0:1, 1:2], sca[0:1, 0:1]))
            sv.inc(v_.drain())
            t_.wait_ge(sv.h, sv.n)
            stt.inc(t_.matmul(PS[0:128, 6, 0:1], ones_f1[0:1, :], sca[0:1, 1:2],
                              start=True, stop=True))
            stt.inc(t_.matmul(PS[0:128, 6, 1:2], ones_f1[0:1, :], sca[0:1, 0:1],
                              start=True, stop=True))
            v_.wait_ge(stt.h, stt.n)
            sv.inc(v_.tensor_copy(swb[:, 0:2], PS[0:128, 6, 0:2]))
            sv.inc(v_.drain())
            sv.inc(v_.tensor_copy(bias_pool[:, isw_col:isw_col + 1], swb[:, 1:2]))
            sv.inc(v_.drain())
            for rr in range(nr):
                sy_.wait_ge(sv.h, sv.n)
                sy_.wait_ge(stt.h, stt.n)
                sd.inc(sy_.dma_start(wch[:, :], w_ap[rr * 128:(rr + 1) * 128, :]), 16)
                v_.wait_ge(sd.h, sd.n)
                v_.wait_ge(stt.h, stt.n)
                sv.inc(v_.tensor_scalar(qch[:, :], wch[:, :], swb[:, 0:1], None,
                                        op0=ALU.mult))
                sv.inc(v_.drain())
                sv.inc(v_.tensor_scalar(wch[:, :], qch[:, :], 1.0, -1.0,
                                        op0=ALU.min, op1=ALU.max))
                sv.inc(v_.drain())
                t_.wait_ge(sv.h, sv.n)
                t_.wait_ge(ss.h, ss.n)
                for jj in range(cols_n // 128):
                    b = jj % 2
                    off = jj // 2 % 2
                    tgt = PS[:, b, :].rearrange("p (o q) -> p o q", o=4)[:, off, :]
                    stt.inc(t_.transpose(tgt, wch[:, jj * 128:(jj + 1) * 128], ident_f[:, :]))
                    if jj % 4 == 3 or jj == cols_n // 128 - 1:
                        s_.wait_ge(stt.h, stt.n)
                        s_.wait_ge(sa.h, sa.n)
                        for jj2 in range(jj - jj % 4, jj + 1):
                            b2 = jj2 % 2
                            off2 = jj2 // 2 % 2
                            srcp = PS[:, b2, :].rearrange("p (o q) -> p o q", o=4)[:, off2, :]
                            ins = s_.activation(tstg[:, jj2, :], srcp, AF.Identity)
                        ss.inc(ins)
                        t_.wait_ge(ss.h, ss.n)
                sy_ps = nc.scalar
                sy_ps.wait_ge(ss.h, ss.n)
                dst = scr_dst.rearrange("(j p) f -> p j f", p=128)[:, :, rr * 128:(rr + 1) * 128]
                sa.inc(sy_ps.dma_start(dst, tstg[:, :, :]), 16)
            barrier()

    def f0_ffn():
        NG = T // 512
        scr_xres = nc.dram_tensor("scr_xres", [C, T], F32, kind="Internal").ap()
        with nc.named_scope("fqw"):
            quant_weight_prep(din["F0_w1"], FF, C, scr_qw1, 250)
            quant_weight_prep(din["F0_w2"], C, FF, scr_qw2, 251)
        barrier()
        sd.inc(sy_.dma_start(scr_xres.rearrange("(k p) t -> p k t", p=128), XT[:, :, :]), 16)
        _f0st = contextlib.ExitStack()
        xb = XT.bitcast(BF16).reshape([128, 8 * T * 2])
        nqw, nqx, nfr = 8 * FF, 32 * 512, 3 * FF
        if 8 * T * 2 >= nqw + nqx + nfr:
            qw1v = xb[:, 0:nqw].rearrange("p (a f) -> p a f", a=8)
            qx2T = xb[:, nqw:nqw + nqx].rearrange("p (a f) -> p a f", a=32)
            frow = xb[0:1, nqw + nqx:nqw + nqx + nfr].rearrange("p (a f) -> p a f", a=3)
        elif 8 * T * 2 >= nqw:
            qw1v = xb[:, 0:nqw].rearrange("p (a f) -> p a f", a=8)
            qx2T = _f0st.enter_context(sbt("qx2T_r", [128, 32, 512], BF16))[:, :, :]
            frow = _f0st.enter_context(sbt("frow_r", [1, 3, FF], BF16))[:, :, :]
        else:
            qw1v = _f0st.enter_context(sbt("qw1v_r", [128, 8, FF], BF16))[:, :, :]
            qx2T = _f0st.enter_context(sbt("qx2T_r", [128, 32, 512], BF16))[:, :, :]
            frow = _f0st.enter_context(sbt("frow_r", [1, 3, FF], BF16))[:, :, :]
        sy_.wait_ge(sd.h, sd.n)
        sd.inc(sy_.dma_start(qw1v[:, :, :],
                             scr_qw1.rearrange("(k p) f -> p k f", p=128)), 16)
        g_.wait_ge(sd.h, sd.n - 16)
        sg.inc(g_.dma_start(frow[0:1, 0, :], din["F0_b1"].rearrange("(o n) -> o n", o=1)), 16)
        sg.inc(g_.dma_start(frow[0:1, 1, :], din["F0_ln2_g"].rearrange("(o n) -> o n", o=1)), 16)
        sg.inc(g_.dma_start(frow[0:1, 2, :], din["F0_ln2_b"].rearrange("(o n) -> o n", o=1)), 16)
        barrier()
        with contextlib.ExitStack() as ph:
            qxT = ph.enter_context(sbt("qxT", [128, 8, 512], BF16))
            xls = ph.enter_context(sbt("xls_f0", [128, 8, 512], BF16))
            gb = ph.enter_context(sbt("gb", [128, FF], F32))
            t0b = ph.enter_context(sbt("t0b", [128, 512], F32))
            t0c = ph.enter_context(sbt("t0c", [128, 512], F32))
            t0x = [t0b, t0c]
            y2b = ph.enter_context(sbt("y2b", [128, 8, 512], F16))
            qib = ph.enter_context(sbt("qib", [128, 512], I32))
            qib2 = ph.enter_context(sbt("qib2", [128, 512], I32))
            cols = ph.enter_context(sbt("cols", [128, 24], F32))
            pool1 = ph.enter_context(sbt("pool1", [128, 32], F32))
            cs2 = [ph.enter_context(sbt(f"cslab{i}", [128, 8, 128], BF16))
                   for i in range(2)]
            xg = ph.enter_context(sbt("xg", [128, 4, 512], F32))
            _scC = [None]
            pstate = {"evd": {}, "bad": {}, "qtt": {}}
            csd = {0: 0, 1: 0}
            csr = {0: 0, 1: 0}

            def cs_issue(qi):
                cc2, hq2 = qi // 4, qi % 4
                q_, qs = (g_, sg) if qi % 2 == 0 else (nc.scalar, sa)
                q_.wait_ge(stt.h, csr[qi % 2])
                q_.wait_ge(qs.h, qs.n)
                qs.inc(q_.dma_start(
                    cs2[qi % 2][:, :, :],
                    scr_qw2.rearrange("(j p) c -> p j c", p=128)
                    [:, hq2 * 8:hq2 * 8 + 8, cc2 * 128:(cc2 + 1) * 128]), 16)
                csd[qi % 2] = qs.n
            for g0 in range(NG):
                gl = slice(g0 * 512, (g0 + 1) * 512)
                sy_.wait_ge(stt.h, stt.n)
                sy_.wait_ge(sv.h, sv.n)
                sy_.wait_ge(sd.h, sd.n)
                srcx = scr_xln.rearrange("(k p) t -> p k t", p=128)[:, :, gl]
                sd.inc(sy_.dma_start(xls[:, :, :], srcx), 16)
                if _scC[0] is not None:
                    _scC[0].__exit__(None, None, None)
                    _scC[0] = None
                _scA = nc.named_scope("fA"); _scA.__enter__()
                v_.wait_ge(sd.h, sd.n)
                for cc in range(8):
                    v_.wait_ge(stt.h, stt.n)
                    sv.inc(v_.tensor_tensor(t0b[:, :], xls[:, cc, :], xls[:, cc, :],
                                            op=ALU.mult))
                    t_.wait_ge(sv.h, sv.n)
                    stt.inc(t_.matmul(PS[0:1, 6, :], ones_c128b[:, :], xls[:, cc, :],
                                      start=(cc == 0), stop=(cc == 7)))
                    stt.inc(t_.matmul(PS[0:1, 7, :], ones_c128[:, :], t0b[:, :],
                                      start=(cc == 0), stop=(cc == 7)))
                v_.wait_ge(stt.h, stt.n)
                v_.tensor_scalar(rows[:, 0, :], PS[0:1, 6, :], 1.0 / C, None, op0=ALU.mult)
                sv.inc(v_.tensor_scalar(rows[:, 1, :], PS[0:1, 7, :], 1.0 / C, None,
                                        op0=ALU.mult))
                sv.inc(v_.drain())
                sv.inc(v_.tensor_tensor(rows[:, 2, :], rows[:, 0, :], rows[:, 0, :],
                                        op=ALU.mult))
                sv.inc(v_.drain())
                sv.inc(v_.tensor_tensor(rows[:, 1, :], rows[:, 1, :], rows[:, 2, :],
                                        op=ALU.subtract))
                sv.inc(v_.drain())
                sv.inc(v_.tensor_scalar(rows[:, 1, :], rows[:, 1, :], LN_EPS, None,
                                        op0=ALU.add))
                sv.inc(v_.drain())
                s_.wait_ge(sv.h, sv.n)
                ss.inc(s_.activation(rows[:, 2, :], rows[:, 1, :], AF.Ln))
                ss.inc(s_.drain())
                ss.inc(s_.activation(rows[:, 1, :], rows[:, 2, :], AF.Exp, scale=-0.5))
                ss.inc(s_.drain())
                v_.wait_ge(ss.h, ss.n)
                sv.inc(v_.tensor_tensor(rows[:, 2, :], rows[:, 0, :], rows[:, 1, :],
                                        op=ALU.mult))
                sv.inc(v_.drain())
                t_.wait_ge(sv.h, sv.n)
                stt.inc(t_.matmul(PS[:, 2, :], ones_f1[:, :], rows[:, 1, :],
                                  start=True, stop=True))
                stt.inc(t_.matmul(PS[:, 3, :], ones_f1[:, :], rows[:, 2, :],
                                  start=True, stop=True))
                v_.wait_ge(stt.h, stt.n)
                for cc in range(8):
                    sv.inc(v_.tensor_tensor(y2b[:, cc, :], xls[:, cc, :], PS[:, 2, :],
                                            op=ALU.mult))
                sv.inc(v_.drain())
                for cc in range(8):
                    sv.inc(v_.tensor_tensor(y2b[:, cc, :], y2b[:, cc, :], PS[:, 3, :],
                                            op=ALU.subtract))
                sv.inc(v_.drain())
                s_.wait_ge(sv.h, sv.n)
                for cc in range(8):
                    ins = s_.activation(y2b[:, cc, :], y2b[:, cc, :], AF.Identity,
                                        scale=bias_slice("F0_ln1_g", cc),
                                        bias=bias_slice("F0_ln1_b", cc))
                    ss.inc(ins)
                    ss.inc(s_.drain())
                t_.wait_ge(ss.h, ss.n)
                for tc in range(4):
                    for cc in range(8):
                        b = 4 + cc % 2
                        off = cc // 2 % 4
                        tgt = PS16[:, b, :].rearrange("p (o q) -> p o q", o=8)[:, off, :]
                        stt.inc(t_.transpose(tgt, y2b[:, cc, tc * 128:(tc + 1) * 128],
                                             ident_h[:, :]))
                        if cc % 2 == 1:
                            v_.wait_ge(stt.h, stt.n)
                            for c2 in (cc - 1, cc):
                                b2 = 4 + c2 % 2
                                off2 = c2 // 2 % 4
                                sp = PS16[:, b2, :].rearrange("p (o q) -> p o q", o=8)[:, off2, :]
                                ins = v_.reduce_max(pool1[:, tc * 8 + c2:tc * 8 + c2 + 1],
                                                    sp, axis=AX.X, apply_absolute_value=True)
                            sv.inc(ins)
                            t_.wait_ge(sv.h, sv.n)
                sv.inc(v_.drain())
                for tc in range(4):
                    sv.inc(v_.reduce_max(cols[:, tc:tc + 1], pool1[:, tc * 8:(tc + 1) * 8],
                                         axis=AX.X))
                sv.inc(v_.drain())
                sv.inc(v_.tensor_scalar(cols[:, 0:4], cols[:, 0:4], 1e-5, None, op0=ALU.max))
                sv.inc(v_.drain())
                sv.inc(v_.reciprocal(cols[:, 4:8], cols[:, 0:4]))
                sv.inc(v_.drain())
                sv.inc(v_.tensor_scalar(cols[:, 4:8], cols[:, 4:8], 127.0, None, op0=ALU.mult))
                sv.inc(v_.tensor_scalar(cols[:, 8:12], cols[:, 0:4], 1.0 / 127.0, None,
                                        op0=ALU.mult))
                sv.inc(v_.drain())
                sv.inc(v_.tensor_scalar(cols[:, 8:12], cols[:, 8:12],
                                        bias_pool[:, 250:251], None, op0=ALU.mult))
                sv.inc(v_.drain())
                t_.wait_ge(sv.h, sv.n)
                for tc in range(4):
                    stt.inc(t_.transpose(PS[0:1, 6, tc * 128:(tc + 1) * 128],
                                         cols[:, 4 + tc:5 + tc], ident_f[:, :]))
                s_.wait_ge(stt.h, stt.n)
                ins = s_.copy(rows[0:1, 3, :], PS[0:1, 6, :])
                ss.inc(ins)
                t_.wait_ge(ss.h, ss.n)
                stt.inc(t_.matmul(PS[:, 7, :], ones_f1[:, :], rows[:, 3, :],
                                  start=True, stop=True))
                v_.wait_ge(stt.h, stt.n)
                for cc in range(8):
                    sv.inc(v_.tensor_tensor(y2b[:, cc, :], y2b[:, cc, :], PS[:, 7, :],
                                            op=ALU.mult))
                sv.inc(v_.drain())
                qrd = {0: 0, 1: 0}
                for cc in range(8):
                    qb = qib if cc % 2 == 0 else qib2
                    s_.wait_ge(sv.h, sv.n if cc < 2 else qrd[cc % 2])
                    ss.inc(s_.copy(qb[:, :], y2b[:, cc, :]))
                    v_.wait_ge(ss.h, ss.n)
                    sv.inc(v_.tensor_copy(qxT[:, cc, :], qb[:, :]))
                    qrd[cc % 2] = sv.n
                _scA.__exit__(None, None, None)
                _scB = nc.named_scope("fB"); _scB.__enter__()
                sy_.wait_ge(sd.h, sd.n)
                for tc in range(4):
                    tcl = slice(tc * 128, (tc + 1) * 128)
                    pevd, pbad, pqtt = pstate["evd"], pstate["bad"], pstate["qtt"]
                    mmd, evd, bad, gld, qtt = {}, {}, {}, {-1: 0, -2: 0}, {}

                    def b1_mm(f5):
                        b = f5 % 2
                        fl = slice(f5 * 512, (f5 + 1) * 512)
                        if f5 < 2:
                            if tc == 0:
                                t_.wait_ge(ss.h, ss.n)
                                t_.wait_ge(sv.h, sv.n)
                            else:
                                t_.wait_ge(ss.h, pevd.get(6 + f5, 0))
                                t_.wait_ge(sv.h, pbad.get(6 + f5, 0))
                        else:
                            t_.wait_ge(sv.h, bad.get(f5 - 2, 0))
                            t_.wait_ge(ss.h, evd.get(f5 - 2, 0))
                        stt.inc(t_.matmul(PS[:, 4 + b, :], ones_b1[:, :], frow[0:1, 0, fl],
                                          start=True, stop=True))
                        for ci in range(8):
                            ins = t_.matmul(bank(b), qxT[:, ci, tcl], qw1v[:, ci, fl],
                                            start=(ci == 0), stop=(ci == 7))
                        stt.inc(ins)
                        mmd[f5] = stt.n

                    def b1_evac(f5):
                        b = f5 % 2
                        s_.wait_ge(stt.h, mmd[f5])
                        s_.wait_ge(sv.h, bad.get(f5 - 1, 0))
                        ss.inc(s_.activation(t0x[b][:, :], bank(b), AF.Identity,
                                             scale=cols[:, 8 + tc:9 + tc]))
                        evd[f5] = ss.n

                    def b1_badd(f5):
                        b = f5 % 2
                        v_.wait_ge(ss.h, evd[f5])
                        sv.inc(v_.tensor_tensor(t0x[b][:, :], t0x[b][:, :], PS[:, 4 + b, :],
                                                op=ALU.add))
                        bad[f5] = sv.n

                    def b1_gelu(f5):
                        b = f5 % 2
                        fl = slice(f5 * 512, (f5 + 1) * 512)
                        s_.wait_ge(sv.h, bad[f5])
                        s_.wait_ge(sv.h, pqtt.get(f5, 0))
                        ss.inc(s_.activation(gb[:, fl], t0x[b][:, :], AF.Gelu))
                        gld[f5] = ss.n

                    b1_mm(0); b1_evac(0)
                    for f5 in range(1, 8):
                        b1_mm(f5); b1_badd(f5 - 1); b1_gelu(f5 - 1); b1_evac(f5)
                    b1_badd(7); b1_gelu(7)
                    v_.wait_ge(ss.h, ss.n)
                    sv.inc(v_.reduce_sum(cols[:, 16:17], gb[:, :], axis=AX.X))
                    red = {0: sv.n, 1: sv.n}
                    sqd = {0: 0, 1: 0}
                    for f5 in range(8):
                        fl = slice(f5 * 512, (f5 + 1) * 512)
                        s_.wait_ge(sv.h, red[f5 % 2])
                        ss.inc(s_.activation(t0x[f5 % 2][:, :], gb[:, fl], AF.Square))
                        sqd[f5 % 2] = ss.n
                        v_.wait_ge(ss.h, sqd[f5 % 2])
                        sv.inc(v_.reduce_sum(pool1[:, f5:f5 + 1], t0x[f5 % 2][:, :],
                                             axis=AX.X))
                        red[f5 % 2] = sv.n
                    sv.inc(v_.drain())
                    sv.inc(v_.reduce_sum(cols[:, 17:18], pool1[:, 0:8], axis=AX.X))
                    s_.wait_ge(sv.h, sv.n)
                    ss.inc(s_.activation(cols[:, 19:20], cols[:, 17:18], AF.Identity,
                                         scale=1.0 / FF, bias=cc_eps[:, 0:1]))
                    sv.inc(v_.tensor_scalar(cols[:, 18:19], cols[:, 16:17], 1.0 / FF, None,
                                            op0=ALU.mult))
                    s_.wait_ge(sv.h, sv.n)
                    ss.inc(s_.activation(cols[:, 20:21], cols[:, 18:19], AF.Square))
                    v_.wait_ge(ss.h, ss.n)
                    sv.inc(v_.tensor_tensor(cols[:, 19:20], cols[:, 19:20], cols[:, 20:21],
                                            op=ALU.subtract))
                    s_.wait_ge(sv.h, sv.n)
                    ss.inc(s_.activation(cols[:, 20:21], cols[:, 19:20], AF.Ln))
                    ss.inc(s_.drain())
                    ss.inc(s_.activation(cols[:, 20:21], cols[:, 20:21], AF.Exp, scale=-0.5))
                    v_.wait_ge(ss.h, ss.n)
                    sv.inc(v_.scalar_tensor_tensor(cols[:, 21:22], cols[:, 18:19], -1.0,
                                                   cols[:, 20:21], op0=ALU.mult,
                                                   op1=ALU.mult))
                    s_.wait_ge(sv.h, sv.n)
                    ss.inc(s_.activation(gb[:, :], gb[:, :], AF.Identity,
                                         scale=cols[:, 20:21], bias=cols[:, 21:22]))
                    ss.inc(s_.drain())
                    apd = {0: 0, 1: 0}
                    for f5 in range(8):
                        fl = slice(f5 * 512, (f5 + 1) * 512)
                        pA = 2 if f5 % 2 == 0 else 6
                        t_.wait_ge(sv.h, apd[f5 % 2])
                        if f5 == 0:
                            t_.wait_ge(ss.h, ss.n)
                        stt.inc(t_.matmul(PS[:, pA, :], ones_b1[:, :], frow[0:1, 1, fl],
                                          start=True, stop=True))
                        stt.inc(t_.matmul(PS[:, pA + 1, :], ones_b1[:, :], frow[0:1, 2, fl],
                                          start=True, stop=True))
                        v_.wait_ge(stt.h, stt.n)
                        sv.inc(v_.tensor_tensor(gb[:, fl], gb[:, fl], PS[:, pA, :],
                                                op=ALU.mult))
                        sv.inc(v_.drain())
                        sv.inc(v_.tensor_tensor(gb[:, fl], gb[:, fl], PS[:, pA + 1, :],
                                                op=ALU.add))
                        apd[f5 % 2] = sv.n
                    sv.inc(v_.drain())
                    sv.inc(v_.reduce_max(cols[:, 13:14], gb[:, :], axis=AX.X,
                                         apply_absolute_value=True))
                    s_.wait_ge(sv.h, sv.n)
                    ss.inc(s_.activation(cols[:, 22:23], cols[:, 13:14], AF.Identity,
                                         scale=1.0, bias=0.0))
                    v_.wait_ge(ss.h, ss.n)
                    sv.inc(v_.tensor_scalar(cols[:, 13:14], cols[:, 22:23], 1e-5, None,
                                            op0=ALU.max))
                    sv.inc(v_.drain())
                    sv.inc(v_.reciprocal(cols[:, 14:15], cols[:, 13:14]))
                    s_.wait_ge(sv.h, sv.n)
                    ss.inc(s_.activation(cols[:, 14:15], cols[:, 14:15], AF.Identity,
                                         scale=127.0))
                    sv.inc(v_.tensor_scalar(cols[:, 23:24], cols[:, 13:14], 1.0 / 127.0,
                                            None, op0=ALU.mult))
                    sv.inc(v_.drain())
                    sv.inc(v_.tensor_scalar(cols[:, 4 + tc:5 + tc], cols[:, 23:24],
                                            bias_pool[:, 251:252], None, op0=ALU.mult))
                    v_.wait_ge(ss.h, ss.n)
                    qrd2 = {0: 0, 1: 0}
                    trd = {0: 0, 1: 0}
                    for f5 in range(8):
                        fl = slice(f5 * 512, (f5 + 1) * 512)
                        qb = qib if f5 % 2 == 0 else qib2
                        v_.wait_ge(ss.h, qrd2[f5 % 2])
                        sv.inc(v_.tensor_scalar(qb[:, :], gb[:, fl], cols[:, 14:15],
                                                None, op0=ALU.mult))
                        qtt[f5] = sv.n
                        s_.wait_ge(sv.h, sv.n)
                        s_.wait_ge(stt.h, trd[f5 % 2])
                        ss.inc(s_.copy(t0x[f5 % 2][:, :], qb[:, :]))
                        qrd2[f5 % 2] = ss.n
                        t_.wait_ge(ss.h, ss.n)
                        if f5 == 0:
                            t_.wait_ge(sv.h, sv.n)
                        for q4 in range(4):
                            bq = 2 + q4 % 2
                            off = q4 // 2
                            tgt = PS[:, bq, :].rearrange("p (o q) -> p o q", o=4)[:, off, :]
                            stt.inc(t_.transpose(tgt, t0x[f5 % 2][:, q4 * 128:(q4 + 1) * 128],
                                                 ident_f[:, :]))
                        trd[f5 % 2] = stt.n
                        s_.wait_ge(stt.h, stt.n)
                        for q4 in range(4):
                            bq = 2 + q4 % 2
                            off = q4 // 2
                            sp = PS[:, bq, :].rearrange("p (o q) -> p o q", o=4)[:, off, :]
                            ins = s_.activation(qx2T[:, f5 * 4 + q4, tcl], sp, AF.Identity)
                        ss.inc(ins)
                    pstate["evd"], pstate["bad"], pstate["qtt"] = evd, bad, qtt
                _scB.__exit__(None, None, None)
                _scC[0] = nc.named_scope("fC"); _scC[0].__enter__()
                t_.wait_ge(ss.h, ss.n)
                t_.wait_ge(sv.h, sv.n)
                for tc in range(4):
                    stt.inc(t_.transpose(PS[0:1, 6, tc * 128:(tc + 1) * 128],
                                         cols[:, 4 + tc:5 + tc], ident_f[:, :]))
                s_.wait_ge(stt.h, stt.n)
                ins = s_.copy(rows[0:1, 3, :], PS[0:1, 6, :])
                ss.inc(ins)
                t_.wait_ge(ss.h, ss.n)
                stt.inc(t_.matmul(PS[:, 3, :], ones_f1[:, :], rows[:, 3, :],
                                  start=True, stop=True))
                for hf in range(2):
                  sy_.wait_ge(sv.h, sv.n)
                  sy_.wait_ge(sd.h, sd.n)
                  sd.inc(sy_.dma_start(
                      xg[:, :, :],
                      scr_xres.rearrange("(k p) t -> p k t", p=128)[:, hf * 4:hf * 4 + 4, gl]),
                      16)
                  v_.wait_ge(sd.h, sd.n)
                  for cc in range(hf * 4, hf * 4 + 4):
                    b = cc % 2
                    for hq in range(4):
                        qi = cc * 4 + hq
                        if qi == 0:
                            cs_issue(0)
                        if qi + 1 < 32:
                            cs_issue(qi + 1)
                        t_.wait_ge((sg if qi % 2 == 0 else sa).h, csd[qi % 2])
                        if hq == 0:
                            t_.wait_ge(ss.h, ss.n)
                        for fc in range(8):
                            ins = t_.matmul(bank(b), cs2[qi % 2][:, fc, :],
                                            qx2T[:, hq * 8 + fc, :],
                                            start=(hq == 0 and fc == 0),
                                            stop=(hq == 3 and fc == 7))
                        stt.inc(ins)
                        csr[qi % 2] = stt.n
                    s_.wait_ge(stt.h, stt.n)
                    s_.wait_ge(sv.h, sv.n)
                    ss.inc(s_.activation(t0b[:, :], bank(b), AF.Identity))
                    v_.wait_ge(ss.h, ss.n)
                    sv.inc(v_.tensor_tensor(t0b[:, :], t0b[:, :], PS[:, 3, :], op=ALU.mult))
                    sv.inc(v_.drain())
                    sv.inc(v_.scalar_tensor_tensor(xg[:, cc - hf * 4, :], t0b[:, :],
                                                   bias_slice("F0_b2", cc), xg[:, cc - hf * 4, :],
                                                   op0=ALU.add, op1=ALU.add))
                    sv.inc(v_.drain())
                  sy_.wait_ge(sv.h, sv.n)
                  sd.inc(sy_.dma_start(
                      scr_xres.rearrange("(k p) t -> p k t", p=128)[:, hf * 4:hf * 4 + 4, gl],
                      xg[:, :, :]), 16)
            if _scC[0] is not None:
                _scC[0].__exit__(None, None, None)
                _scC[0] = None
            barrier()
            sd.inc(sy_.dma_start(XT[:, :, :],
                                 scr_xres.rearrange("(k p) t -> p k t", p=128)), 16)
            barrier()
        _f0st.close()

    def f1_ffn():
      with contextlib.ExitStack() as phx:
        b1g = phx.enter_context(sbt("b1g", [128, 24], F32))
        with contextlib.ExitStack() as ph:
            wp1 = ph.enter_context(sbt("wp1", [128, 8, NAV], F32))
            wp2 = ph.enter_context(sbt("wp2", [128, NE], F32))
            growA = ph.enter_context(sbt("growA", [1, NAV], F32))
            growB = ph.enter_context(sbt("growB", [1, NAV], F32))
            gcol = ph.enter_context(sbt("gcol", [128, 1], F32))
            lgrow = ph.enter_context(sbt("lgrow", [1, 16], F32))
            ohb = ph.enter_context(sbt("ohb", [128, NE], F32))
            de_f32 = ph.enter_context(sbt("de_f32", [128, NE, 128], F32))
            bsel = ph.enter_context(sbt("bsel", [8, FE + C], F32))
            webf = [ph.enter_context(sbt(f"webf{i}", [128, FE], F32))
                    for i in range(2)]
            wst = ph.enter_context(sbt("wst", [128, 4, 512], BF16))
            rowtmp = ph.enter_context(sbt("rowtmp", [1, 2, 512], F32))
            sc_.inc(sy_.dma_start(wp1[:, :, :],
                                  din["F1_Wp1"].rearrange("(k p) n -> p k n", p=128)), 16)
            sc_.inc(sy_.dma_start(wp2[0:NAV, :], din["F1_Wp2"]), 16)
            sc_.inc(sy_.dma_start(growB[0:1, :],
                                  din["F1_bp1"].rearrange("(o n) -> o n", o=1)), 16)
            sc_.inc(sy_.dma_start(lgrow[0:1, 0:NE],
                                  din["F1_bp2"].rearrange("(o n) -> o n", o=1)), 16)
            sc_.inc(sy_.dma_start(bsel[:, 0:FE], din["F1_b1"]), 16)
            sc_.inc(sy_.dma_start(bsel[:, FE:FE + C], din["F1_b2"]), 16)
            barrier()
            for ci in range(8):
                stt.inc(t_.matmul(PS[0:1, 0, 0:NAV], XT[:, ci, T - 1:T], wp1[:, ci, :],
                                  start=(ci == 0), stop=(ci == 7)))
                stt.inc(t_.matmul(PS[0:1, 1, 0:NAV], ones_c128[:, :], wp1[:, ci, :],
                                  start=(ci == 0), stop=(ci == 7)))
            v_.wait_ge(stt.h, stt.n)
            sv.inc(v_.tensor_tensor(consts[0:1, 5:6], consts[0:1, 2:3], consts[0:1, 3:4],
                                    op=ALU.mult))
            sv.inc(v_.drain())
            sv.inc(v_.tensor_scalar(growA[0:1, :], PS[0:1, 0, 0:NAV], consts[0:1, 3:4],
                                    None, op0=ALU.mult))
            sv.inc(v_.drain())
            sv.inc(v_.scalar_tensor_tensor(growA[0:1, :], PS[0:1, 1, 0:NAV],
                                           consts[0:1, 5:6], growA[0:1, :],
                                           op0=ALU.mult, op1=ALU.subtract))
            sv.inc(v_.drain())
            sv.inc(v_.tensor_tensor(growA[0:1, :], growB[0:1, :], growA[0:1, :],
                                    op=ALU.subtract))
            sv.inc(v_.drain())
            s_.wait_ge(sv.h, sv.n)
            ss.inc(s_.activation(growA[0:1, :], growA[0:1, :], AF.Gelu))
            ss.inc(s_.drain())
            t_.wait_ge(ss.h, ss.n)
            stt.inc(t_.transpose(PS[0:NAV, 2, 0:1], growA[0:1, :], ident_f[0:1, 0:1]))
            s_.wait_ge(stt.h, stt.n)
            ss.inc(s_.copy(gcol[0:NAV, :], PS[0:NAV, 2, 0:1]))
            t_.wait_ge(ss.h, ss.n)
            stt.inc(t_.matmul(PS[0:1, 3, 0:NE], gcol[0:NAV, :], wp2[0:NAV, :],
                              start=True, stop=True))
            v_.wait_ge(stt.h, stt.n)
            sv.inc(v_.tensor_tensor(lgrow[0:1, 8:8 + NE], PS[0:1, 3, 0:NE],
                                    lgrow[0:1, 0:NE], op=ALU.add))
            sv.inc(v_.drain())
            sv.inc(v_.reduce_max(consts[0:1, 6:7], lgrow[0:1, 8:8 + NE], axis=AX.X))
            sv.inc(v_.drain())
            sv.inc(v_.tensor_scalar(lgrow[0:1, 0:NE], lgrow[0:1, 8:8 + NE],
                                    consts[0:1, 6:7], None, op0=ALU.is_ge))
            sv.inc(v_.drain())
            t_.wait_ge(sv.h, sv.n)
            stt.inc(t_.matmul(PS[:, 4, 0:NE], ones_f1[:, :], lgrow[0:1, 0:NE],
                              start=True, stop=True))
            v_.wait_ge(stt.h, stt.n)
            sv.inc(v_.tensor_copy(ohb[:, :], PS[:, 4, 0:NE]))
            sv.inc(v_.drain())
            for e in range(NE):
                sv.inc(v_.tensor_scalar(de_f32[:, e, :], ident_f[:, :], ohb[:, e:e + 1],
                                        None, op0=ALU.mult))
            sv.inc(v_.drain())
            stt.inc(t_.transpose(PS[0:NE, 5, 0:1], lgrow[0:1, 0:NE], ident_f[0:1, 0:1]))
            s_.wait_ge(stt.h, stt.n)
            ss.inc(s_.copy(gcol[0:NE, :], PS[0:NE, 5, 0:1]))
            t_.wait_ge(ss.h, ss.n)
            t_.wait_ge(sv.h, sv.n)
            for j in range(4):
                stt.inc(t_.matmul(PS[0:1, 6, :], gcol[0:NE, :],
                                  bsel[:, j * 512:(j + 1) * 512], start=True, stop=True))
                stt.inc(t_.matmul(PS[0:1, 7, :], gcol[0:NE, :],
                                  bsel[:, FE + (j % 2) * 512:FE + (j % 2 + 1) * 512]
                                  if j < 2 else bsel[:, 0:512],
                                  start=True, stop=True))
                s_.wait_ge(stt.h, stt.n)
                for q in range(4):
                    ins = s_.copy(rowtmp[0:1, 0, q * 128:(q + 1) * 128],
                                  PS[0:1, 6, q * 128:(q + 1) * 128])
                ss.inc(ins)
                t_.wait_ge(ss.h, ss.n)
                for q in range(4):
                    stt.inc(t_.transpose(PS[0:128, 4, q:q + 1],
                                         rowtmp[0:1, 0, q * 128:(q + 1) * 128],
                                         ident_f[0:1, 0:1]))
                s_.wait_ge(stt.h, stt.n)
                for q in range(4):
                    ins = s_.copy(b1g[:, j * 4 + q:j * 4 + q + 1], PS[0:128, 4, q:q + 1])
                ss.inc(ins)
                if j < 2:
                    for q in range(4):
                        ins = s_.copy(rowtmp[0:1, 1, q * 128:(q + 1) * 128],
                                      PS[0:1, 7, q * 128:(q + 1) * 128])
                    ss.inc(ins)
                    t_.wait_ge(ss.h, ss.n)
                    for q in range(4):
                        stt.inc(t_.transpose(PS[0:128, 5, q:q + 1],
                                             rowtmp[0:1, 1, q * 128:(q + 1) * 128],
                                             ident_f[0:1, 0:1]))
                    s_.wait_ge(stt.h, stt.n)
                    for q in range(4):
                        ins = s_.copy(b1g[:, 16 + j * 4 + q:16 + j * 4 + q + 1],
                                      PS[0:128, 5, q:q + 1])
                    ss.inc(ins)
                t_.wait_ge(ss.h, ss.n)
            barrier()
            sg.inc(g_.dma_start(scr_wo, din["F1_Wo"]), 16)
            wseq = ([("w1", i, e) for i in range(8) for e in range(NE)] +
                    [("w2", i, e) for i in range(16) for e in range(NE)])
            wdma = {0: 0, 1: 0}
            wread = {0: 0, 1: 0}

            def w_issue(idx):
                k, i, e = wseq[idx]
                src = (din["F1_W1"][e, i * 128:(i + 1) * 128, :] if k == "w1"
                       else din["F1_W2"][e, i * 128:(i + 1) * 128, :])
                ncols = FE if k == "w1" else C
                q_, qs = (sy_, sd) if idx % 2 == 0 else (nc.scalar, sa)
                q_.wait_ge(stt.h, wread[idx % 2])
                q_.wait_ge(qs.h, qs.n)
                qs.inc(q_.dma_start(webf[idx % 2][:, 0:ncols], src), 16)
                wdma[idx % 2] = qs.n

            w_issue(0)
            w_issue(1)
            for dcc in range(8):
                for e in range(NE):
                    idx = dcc * NE + e
                    t_.wait_ge((sd if idx % 2 == 0 else sa).h, wdma[idx % 2])
                    if e == 0:
                        t_.wait_ge(ss.h, ss.n)
                    for fc in range(4):
                        ins = t_.matmul(PS[:, fc, :], de_f32[:, e, :],
                                        webf[idx % 2][:, fc * 512:(fc + 1) * 512],
                                        start=(e == 0), stop=(e == NE - 1))
                    stt.inc(ins)
                    wread[idx % 2] = stt.n
                    if idx + 2 < len(wseq):
                        w_issue(idx + 2)
                s_.wait_ge(stt.h, stt.n)
                s_.wait_ge(sw_.h, sw_.n)
                for fc in range(4):
                    ins = s_.activation(wst[:, fc, :], PS[:, fc, :], AF.Identity)
                ss.inc(ins)
                sy_.wait_ge(ss.h, ss.n)
                sw_.inc(sy_.dma_start(scr_w1s[dcc * 128:(dcc + 1) * 128, :],
                                      wst.rearrange("p a t -> p (a t)")), 16)
                t_.wait_ge(ss.h, ss.n)
            for fcc in range(16):
                for e in range(NE):
                    idx = 64 + fcc * NE + e
                    t_.wait_ge((sd if idx % 2 == 0 else sa).h, wdma[idx % 2])
                    if e == 0:
                        t_.wait_ge(ss.h, ss.n)
                    for fc in range(2):
                        ins = t_.matmul(PS[:, fc, :], de_f32[:, e, :],
                                        webf[idx % 2][:, fc * 512:(fc + 1) * 512],
                                        start=(e == 0), stop=(e == NE - 1))
                    stt.inc(ins)
                    wread[idx % 2] = stt.n
                    if idx + 2 < len(wseq):
                        w_issue(idx + 2)
                s_.wait_ge(stt.h, stt.n)
                s_.wait_ge(sw_.h, sw_.n)
                for fc in range(2):
                    ins = s_.activation(wst[:, fc, :], PS[:, fc, :], AF.Identity)
                ss.inc(ins)
                sy_.wait_ge(ss.h, ss.n)
                sw_.inc(sy_.dma_start(scr_w2s[fcc * 128:(fcc + 1) * 128, :],
                                      wst[:, 0:2, :].rearrange("p a t -> p (a t)")), 16)
                t_.wait_ge(ss.h, ss.n)
            barrier()
        with contextlib.ExitStack() as ph:
            w1s = [ph.enter_context(sbt(f"w1s{i}", [128, 8, 512], BF16)) for i in range(2)]
            w2h = [ph.enter_context(sbt(f"w2h{i}", [128, 8, 128], BF16))
                   for i in range(2)]
            woh = [ph.enter_context(sbt(f"woh{i}", [128, 4, 128], BF16))
                   for i in range(2)]
            hT = ph.enter_context(sbt("hT", [128, 16, 512], BF16))
            rT_ = ph.enter_context(sbt("rT_", [128, 8, 512], BF16))
            xls = ph.enter_context(sbt("xls_f1", [128, 8, 512], BF16))
            t1b = ph.enter_context(sbt("t1b", [128, 512], F32))
            t2b = ph.enter_context(sbt("t2b", [128, 512], F32))
            barrier()
            w2d = {0: 0, 1: 0}
            w2r = {0: 0, 1: 0}
            wod = {0: 0, 1: 0}
            wor = {0: 0, 1: 0}

            def w2_issue(hh):
                hm = hh % 16
                cc2, hf2 = hm // 2, hm % 2
                q_, qs = (sy_, sd) if hh % 2 == 0 else (nc.scalar, sa)
                q_.wait_ge(stt.h, w2r[hh % 2])
                q_.wait_ge(qs.h, qs.n)
                qs.inc(q_.dma_start(
                    w2h[hh % 2][:, :, :],
                    scr_w2s.rearrange("(j p) c -> p j c", p=128)
                    [:, hf2 * 8:hf2 * 8 + 8, cc2 * 128:(cc2 + 1) * 128]), 16)
                w2d[hh % 2] = qs.n

            def wo_issue(hh):
                hm = hh % 16
                jc2, hf2 = hm // 2, hm % 2
                g_.wait_ge(stt.h, wor[hh % 2])
                g_.wait_ge(sg.h, sg.n)
                sg.inc(g_.dma_start(
                    woh[hh % 2][:, :, :],
                    scr_wo.rearrange("(k p) c -> p k c", p=128)
                    [:, hf2 * 4:hf2 * 4 + 4, jc2 * 128:(jc2 + 1) * 128]), 16)
                wod[hh % 2] = sg.n

            w2_issue(0)
            w2_issue(1)
            wo_issue(0)
            wo_issue(1)
            for tsub in range(NS):
                tl = slice(tsub * 512, (tsub + 1) * 512)
                sy_.wait_ge(stt.h, stt.n)
                sy_.wait_ge(sv.h, sv.n)
                sy_.wait_ge(sd.h, sd.n)
                srcx = scr_xln.rearrange("(k p) t -> p k t", p=128)[:, :, tl]
                sd.inc(sy_.dma_start(xls[:, :, :], srcx), 16)
                t_.wait_ge(sd.h, sd.n)
                evd = [0, 0]
                w1_ready = [0, 0]
                for fq in range(4):
                    wbuf = w1s[fq % 2]
                    sy_.wait_ge(stt.h, w1_ready[fq % 2])
                    sy_.wait_ge(sd.h, sd.n)
                    sd.inc(sy_.dma_start(
                        wbuf[:, :, :],
                        scr_w1s.rearrange("(k p) f -> p k f", p=128)[:, :, fq * 512:(fq + 1) * 512]),
                        16)
                    t_.wait_ge(sd.h, sd.n)
                    for f4 in range(4):
                        fcc = fq * 4 + f4
                        b = fcc % 2
                        t_.wait_ge(ss.h, evd[b])
                        for ci in range(8):
                            ins = t_.matmul(bank(b), wbuf[:, ci, f4 * 128:(f4 + 1) * 128],
                                            xls[:, ci, :], start=(ci == 0), stop=(ci == 7))
                        stt.inc(ins)
                        s_.wait_ge(stt.h, stt.n)
                        ss.inc(s_.activation(hT[:, fcc, :], bank(b), AF.Gelu,
                                             bias=b1g[:, fcc:fcc + 1]))
                        evd[b] = ss.n
                    w1_ready[fq % 2] = stt.n
                rTev = {2: 0, 3: 0}
                for cc in range(8):
                    b = 2 + cc % 2
                    for hf2 in range(2):
                        hh = tsub * 16 + cc * 2 + hf2
                        t_.wait_ge((sd if hh % 2 == 0 else sa).h, w2d[hh % 2])
                        if hf2 == 0:
                            t_.wait_ge(ss.h, ss.n if cc == 0 else rTev[b])
                        for fc in range(8):
                            ins = t_.matmul(bank(b), w2h[hh % 2][:, fc, :],
                                            hT[:, hf2 * 8 + fc, :],
                                            start=(hf2 == 0 and fc == 0),
                                            stop=(hf2 == 1 and fc == 7))
                        stt.inc(ins)
                        w2r[hh % 2] = stt.n
                        if hh + 2 < NS * 16:
                            w2_issue(hh + 2)
                    s_.wait_ge(stt.h, stt.n)
                    ss.inc(s_.activation(rT_[:, cc, :], bank(b), AF.Identity,
                                         bias=b1g[:, 16 + cc:17 + cc]))
                    rTev[b] = ss.n
                outev = {4: 0, 5: 0}
                for jc in range(8):
                    b = 4 + jc % 2
                    for hf2 in range(2):
                        hh = tsub * 16 + jc * 2 + hf2
                        t_.wait_ge(sg.h, wod[hh % 2])
                        if hf2 == 0:
                            if jc == 0:
                                t_.wait_ge(sv.h, sv.n)
                                t_.wait_ge(ss.h, ss.n)
                            else:
                                t_.wait_ge(ss.h, outev[b])
                        for hc in range(4):
                            ins = t_.matmul(bank(b), woh[hh % 2][:, hc, :],
                                            rT_[:, hf2 * 4 + hc, :],
                                            start=(hf2 == 0 and hc == 0),
                                            stop=(hf2 == 1 and hc == 3))
                        stt.inc(ins)
                        wor[hh % 2] = stt.n
                        if hh + 2 < NS * 16:
                            wo_issue(hh + 2)
                    s_.wait_ge(stt.h, stt.n)
                    s_.wait_ge(sv.h, sv.n)
                    ss.inc(s_.activation(t1b[:, :], bank(b), AF.Identity, scale=0.3,
                                         bias=bias_slice("F1_bo_s3", jc)))
                    outev[b] = ss.n
                    v_.wait_ge(ss.h, ss.n)
                    sv.inc(v_.tensor_tensor(t2b[:, :], t1b[:, :], xls[:, jc, :],
                                            op=ALU.add))
                    sv.inc(v_.drain())
                    sv.inc(v_.tensor_tensor(XT[:, jc, tl], XT[:, jc, tl], t2b[:, :],
                                            op=ALU.add))
                    sv.inc(v_.drain())
            barrier()

    def scoped(name, fn):
        with nc.named_scope(name):
            fn()

    for ph_name in phases:
        if ph_name == "dbg_qw":
            dq1 = nc.dram_tensor("dbg_qw1", [C, FF], F32, kind="ExternalOutput").ap()
            dq2 = nc.dram_tensor("dbg_qw2", [FF, C], F32, kind="ExternalOutput").ap()
            quant_weight_prep(din["F0_w1"], FF, C, scr_qw1, 250)
            quant_weight_prep(din["F0_w2"], C, FF, scr_qw2, 251)
            barrier()
            with sbt("dbgbuf", [128, FF], BF16) as db, sbt("dbgbuff", [128, FF], F32) as dbf:
                for rr in range(8):
                    sy_.wait_ge(sd.h, sd.n)
                    sy_.wait_ge(sv.h, sv.n)
                    sd.inc(sy_.dma_start(db[:, :], scr_qw1[rr * 128:(rr + 1) * 128, :]), 16)
                    v_.wait_ge(sd.h, sd.n)
                    sv.inc(v_.tensor_copy(dbf[:, :], db[:, :]))
                    sv.inc(v_.drain())
                    sy_.wait_ge(sv.h, sv.n)
                    sd.inc(sy_.dma_start(dq1[rr * 128:(rr + 1) * 128, :], dbf[:, :]), 16)
                for rr in range(32):
                    sy_.wait_ge(sd.h, sd.n)
                    sy_.wait_ge(sv.h, sv.n)
                    sd.inc(sy_.dma_start(db[:, 0:C], scr_qw2[rr * 128:(rr + 1) * 128, :]), 16)
                    v_.wait_ge(sd.h, sd.n)
                    sv.inc(v_.tensor_copy(dbf[:, 0:C], db[:, 0:C]))
                    sv.inc(v_.drain())
                    sy_.wait_ge(sv.h, sv.n)
                    sd.inc(sy_.dma_start(dq2[rr * 128:(rr + 1) * 128, :], dbf[:, 0:C]), 16)
            sy_.wait_ge(sd.h, sd.n)
        elif ph_name == "l0a":
            scoped("ln0", ln_and_xln); scoped("mla", mla_attn)
        elif ph_name == "f0":
            scoped("ln1", ln_and_xln); scoped("f0", f0_ffn)
        elif ph_name == "l1a":
            scoped("ln2", ln_and_xln); scoped("qkv1", mha_qkv)
            scoped("mha", lambda: mha_attn_core("L1_Wo", "L1_bo", "L1_bv"))
        elif ph_name == "f1":
            scoped("ln3", ln_and_xln); scoped("f1", f1_ffn)
    scoped("epi", epilogue)

    st.close()
    return nc, din


def kernel(**inputs):
    T = inputs["L0_attn_x"].shape[1]
    nc, _ = build(T=T)
    ident = np.eye(128, dtype=np.float32)
    base = {
        "ident": ident,
        "ident_bf": ident.astype(ml_dtypes.bfloat16),
        "ident_h": ident.astype(np.float16),
        "iota8": np.arange(8, dtype=np.float32)[None],
        "L0_wq_d": inputs["L0_attn_wq_d"], "L0_bq_d": inputs["L0_attn_bq_d"],
        "L0_w_qk": inputs["L0_attn_w_qk"], "L0_b_qk": inputs["L0_attn_b_qk"],
        "L0_wkv_d": inputs["L0_attn_wkv_d"], "L0_bkv_d": inputs["L0_attn_bkv_d"],
        "L0_wv_u": inputs["L0_attn_wv_u"], "L0_bv_u": inputs["L0_attn_bv_u"],
        "L0_w_proj": inputs["L0_attn_w_proj"], "L0_b_proj": inputs["L0_attn_b_proj"],
        "F0_ln1_g": inputs["L0_ffn_ln1_g"], "F0_ln1_b": inputs["L0_ffn_ln1_b"],
        "F0_w1": inputs["L0_ffn_w1"], "F0_b1": inputs["L0_ffn_b1"],
        "F0_ln2_g": inputs["L0_ffn_ln2_g"], "F0_ln2_b": inputs["L0_ffn_ln2_b"],
        "F0_w2": inputs["L0_ffn_w2"], "F0_b2": inputs["L0_ffn_b2"],
        "L1_Wq": inputs["L1_attn_Wq"], "L1_bq": inputs["L1_attn_bq"],
        "L1_Wk": inputs["L1_attn_Wk"], "L1_bk": inputs["L1_attn_bk"],
        "L1_Wv": inputs["L1_attn_Wv"], "L1_bv": inputs["L1_attn_bv"],
        "L1_Wo": inputs["L1_attn_Wo"], "L1_bo": inputs["L1_attn_bo"],
        "F1_Wp1": inputs["L1_ffn_Wp1"], "F1_bp1": inputs["L1_ffn_bp1"],
        "F1_Wp2": inputs["L1_ffn_Wp2"], "F1_bp2": inputs["L1_ffn_bp2"],
        "F1_W1": inputs["L1_ffn_W1"], "F1_b1": inputs["L1_ffn_b1"],
        "F1_W2": inputs["L1_ffn_W2"], "F1_b2": inputs["L1_ffn_b2"],
        "F1_Wo": inputs["L1_ffn_Wo"], "F1_bo": inputs["L1_ffn_bo"],
    }
    x = np.ascontiguousarray(inputs["L0_attn_x"])
    in_maps = [dict(base, x=np.ascontiguousarray(x[b])) for b in range(x.shape[0])]
    res = run_bass_kernel_spmd(nc, in_maps, core_ids=list(range(x.shape[0])))
    return np.stack([r["out"] for r in res.results], axis=0)


# version 26
# speedup vs baseline: 1.1657x; 1.1657x over previous
import contextlib
import numpy as np
import ml_dtypes
import concourse.bass as bass
import concourse.mybir as mybir
from concourse.bass_utils import run_bass_kernel_spmd

F32 = mybir.dt.float32
BF16 = mybir.dt.bfloat16
I32 = mybir.dt.int32
F16 = mybir.dt.float16
AF = mybir.ActivationFunctionType
ALU = mybir.AluOpType
AX = mybir.AxisListType

C = 1024
NH = 16
HD = 64
CQ = 512
KV = 64
FF = 4096
NAV = 128
NE = 8
FE = 2048
LN_EPS = 1e-5
S4 = float(HD ** 0.25)
ANCHOR = 8.0

MAGIC = 2048.0


class Sem:

    def __init__(self, st, nc, name):
        self.h = st.enter_context(nc.semaphore(name))
        self.n = 0

    def inc(self, instr, k=1):
        instr.then_inc(self.h, k)
        self.n += k
        return instr


class Ctx:
    def __init__(self, nc, st, T):
        self.nc = nc
        self.st = st
        self.T = T
        self.NT = T // 128
        self.NS = T // 512


def sb(ctx, name, shape, dtype):
    return ctx.st.enter_context(ctx.nc.sbuf_tensor(name, shape, dtype))


def build(T=4096, phases=("l0a", "f0", "l1a", "f1")):
    nc = bass.Bass("TRN2", target_bir_lowering=False)
    st = contextlib.ExitStack()
    ctx = Ctx(nc, st, T)
    NT, NS = ctx.NT, ctx.NS

    din = {}

    def dram_in(name, shape, dtype=F32):
        din[name] = nc.dram_tensor(name, list(shape), dtype, kind="ExternalInput").ap()
        return din[name]

    x_in = dram_in("x", (T, C))
    out_hbm = nc.dram_tensor("out", [T, C], F32, kind="ExternalOutput").ap()

    dram_in("L0_wq_d", (CQ, C)); dram_in("L0_bq_d", (CQ,))
    dram_in("L0_w_qk", (C, CQ)); dram_in("L0_b_qk", (C,))
    dram_in("L0_wkv_d", (KV, C)); dram_in("L0_bkv_d", (KV,))
    dram_in("L0_wv_u", (C, KV)); dram_in("L0_bv_u", (C,))
    dram_in("L0_w_proj", (C, C)); dram_in("L0_b_proj", (C,))
    dram_in("F0_ln1_g", (C,)); dram_in("F0_ln1_b", (C,))
    dram_in("F0_w1", (FF, C)); dram_in("F0_b1", (FF,))
    dram_in("F0_ln2_g", (FF,)); dram_in("F0_ln2_b", (FF,))
    dram_in("F0_w2", (C, FF)); dram_in("F0_b2", (C,))
    dram_in("L1_Wq", (C, C)); dram_in("L1_bq", (C,))
    dram_in("L1_Wk", (C, C)); dram_in("L1_bk", (C,))
    dram_in("L1_Wv", (C, C)); dram_in("L1_bv", (C,))
    dram_in("L1_Wo", (C, C)); dram_in("L1_bo", (C,))
    dram_in("F1_Wp1", (C, NAV)); dram_in("F1_bp1", (NAV,))
    dram_in("F1_Wp2", (NAV, NE)); dram_in("F1_bp2", (NE,))
    dram_in("F1_W1", (NE, C, FE)); dram_in("F1_b1", (NE, FE))
    dram_in("F1_W2", (NE, FE, C)); dram_in("F1_b2", (NE, C))
    dram_in("F1_Wo", (C, C)); dram_in("F1_bo", (C,))
    dram_in("ident", (128, 128))
    dram_in("ident_bf", (128, 128), BF16)
    dram_in("ident_h", (128, 128), F16)
    dram_in("iota8", (1, 8))

    scr_xln = nc.dram_tensor("scr_xln", [C, T], BF16, kind="Internal").ap()
    scr_q = nc.dram_tensor("scr_q", [C, T], BF16, kind="Internal").ap()
    scr_k = nc.dram_tensor("scr_k", [C, T], BF16, kind="Internal").ap()
    scr_v = nc.dram_tensor("scr_v", [T, C], BF16, kind="Internal").ap()
    scr_wt = nc.dram_tensor("scr_wt", [C, C], BF16, kind="Internal").ap()
    scr_qw1 = nc.dram_tensor("scr_qw1", [C, FF], BF16, kind="Internal").ap()
    scr_qw2 = nc.dram_tensor("scr_qw2", [FF, C], BF16, kind="Internal").ap()
    scr_w1s = nc.dram_tensor("scr_w1s", [C, FE], BF16, kind="Internal").ap()
    scr_w2s = nc.dram_tensor("scr_w2s", [FE, C], BF16, kind="Internal").ap()
    scr_wo = nc.dram_tensor("scr_wo", [C, C], BF16, kind="Internal").ap()

    XT = sb(ctx, "XT", [128, 8, T], F32)
    rows = sb(ctx, "rows", [1, 5, 512], F32)
    ones_f1 = sb(ctx, "ones_f1", [1, 128], F32)
    ones_b1 = sb(ctx, "ones_b1", [1, 128], BF16)
    ones_c128 = sb(ctx, "ones_c128", [128, 1], F32)
    ones_c128b = sb(ctx, "ones_c128b", [128, 1], BF16)
    negones64 = sb(ctx, "negones64", [64, 1], BF16)
    ident_f = sb(ctx, "ident_f", [128, 128], F32)
    ident_h = sb(ctx, "ident_h_sb", [128, 128], F16)
    bias_pool = sb(ctx, "bias_pool", [128, 256], F32)
    BIAS = {}

    PS = ctx.st.enter_context(nc.psum_tensor("PS", [128, 8, 512], F32))
    PS16 = PS.bitcast(F16)

    sd = Sem(st, nc, "sd")
    sa = Sem(st, nc, "sa")
    sg = Sem(st, nc, "sg")
    stt = Sem(st, nc, "stt")
    sv = Sem(st, nc, "sv")
    ss = Sem(st, nc, "ss")
    sc_ = Sem(st, nc, "sc_")
    sq_ = Sem(st, nc, "sq_")
    sw_ = Sem(st, nc, "sw_")

    blk = st.enter_context(nc.Block())
    t_, v_, s_, sy_, g_ = nc.tensor, nc.vector, nc.scalar, nc.sync, nc.gpsimd
    _names = {}

    def sbt(name, shape, dtype):
        k = _names.get(name, 0)
        _names[name] = k + 1
        return nc.sbuf_tensor(name if k == 0 else name + "__" + str(k), shape, dtype)

    def bank(b, n=1, p0=0, p1=128):
        return PS[p0:p1, b:b + n, :]

    def wait_all(eng):
        eng.wait_ge(sc_.h, sc_.n)
        eng.wait_ge(sq_.h, sq_.n)
        eng.wait_ge(sw_.h, sw_.n)
        eng.wait_ge(stt.h, stt.n)
        eng.wait_ge(sv.h, sv.n)
        eng.wait_ge(ss.h, ss.n)
        eng.wait_ge(sd.h, sd.n)
        eng.wait_ge(sa.h, sa.n)
        eng.wait_ge(sg.h, sg.n)

    def barrier():
        for eng in (t_, v_, s_, sy_, g_):
            wait_all(eng)

    sc_.inc(sy_.dma_start(ident_f[:, :], din["ident"]), 16)
    sc_.inc(sy_.dma_start(ident_h[:, :], din["ident_h"]), 16)

    bias_col = [0]

    def load_bias(name, vec_ap, n, queue=sy_, qsem=sc_):
        c0 = bias_col[0]
        k = (n + 127) // 128
        assert n % 128 == 0 or n < 128
        ap = vec_ap.rearrange("(k p) -> p k", p=min(n, 128)) if n >= 128 else vec_ap
        if n >= 128:
            qsem.inc(queue.dma_start(bias_pool[:, c0:c0 + k], ap, allow_slow_non_contiguous=True), 16)
        else:
            qsem.inc(queue.dma_start(bias_pool[0:n, c0:c0 + 1], vec_ap.rearrange("(n o) -> n o", o=1), allow_slow_non_contiguous=True), 16)
        BIAS[name] = (c0, k)
        bias_col[0] += k

    def load_bias64(name, vec_ap, n):
        c0 = bias_col[0]
        k = n // 64
        ap = vec_ap.rearrange("(h d) -> d h", d=64)
        sc_.inc(sy_.dma_start(bias_pool[0:64, c0:c0 + k], ap,
                              allow_slow_non_contiguous=True), 16)
        BIAS[name + "_64"] = (c0, k)
        bias_col[0] += k

    def bias64_slice(name, h):
        c0, k = BIAS[name + "_64"]
        return bias_pool[0:64, c0 + h:c0 + h + 1]

    for nm in ["L0_bq_d", "L0_b_qk", "L0_bkv_d", "L0_bv_u", "L0_b_proj",
               "F0_ln1_g", "F0_ln1_b", "F0_b2",
               "L1_bq", "L1_bk", "L1_bv", "L1_bo", "F1_bo"]:
        n = din[nm].tensor.shape[0]
        load_bias(nm, din[nm], n)
    for nm in ["F0_b1", "F0_ln2_g", "F0_ln2_b"]:
        load_bias(nm, din[nm], FF)
    load_bias64("L0_b_qk", din["L0_b_qk"], C)

    def bias_slice(name, j, p0=0, p1=128):
        c0, k = BIAS[name]
        return bias_pool[p0:p1, c0 + j:c0 + j + 1]

    consts = sb(ctx, "consts", [128, 8], F32)
    cc_eps = sb(ctx, "cc_eps", [128, 1], F32)
    v_.memset(cc_eps[:, :], LN_EPS)
    v_.memset(consts[:, 0:1], -ANCHOR)
    v_.memset(consts[:, 1:2], 0.0)
    v_.memset(ones_f1[:, :], 1.0)
    v_.memset(ones_b1[:, :], 1.0)
    v_.memset(ones_c128[:, :], 1.0)
    v_.memset(ones_c128b[:, :], 1.0)
    v_.memset(negones64[:, :], -1.0)
    v_.memset(rows[:, :, :], 0.0)
    sv.inc(v_.drain())

    v_.wait_ge(sc_.h, sc_.n)
    for nm, scale in [("L1_bq", 1.0 / S4), ("L1_bk", 1.0 / S4), ("L0_b_qk", 0.125),
                      ("F1_bo", 0.3)]:
        c0, k = BIAS[nm]
        c1 = bias_col[0]
        v_.tensor_scalar(bias_pool[:, c1:c1 + k], bias_pool[:, c0:c0 + k], scale, None,
                         op0=ALU.mult)
        BIAS[nm + ("_s3" if nm == "F1_bo" else "_s")] = (c1, k)
        bias_col[0] += k
    c0, k = BIAS["L0_b_qk_64"]
    c1 = bias_col[0]
    v_.tensor_scalar(bias_pool[0:64, c1:c1 + k], bias_pool[0:64, c0:c0 + k], 0.125, None,
                     op0=ALU.mult)
    BIAS["L0_b_qk_s_64"] = (c1, k)
    bias_col[0] += k
    sv.inc(v_.drain())

    with sbt("xin_stage", [128, C], F32) as xin_stage:
      for tc in range(NT):
          sd.inc(sy_.dma_start(xin_stage[:, :], x_in[tc * 128:(tc + 1) * 128, :]), 16)
          t_.wait_ge(sd.h, sd.n)
          t_.wait_ge(sc_.h, sc_.n)
          if tc > 0:
              t_.wait_ge(ss.h, ss.n)
          for cc in range(8):
              b = cc // 4
              off = cc % 4
              tgt = PS[:, b, :].rearrange("p (o q) -> p o q", o=4)[:, off, :]
              ins = t_.transpose(tgt, xin_stage[:, cc * 128:(cc + 1) * 128], ident_f[:, :])
          stt.inc(ins)
          s_.wait_ge(stt.h, stt.n)
          for cc in range(8):
              b = cc // 4
              off = cc % 4
              src = PS[:, b, :].rearrange("p (o q) -> p o q", o=4)[:, off, :]
              ins = s_.copy(XT[:, cc, tc * 128:(tc + 1) * 128], src)
          ss.inc(ins)
          sy_.wait_ge(stt.h, stt.n)


    barrier()

    def ln_and_xln():
        with contextlib.ExitStack() as ph:
            xst = [ph.enter_context(sbt(f"xls_ln{i}", [128, 8, 512], BF16))
                   for i in range(2)]
            xl = [ph.enter_context(sbt(f"xl_ln{i}", [128, 512], F32)) for i in range(4)]
            sq2 = [ph.enter_context(sbt(f"sq_ln{i}", [128, 512], F32)) for i in range(2)]
            rows2 = ph.enter_context(sbt("rows2_ln", [1, 5, 512], F32))
            R = [rows, rows2]
            BK = {0: (6, 7, 4, 5), 1: (2, 3, 0, 1)}
            lnss = [0, 0]
            lnsv = [0, 0]
            lndma = [0, 0]
            sqrd = {0: 0, 1: 0}
            mmN, mmB = {}, {}

            def stA(t):
                tl = slice(t * 512, (t + 1) * 512)
                p = t % 2
                s6, s7, _, _ = BK[p]
                for cc in range(8):
                    v_.wait_ge(stt.h, sqrd[cc % 2])
                    sv.inc(v_.tensor_tensor(sq2[cc % 2][:, :], XT[:, cc, tl],
                                            XT[:, cc, tl], op=ALU.mult))
                    t_.wait_ge(sv.h, sv.n)
                    if cc == 0:
                        t_.wait_ge(ss.h, lnss[p])
                    stt.inc(t_.matmul(bank(s6, 1, 0, 1), ones_c128[:, :], XT[:, cc, tl],
                                      start=(cc == 0), stop=(cc == 7)))
                    stt.inc(t_.matmul(bank(s7, 1, 0, 1), ones_c128[:, :], sq2[cc % 2][:, :],
                                      start=(cc == 0), stop=(cc == 7)))
                    sqrd[cc % 2] = stt.n
                mmN[t] = stt.n

            def stB(t):
                p = t % 2
                s6, s7, b4, b5 = BK[p]
                R_ = R[p]
                v_.wait_ge(stt.h, mmN[t])
                v_.tensor_scalar(R_[:, 0, :], bank(s6, 1, 0, 1), 1.0 / C, None,
                                 op0=ALU.mult)
                sv.inc(v_.drain())
                s_.wait_ge(sv.h, sv.n)
                ss.inc(s_.activation(R_[:, 4, :], R_[:, 0, :], AF.Square))
                ss.inc(s_.activation(R_[:, 3, :], bank(s7, 1, 0, 1), AF.Identity,
                                     scale=1.0 / C, bias=cc_eps[0:1, 0:1]))
                lnss[p] = ss.n
                v_.wait_ge(ss.h, ss.n)
                sv.inc(v_.tensor_tensor(R_[:, 3, :], R_[:, 3, :], R_[:, 4, :],
                                        op=ALU.subtract))
                s_.wait_ge(sv.h, sv.n)
                ss.inc(s_.activation(R_[:, 4, :], R_[:, 3, :], AF.Ln))
                ss.inc(s_.drain())
                ss.inc(s_.activation(R_[:, 1, :], R_[:, 4, :], AF.Exp, scale=-0.5))
                v_.wait_ge(ss.h, ss.n)
                sv.inc(v_.tensor_tensor(R_[:, 2, :], R_[:, 0, :], R_[:, 1, :],
                                        op=ALU.mult))
                sv.inc(v_.drain())
                if t == NS - 1:
                    sv.inc(v_.tensor_copy(consts[0:1, 2:3], R_[0:1, 0, 511:512]))
                    sv.inc(v_.tensor_copy(consts[0:1, 3:4], R_[0:1, 1, 511:512]))
                    sv.inc(v_.drain())
                t_.wait_ge(sv.h, sv.n)
                t_.wait_ge(sv.h, lnsv[p])
                stt.inc(t_.matmul(bank(b4, 1), ones_f1[:, :], R_[:, 1, :],
                                  start=True, stop=True))
                stt.inc(t_.matmul(bank(b5, 1), ones_f1[:, :], R_[:, 2, :],
                                  start=True, stop=True))
                mmB[t] = stt.n

            def stC(t):
                tl = slice(t * 512, (t + 1) * 512)
                p = t % 2
                _, _, b4, b5 = BK[p]
                v_.wait_ge(stt.h, mmB[t])
                v_.wait_ge(sd.h, lndma[p])
                for cc in range(8):
                    sv.inc(v_.tensor_tensor(xl[cc % 4][:, :], XT[:, cc, tl], bank(b4, 1),
                                            op=ALU.mult))
                    if cc % 4 == 3:
                        sv.inc(v_.drain())
                        for cc2 in range(cc - 3, cc + 1):
                            sv.inc(v_.tensor_tensor(xst[p][:, cc2, :], xl[cc2 % 4][:, :],
                                                    bank(b5, 1), op=ALU.subtract))
                        sv.inc(v_.drain())
                lnsv[p] = sv.n
                sy_.wait_ge(sv.h, sv.n)
                sy_.wait_ge(sd.h, sd.n)
                dst = scr_xln.rearrange("(k p) t -> p k t", p=128)[:, :, tl]
                sd.inc(sy_.dma_start(dst, xst[p][:, :, :]), 16)
                lndma[p] = sd.n

            stA(0)
            stB(0)
            for t in range(NS):
                if t + 1 < NS:
                    stA(t + 1)
                stC(t)
                if t + 1 < NS:
                    stB(t + 1)
            barrier()

    _wtp_id = [0]

    def wt_prep(w_ap, rows_n, cols_n, dst_sb, scale=1.0):
        nr, nc_ = (rows_n + 127) // 128, (cols_n + 127) // 128
        _wtp_id[0] += 1
        _wst = contextlib.ExitStack()
        wnat = _wst.enter_context(
            sbt("wnat%d" % _wtp_id[0], [128, cols_n], F32))
        for rr in range(nr):
            h_ = min(128, rows_n - rr * 128)
            sy_.wait_ge(stt.h, stt.n)
            sd.inc(sy_.dma_start(wnat[0:h_, 0:cols_n], w_ap[rr * 128:rr * 128 + h_, :]), 16)
            t_.wait_ge(sd.h, sd.n)
            t_.wait_ge(ss.h, ss.n)
            for jj in range(nc_):
                b = jj % 2
                off = jj // 2 % 2
                w_ = min(128, cols_n - jj * 128)
                tgt = PS[:, b, :].rearrange("p (o q) -> p o q", o=4)[0:w_, off, 0:h_]
                stt.inc(t_.transpose(tgt, wnat[0:h_, jj * 128:jj * 128 + w_],
                                     ident_f[0:h_, 0:h_]))
                if jj % 4 == 3 or jj == nc_ - 1:
                    s_.wait_ge(stt.h, stt.n)
                    for jj2 in range(jj - jj % 4, jj + 1):
                        b2 = jj2 % 2
                        off2 = jj2 // 2 % 2
                        w2 = min(128, cols_n - jj2 * 128)
                        src = PS[:, b2, :].rearrange("p (o q) -> p o q", o=4)[0:w2, off2, 0:h_]
                        ins = s_.activation(dst_sb[0:w2, jj2, rr * 128:rr * 128 + h_], src,
                                            AF.Identity, scale=scale)
                    ss.inc(ins)
                    t_.wait_ge(ss.h, ss.n)
        barrier()
        _wst.close()

    def wt_to_scr(w_name, scale=1.0):
        with contextlib.ExitStack() as ph:
            wt_full = ph.enter_context(sbt("wtf_" + w_name, [128, 8, C], BF16))
            wt_prep(din[w_name], C, C, wt_full, scale=scale)
            barrier()
            sd.inc(sy_.dma_start(scr_wt.rearrange("(k p) c -> p k c", p=128),
                                 wt_full[:, :, :]), 16)
            barrier()

    def epilogue():
        barrier()
        xout_stage = sb(ctx, "xout_stage", [128, C], F32)
        for tc in range(NT):
            t_.wait_ge(sd.h, sd.n)
            for cc in range(8):
                b = cc // 4
                off = cc % 4
                tgt = PS[:, b, :].rearrange("p (o q) -> p o q", o=4)[:, off, :]
                ins = t_.transpose(tgt, XT[:, cc, tc * 128:(tc + 1) * 128], ident_f[:, :])
            stt.inc(ins)
            s_.wait_ge(stt.h, stt.n)
            for cc in range(8):
                b = cc // 4
                off = cc % 4
                src = PS[:, b, :].rearrange("p (o q) -> p o q", o=4)[:, off, :]
                ins = s_.copy(xout_stage[:, cc * 128:(cc + 1) * 128], src)
            ss.inc(ins)
            sy_.wait_ge(ss.h, ss.n)
            sd.inc(sy_.dma_start(out_hbm[tc * 128:(tc + 1) * 128, :], xout_stage[:, :]), 16)
            t_.wait_ge(ss.h, ss.n)
        sy_.wait_ge(sd.h, sd.n)

    xadd_state = {"done": 0}
    wtp = [sb(ctx, "wtp0", [128, C], BF16), sb(ctx, "wtp1", [128, C], BF16)]
    wtp_last = [0, 0]

    def proj_pair(bo_name, pair, first, y_pair, wn=None):
        slab = wtp[pair % 2]
        sy_.wait_ge(stt.h, wtp_last[pair % 2])
        sy_.wait_ge(sw_.h, sw_.n)
        sw_.inc(sy_.dma_start(slab[:, :], scr_wt[pair * 128:(pair + 1) * 128, :]), 16)
        t_.wait_ge(sw_.h, sw_.n)
        nts = y_pair.shape[-1] // 512
        toff = 0 if wn is None else wn * (y_pair.shape[-1])
        t_.wait_ge(sv.h, sv.n)
        t_.wait_ge(ss.h, ss.n)
        add_done = {0: 0, 1: 0}
        for cc in range(8):
            for ts2 in range(nts):
                b = (cc * nts + ts2) % 2
                t_.wait_ge(sv.h, add_done[b])
                stt.inc(t_.matmul(bank(6 + b), slab[:, cc * 128:(cc + 1) * 128],
                                  y_pair[:, ts2 * 512:(ts2 + 1) * 512],
                                  start=True, stop=True))
                v_.wait_ge(stt.h, stt.n)
                tl2 = slice(toff + ts2 * 512, toff + (ts2 + 1) * 512)
                if first:
                    ins = v_.scalar_tensor_tensor(XT[:, cc, tl2], bank(6 + b),
                                                  bias_slice(bo_name, cc), XT[:, cc, tl2],
                                                  op0=ALU.add, op1=ALU.add)
                else:
                    ins = v_.tensor_tensor(XT[:, cc, tl2], XT[:, cc, tl2], bank(6 + b),
                                           op=ALU.add)
                sv.inc(ins)
                add_done[b] = sv.n
        wtp_last[pair % 2] = stt.n
        sv.inc(v_.drain())
        xadd_state["done"] = sv.n

    def mha_qkv():
        with contextlib.ExitStack() as ph:
            wt_q = ph.enter_context(sbt("wt_q", [128, 8, C], BF16))
            wt_k = ph.enter_context(sbt("wt_k", [128, 8, C], BF16))
            stage_qk = ph.enter_context(sbt("stage_qk", [128, 4, 512], BF16))
            xls2 = [ph.enter_context(sbt(f"xls_qkv{i}", [128, 8, 512], BF16))
                    for i in range(2)]
            vstage = ph.enter_context(sbt("vstage", [128, 2, 1024], BF16))
            wt_prep(din["L1_Wq"], C, C, wt_q, scale=1.0 / S4)
            wt_prep(din["L1_Wk"], C, C, wt_k, scale=1.0 / S4)
            wt_v = wt_q
            barrier()
            xdma = [0, 0]
            xread = [0, 0]

            def x_issue(ts):
                buf = ts % 2
                sy_.wait_ge(stt.h, xread[buf])
                sy_.wait_ge(sq_.h, sq_.n)
                src = scr_xln.rearrange("(k p) t -> p k t", p=128)[
                    :, :, ts * 512:(ts + 1) * 512]
                sq_.inc(sy_.dma_start(xls2[buf][:, :, :], src), 16)
                xdma[buf] = sq_.n

            evac_done = [0, 0, 0, 0]
            x_issue(0)
            for tsub in range(NS):
                tl = slice(tsub * 512, (tsub + 1) * 512)
                xls = xls2[tsub % 2]
                if tsub + 1 < NS:
                    x_issue(tsub + 1)
                t_.wait_ge(sq_.h, xdma[tsub % 2])
                s_.wait_ge(sd.h, sd.n)
                for mat, wt, bname, scr in ((0, wt_q, "L1_bq_s", scr_q),
                                            (1, wt_k, "L1_bk_s", scr_k)):
                    s_.wait_ge(sd.h, sd.n)
                    for co in range(8):
                        b = co % 2
                        if co == 4:
                            s_.wait_ge(sd.h, sd.n)
                        t_.wait_ge(ss.h, evac_done[b])
                        for ci in range(8):
                            ins = t_.matmul(bank(b), wt[:, ci, co * 128:(co + 1) * 128],
                                            xls[:, ci, :], start=(ci == 0), stop=(ci == 7))
                        stt.inc(ins)
                        s_.wait_ge(stt.h, stt.n)
                        ss.inc(s_.activation(stage_qk[:, co % 4, :], bank(b), AF.Identity,
                                             bias=bias_slice(bname, co)))
                        evac_done[b] = ss.n
                        if co % 4 == 3:
                            sy_.wait_ge(ss.h, ss.n)
                            h4 = co // 4
                            dstm = scr.rearrange("(k p) t -> p k t",
                                                 p=128)[:, h4 * 4:h4 * 4 + 4, tl]
                            sd.inc(sy_.dma_start(dstm, stage_qk[:, :, :]), 16)
                xread[tsub % 2] = stt.n
            barrier()
            wt_prep(din["L1_Wv"], C, C, wt_v, scale=1.0)
            barrier()
            xdma = [0, 0]
            xread = [0, 0]
            x_issue2 = x_issue
            x_issue2(0)
            for tsub in range(NS):
                tl = slice(tsub * 512, (tsub + 1) * 512)
                xls = xls2[tsub % 2]
                if tsub + 1 < NS:
                    x_issue2(tsub + 1)
                t_.wait_ge(sq_.h, xdma[tsub % 2])
                s_.wait_ge(sd.h, sd.n)
                for tci in range(4):
                    if tci == 2:
                        s_.wait_ge(sd.h, sd.n)
                    for ch in range(2):
                        b = 2 + ch
                        t_.wait_ge(ss.h, evac_done[b])
                        for ci in range(8):
                            ins = t_.matmul(bank(b), xls[:, ci, tci * 128:(tci + 1) * 128],
                                            wt_v[:, ci, ch * 512:(ch + 1) * 512],
                                            start=(ci == 0), stop=(ci == 7))
                        stt.inc(ins)
                        s_.wait_ge(stt.h, stt.n)
                        ss.inc(s_.activation(vstage[:, tci % 2, ch * 512:(ch + 1) * 512],
                                             bank(b), AF.Identity))
                        evac_done[b] = ss.n
                    if tci % 2 == 1:
                        sy_.wait_ge(ss.h, ss.n)
                        t0 = tsub * 512 + (tci - 1) * 128
                        sd.inc(sy_.dma_start(
                            scr_v[t0:t0 + 256, :].rearrange("(a p) c -> p a c", p=128),
                            vstage[:, :, :]), 16)
                xread[tsub % 2] = stt.n
            barrier()

    def mha_attn_core(wo_name, bo_name, bv_name):
        NC_ = ctx.NT
        NW = T // 1024
        wt_to_scr(wo_name)
        with contextlib.ExitStack() as ph:
            kaug = ph.enter_context(sbt("kaug", [65, T], BF16))
            qaug2 = [ph.enter_context(sbt(f"qaug_{i}", [65, T], BF16))
                     for i in range(2)]
            v4 = ph.enter_context(sbt("v4", [128, NC_, 2, 66], BF16))
            e_sb = [ph.enter_context(sbt(f"e_sb{i}", [128, 1024], BF16))
                    for i in range(2)]
            qk_tmp = ph.enter_context(sbt("qk_tmp", [64, 2048], BF16))
            den_row = ph.enter_context(sbt("den_row", [1, 1024], F32))
            u_sb = ph.enter_context(sbt("u_sb", [64, 1024], F32))
            y_pair = ph.enter_context(sbt("y_pair", [128, T], BF16))
            v_.memset(v4[:, :, :, 64:65], 1.0)
            v_.memset(kaug[64:65, :], 1.0)
            sv.inc(v_.drain())
            barrier()
            exp_done = {0: 0, 2: 0}
            v4_ready = [0]
            ep_state = {0: 0, 1: 0}
            qread = [0, 0]
            anch_mm = [0]
            bdma = [0]
            for h in range(NH):
                qaug = qaug2[h % 2]
                if h % 2 == 0:
                    srcv = scr_v.rearrange("(c p) (g d) -> p c g d", p=128, d=64)
                    sy_.wait_ge(stt.h, stt.n)
                    sy_.wait_ge(sd.h, sd.n)
                    for g2 in range(2):
                        sd.inc(sy_.dma_start(v4[:, :, g2, 0:64],
                                             srcv[:, :, h + g2, :]), 16)
                    v4_ready[0] = sd.n
                sy_.wait_ge(stt.h, qread[(h + 1) % 2])
                sy_.wait_ge(sq_.h, sq_.n)
                sq_.inc(sy_.dma_start(kaug[0:64, :],
                                      scr_k[h * 64:(h + 1) * 64, :]), 16)
                if h == 0:
                    sq_.inc(sy_.dma_start(qaug[0:64, :], scr_q[0:64, :]), 16)
                if h + 1 < NH:
                    sq_.inc(sy_.dma_start(qaug2[(h + 1) % 2][0:64, :],
                                          scr_q[(h + 1) * 64:(h + 2) * 64, :]), 16)
                bdma[0] = sq_.n
                t_.wait_ge(sd.h, v4_ready[0])
                for hf2 in range(max(1, T // 2048)):
                    hl = slice(hf2 * 2048, min((hf2 + 1) * 2048, T))
                    v_.wait_ge(sq_.h, bdma[0])
                    v_.wait_ge(stt.h, anch_mm[0])
                    sv.inc(v_.tensor_tensor(qk_tmp[:, :], qaug[0:64, hl], kaug[0:64, hl],
                                            op=ALU.mult))
                    t_.wait_ge(sv.h, sv.n)
                    if hf2 == 0:
                        t_.wait_ge(ss.h, ss.n)
                    for j4 in range(4):
                        stt.inc(t_.matmul(PS[0:1, 4 + (j4 % 2) * 2 + j4 // 2, :],
                                          negones64[:, :],
                                          qk_tmp[:, j4 * 512:(j4 + 1) * 512],
                                          start=True, stop=True))
                    anch_mm[0] = stt.n
                    s_.wait_ge(stt.h, stt.n)
                    for j4 in range(4):
                        ins = s_.activation(qaug[64:65, hf2 * 2048 + j4 * 512:
                                                 hf2 * 2048 + (j4 + 1) * 512],
                                            PS[0:1, 4 + (j4 % 2) * 2 + j4 // 2, :],
                                            AF.Identity, bias=consts[0:1, 0:1])
                    ss.inc(ins)
                    t_.wait_ge(ss.h, ss.n)
                for w in range(NW):
                    wl = slice(w * 1024, (w + 1) * 1024)

                    def emit_qk(sc):
                        sb_ = (sc % 2) * 2
                        t_.wait_ge(ss.h, exp_done[sb_])
                        if sc < 2 and w == 0:
                            t_.wait_ge(sv.h, xadd_state["done"])
                        for half in range(2):
                            ins = t_.matmul(PS[:, sb_ + half, :],
                                            kaug[:, sc * 128:(sc + 1) * 128],
                                            qaug[:, w * 1024 + half * 512:
                                                 w * 1024 + (half + 1) * 512],
                                            start=True, stop=True)
                        stt.inc(ins)
                        qk_done[sc] = stt.n

                    def emit_exp(sc):
                        sb_ = (sc % 2) * 2
                        s_.wait_ge(stt.h, qk_done[sc])
                        ss.inc(s_.activation(e_sb[sc % 2][:, :],
                                             bank(sb_, 2).rearrange("p a t -> p (a t)"),
                                             AF.Exp))
                        exp_done[sb_] = ss.n
                        ex_done[sc] = ss.n

                    pb = 4 + (w % 2) * 2

                    def emit_pv(sc):
                        t_.wait_ge(ss.h, ex_done[sc])
                        if sc == 0:
                            t_.wait_ge(sv.h, ep_state[w % 2])
                        for half in range(2):
                            ins = t_.matmul(PS[0:65, pb + half, :],
                                            v4[:, sc, h % 2, 0:65],
                                            e_sb[sc % 2][:, half * 512:(half + 1) * 512],
                                            start=(sc == 0), stop=(sc == NC_ - 1))
                        stt.inc(ins)

                    qk_done, ex_done = {}, {}
                    emit_qk(0)
                    emit_exp(0)
                    for sc in range(1, NC_):
                        emit_qk(sc)
                        emit_exp(sc)
                        emit_pv(sc - 1)
                    emit_pv(NC_ - 1)
                    s_.wait_ge(stt.h, stt.n)
                    ss.inc(s_.copy(den_row[0:1, :],
                                   PS[64:65, pb:pb + 2, :].rearrange("p a t -> p (a t)")))
                    v_.wait_ge(ss.h, ss.n)
                    sv.inc(v_.reciprocal(den_row[0:1, :], den_row[0:1, :]))
                    s_.wait_ge(sv.h, sv.n)
                    ss.inc(s_.copy(u_sb[:, :],
                                   PS[0:64, pb:pb + 2, :].rearrange("p a t -> p (a t)")))
                    t_.wait_ge(ss.h, ss.n)
                    stt.inc(t_.matmul(PS[0:64, pb, :], ones_f1[0:1, 0:64],
                                      den_row[0:1, 0:512], start=True, stop=True))
                    stt.inc(t_.matmul(PS[0:64, pb + 1, :], ones_f1[0:1, 0:64],
                                      den_row[0:1, 512:1024], start=True, stop=True))
                    v_.wait_ge(stt.h, stt.n)
                    sv.inc(v_.tensor_tensor(u_sb[:, :], u_sb[:, :],
                                            PS[0:64, pb:pb + 2, :].rearrange("p a t -> p (a t)"),
                                            op=ALU.mult))
                    ep_state[w % 2] = sv.n
                    s_.wait_ge(sv.h, sv.n)
                    p0 = (h % 2) * 64
                    ss.inc(s_.activation(y_pair[p0:p0 + 64, wl], u_sb[:, :], AF.Identity,
                                         bias=bias_slice(bv_name, h // 2, p0, p0 + 64)))
                qread[h % 2] = stt.n
                if h % 2 == 1:
                    proj_pair(bo_name, h // 2, h == 1, y_pair)
            barrier()

    def mla_attn():
        NC_ = ctx.NT
        NW = T // 1024
        wt_to_scr("L0_w_proj")
        with contextlib.ExitStack() as ph:
            ckvT = ph.enter_context(sbt("ckvT", [65, T], BF16))
            ckv_tok = ph.enter_context(sbt("ckv_tok", [128, NC_, 66], BF16))
            wvuT = ph.enter_context(sbt("wvuT", [65, C], BF16))
            bkv_row = ph.enter_context(sbt("bkv_row", [1, 64], F32))
            tok_tmp = ph.enter_context(sbt("tok_tmp", [128, 64], F32))
            with contextlib.ExitStack() as phw:
                wvuT_tmp = phw.enter_context(sbt("wvuT_tmp", [128, 1, C], BF16))
                wt_prep(din["L0_wv_u"], C, KV, wvuT_tmp, scale=1.0)
                ss.inc(s_.drain())
                ss.inc(s_.copy(wvuT[0:64, :], wvuT_tmp[0:64, 0, :]))
                barrier()
            sg.inc(g_.dma_start(wvuT[64:65, :],
                                din["L0_bv_u"].rearrange("(o n) -> o n", o=1)), 16)
            sc_.inc(sy_.dma_start(bkv_row[0:1, :],
                                  din["L0_bkv_d"].rearrange("(o n) -> o n", o=1)), 16)
            v_.memset(ckv_tok[:, :, 64:65], 1.0)
            v_.memset(ckvT[64:65, :], 1.0)
            sv.inc(v_.drain())
            barrier()
            with contextlib.ExitStack() as ph2:
                wqdT = ph2.enter_context(sbt("wqdT", [128, 8, CQ], BF16))
                wqkT = ph2.enter_context(sbt("wqkT", [128, 4, C], BF16))
                wkvdT = ph2.enter_context(sbt("wkvdT", [128, 8, KV], BF16))
                xls = ph2.enter_context(sbt("xls_mla", [128, 8, 512], BF16))
                cqc = ph2.enter_context(sbt("cqc", [128, 4, 512], BF16))
                stq = ph2.enter_context(sbt("stq", [128, 8, 512], BF16))
                wt_prep(din["L0_wq_d"], CQ, C, wqdT, scale=1.0)
                wt_prep(din["L0_w_qk"], C, CQ, wqkT, scale=0.125)
                wt_prep(din["L0_wkv_d"], KV, C, wkvdT, scale=1.0)
                barrier()
                evac_done = [0, 0, 0, 0]
                t_.wait_ge(sc_.h, sc_.n)
                stt.inc(t_.matmul(PS[:, 5, 0:64], ones_f1[0:1, :], bkv_row[0:1, :],
                                  start=True, stop=True))
                for tsub in range(NS):
                    tl = slice(tsub * 512, (tsub + 1) * 512)
                    sy_.wait_ge(stt.h, stt.n)
                    src = scr_xln.rearrange("(k p) t -> p k t", p=128)[:, :, tl]
                    sd.inc(sy_.dma_start(xls[:, :, :], src), 16)
                    t_.wait_ge(sd.h, sd.n)
                    s_.wait_ge(sd.h, sd.n)
                    for co in range(4):
                        b = co % 2
                        t_.wait_ge(ss.h, evac_done[b])
                        for ci in range(8):
                            ins = t_.matmul(bank(b), wqdT[:, ci, co * 128:(co + 1) * 128],
                                            xls[:, ci, :], start=(ci == 0), stop=(ci == 7))
                        stt.inc(ins)
                        s_.wait_ge(stt.h, stt.n)
                        ss.inc(s_.activation(cqc[:, co, :], bank(b), AF.Identity,
                                             bias=bias_slice("L0_bq_d", co)))
                        evac_done[b] = ss.n
                    t_.wait_ge(ss.h, ss.n)
                    for co in range(8):
                        b = co % 2
                        t_.wait_ge(ss.h, evac_done[b])
                        for ci in range(4):
                            ins = t_.matmul(bank(b), wqkT[:, ci, co * 128:(co + 1) * 128],
                                            cqc[:, ci, :], start=(ci == 0), stop=(ci == 3))
                        stt.inc(ins)
                        s_.wait_ge(stt.h, stt.n)
                        ss.inc(s_.activation(stq[:, co, :], bank(b), AF.Identity,
                                             bias=bias_slice("L0_b_qk_s", co)))
                        evac_done[b] = ss.n
                    sy_.wait_ge(ss.h, ss.n)
                    dstq = scr_q.rearrange("(k p) t -> p k t", p=128)[:, :, tl]
                    sd.inc(sy_.dma_start(dstq, stq[:, :, :]), 16)
                    b = 2
                    t_.wait_ge(ss.h, evac_done[b])
                    for ci in range(8):
                        ins = t_.matmul(bank(b, 1, 0, 64), wkvdT[:, ci, 0:64],
                                        xls[:, ci, :], start=(ci == 0), stop=(ci == 7))
                    stt.inc(ins)
                    s_.wait_ge(stt.h, stt.n)
                    ss.inc(s_.activation(ckvT[0:64, tl], bank(b, 1, 0, 64), AF.Identity,
                                         bias=bias_slice("L0_bkv_d", 0, 0, 64)))
                    evac_done[b] = ss.n
                    for tci in range(4):
                        sc = tsub * 4 + tci
                        b = 3
                        t_.wait_ge(sv.h, sv.n)
                        t_.wait_ge(ss.h, evac_done[b])
                        for ci in range(8):
                            ins = t_.matmul(PS[:, b, 0:64],
                                            xls[:, ci, tci * 128:(tci + 1) * 128],
                                            wkvdT[:, ci, 0:64],
                                            start=(ci == 0), stop=(ci == 7))
                        stt.inc(ins)
                        s_.wait_ge(stt.h, stt.n)
                        s_.wait_ge(sv.h, sv.n)
                        ss.inc(s_.copy(tok_tmp[:, :], PS[:, b, 0:64]))
                        evac_done[b] = ss.n
                        v_.wait_ge(ss.h, ss.n)
                        sv.inc(v_.tensor_tensor(ckv_tok[:, sc, 0:64], tok_tmp[:, :],
                                                PS[:, 5, 0:64], op=ALU.add))
                barrier()
            ep_state = {0: 0, 1: 0}
            ep_ss = {0: 0, 1: 0}
            ph3 = contextlib.ExitStack()
            qaug2 = [ph3.enter_context(sbt(f"qaug0_{i}", [65, T], BF16))
                     for i in range(2)]
            e_sb = [ph3.enter_context(sbt(f"e0_sb{i}", [128, 1024], BF16))
                    for i in range(2)]
            qk_tmp = ph3.enter_context(sbt("qk_tmp0", [64, 2048], BF16))
            den_row = ph3.enter_context(sbt("den_row0", [1, 1024], F32))
            u_sb = ph3.enter_context(sbt("u_sb0", [65, 1024], F32))
            z_bf = ph3.enter_context(sbt("z_bf", [65, 1024], BF16))
            y_pair = ph3.enter_context(sbt("y_pair0", [128, T], BF16))
            exp_done = {0: 0, 2: 0}
            qread = [0, 0]
            qdma = [0, 0]
            anch_mm = [0]
            for h in range(NH):
                qaug = qaug2[h % 2]
                if h == 0:
                    sy_.wait_ge(sq_.h, sq_.n)
                    sq_.inc(sy_.dma_start(qaug[0:64, :], scr_q[0:64, :]), 16)
                    qdma[0] = sq_.n
                if h + 1 < NH:
                    sy_.wait_ge(stt.h, qread[(h + 1) % 2])
                    sy_.wait_ge(sq_.h, sq_.n)
                    sq_.inc(sy_.dma_start(qaug2[(h + 1) % 2][0:64, :],
                                          scr_q[(h + 1) * 64:(h + 2) * 64, :]), 16)
                    qdma[(h + 1) % 2] = sq_.n
                for hf2 in range(max(1, T // 2048)):
                    hl = slice(hf2 * 2048, min((hf2 + 1) * 2048, T))
                    v_.wait_ge(sq_.h, qdma[h % 2])
                    v_.wait_ge(stt.h, anch_mm[0])
                    sv.inc(v_.tensor_tensor(qk_tmp[:, :], qaug[0:64, hl], ckvT[0:64, hl],
                                            op=ALU.mult))
                    t_.wait_ge(sv.h, sv.n)
                    if hf2 == 0:
                        t_.wait_ge(ss.h, ss.n)
                    for j4 in range(4):
                        stt.inc(t_.matmul(PS[0:1, 4 + (j4 % 2) * 2 + j4 // 2, :],
                                          negones64[:, :],
                                          qk_tmp[:, j4 * 512:(j4 + 1) * 512],
                                          start=True, stop=True))
                    anch_mm[0] = stt.n
                    s_.wait_ge(stt.h, stt.n)
                    for j4 in range(4):
                        ins = s_.activation(qaug[64:65, hf2 * 2048 + j4 * 512:
                                                 hf2 * 2048 + (j4 + 1) * 512],
                                            PS[0:1, 4 + (j4 % 2) * 2 + j4 // 2, :],
                                            AF.Identity, bias=consts[0:1, 0:1])
                    ss.inc(ins)
                    t_.wait_ge(ss.h, ss.n)
                for w in range(NW):
                    wl = slice(w * 1024, (w + 1) * 1024)

                    def emit_qk(sc):
                        sb_ = (sc % 2) * 2
                        t_.wait_ge(ss.h, exp_done[sb_])
                        if sc < 2 and w == 0:
                            t_.wait_ge(sv.h, xadd_state["done"])
                        for half in range(2):
                            ins = t_.matmul(PS[:, sb_ + half, :],
                                            ckvT[:, sc * 128:(sc + 1) * 128],
                                            qaug[:, w * 1024 + half * 512:
                                                 w * 1024 + (half + 1) * 512],
                                            start=True, stop=True)
                        stt.inc(ins)
                        qk_done[sc] = stt.n

                    def emit_exp(sc):
                        sb_ = (sc % 2) * 2
                        s_.wait_ge(stt.h, qk_done[sc])
                        ss.inc(s_.activation(e_sb[sc % 2][:, :],
                                             bank(sb_, 2).rearrange("p a t -> p (a t)"),
                                             AF.Exp))
                        exp_done[sb_] = ss.n
                        ex_done[sc] = ss.n

                    pb = 4 + (w % 2) * 2

                    def emit_pv(sc):
                        t_.wait_ge(ss.h, ex_done[sc])
                        if sc == 0:
                            t_.wait_ge(sv.h, ep_state[w % 2])
                            t_.wait_ge(ss.h, ep_ss[w % 2])
                        for half in range(2):
                            ins = t_.matmul(PS[0:65, pb + half, :],
                                            ckv_tok[:, sc, 0:65],
                                            e_sb[sc % 2][:, half * 512:(half + 1) * 512],
                                            start=(sc == 0), stop=(sc == NC_ - 1))
                        stt.inc(ins)

                    qk_done, ex_done = {}, {}
                    emit_qk(0)
                    emit_exp(0)
                    for sc in range(1, NC_):
                        emit_qk(sc)
                        emit_exp(sc)
                        emit_pv(sc - 1)
                    emit_pv(NC_ - 1)
                    s_.wait_ge(stt.h, stt.n)
                    ss.inc(s_.copy(den_row[0:1, :],
                                   PS[64:65, pb:pb + 2, :].rearrange("p a t -> p (a t)")))
                    v_.wait_ge(ss.h, ss.n)
                    sv.inc(v_.reciprocal(den_row[0:1, :], den_row[0:1, :]))
                    s_.wait_ge(sv.h, sv.n)
                    ss.inc(s_.copy(u_sb[:, :],
                                   PS[0:65, pb:pb + 2, :].rearrange("p a t -> p (a t)")))
                    t_.wait_ge(ss.h, ss.n)
                    stt.inc(t_.matmul(PS[0:65, pb, :], ones_f1[0:1, 0:65],
                                      den_row[0:1, 0:512], start=True, stop=True))
                    stt.inc(t_.matmul(PS[0:65, pb + 1, :], ones_f1[0:1, 0:65],
                                      den_row[0:1, 512:1024], start=True, stop=True))
                    v_.wait_ge(stt.h, stt.n)
                    sv.inc(v_.tensor_tensor(z_bf[:, :], u_sb[:, :],
                                            PS[0:65, pb:pb + 2, :].rearrange("p a t -> p (a t)"),
                                            op=ALU.mult))
                    ep_state[w % 2] = sv.n
                    t_.wait_ge(sv.h, sv.n)
                    for half in range(2):
                        stt.inc(t_.matmul(PS[0:64, pb + half, :],
                                          wvuT[:, h * 64:(h + 1) * 64],
                                          z_bf[:, half * 512:(half + 1) * 512],
                                          start=True, stop=True))
                    s_.wait_ge(stt.h, stt.n)
                    p0 = (h % 2) * 64
                    ss.inc(s_.copy(y_pair[p0:p0 + 64, wl],
                                   PS[0:64, pb:pb + 2, :].rearrange("p a t -> p (a t)")))
                    ep_ss[w % 2] = ss.n
                qread[h % 2] = stt.n
                if h % 2 == 1:
                    proj_pair("L0_b_proj", h // 2, h == 1, y_pair)
            barrier()
            ph3.close()


    _qwp_id = [0]

    def quant_weight_prep(w_ap, rows_n, cols_n, scr_dst, isw_col):
        nr = rows_n // 128
        _qwp_id[0] += 1
        u = str(_qwp_id[0])
        with contextlib.ExitStack() as ph:
            wch = ph.enter_context(sbt("wch" + u, [128, cols_n], F32))
            qch = ph.enter_context(sbt("qch" + u, [128, cols_n], I32))
            parts = ph.enter_context(sbt("parts" + u, [128, nr + 1], F32))
            sca = ph.enter_context(sbt("sca" + u, [1, 4], F32))
            swb = ph.enter_context(sbt("swb" + u, [128, 2], F32))
            tstg = ph.enter_context(sbt("tstg" + u, [128, cols_n // 128, 128], BF16))
            for rr in range(nr):
                sy_.wait_ge(sv.h, sv.n)
                sd.inc(sy_.dma_start(wch[:, :], w_ap[rr * 128:(rr + 1) * 128, :]), 16)
                v_.wait_ge(sd.h, sd.n)
                sv.inc(v_.reduce_sum(parts[:, rr:rr + 1], wch[:, :], axis=AX.X,
                                     apply_absolute_value=True))
            sv.inc(v_.drain())
            sv.inc(v_.reduce_sum(parts[:, nr:nr + 1], parts[:, 0:nr], axis=AX.X))
            sv.inc(v_.drain())
            sv.inc(v_.tensor_copy(wch[:, 0:1], parts[:, nr:nr + 1]))
            sv.inc(v_.drain())
            t_.wait_ge(sv.h, sv.n)
            stt.inc(t_.matmul(PS[0:1, 6, 0:1], ones_c128[:, :], wch[:, 0:1],
                              start=True, stop=True))
            v_.wait_ge(stt.h, stt.n)
            sv.inc(v_.tensor_scalar(sca[0:1, 0:1], PS[0:1, 6, 0:1],
                                    1.0 / (rows_n * cols_n), 1e-5,
                                    op0=ALU.mult, op1=ALU.max))
            sv.inc(v_.drain())
            sv.inc(v_.reciprocal(sca[0:1, 1:2], sca[0:1, 0:1]))
            sv.inc(v_.drain())
            t_.wait_ge(sv.h, sv.n)
            stt.inc(t_.matmul(PS[0:128, 6, 0:1], ones_f1[0:1, :], sca[0:1, 1:2],
                              start=True, stop=True))
            stt.inc(t_.matmul(PS[0:128, 6, 1:2], ones_f1[0:1, :], sca[0:1, 0:1],
                              start=True, stop=True))
            v_.wait_ge(stt.h, stt.n)
            sv.inc(v_.tensor_copy(swb[:, 0:2], PS[0:128, 6, 0:2]))
            sv.inc(v_.drain())
            sv.inc(v_.tensor_copy(bias_pool[:, isw_col:isw_col + 1], swb[:, 1:2]))
            sv.inc(v_.drain())
            for rr in range(nr):
                sy_.wait_ge(sv.h, sv.n)
                sy_.wait_ge(stt.h, stt.n)
                sd.inc(sy_.dma_start(wch[:, :], w_ap[rr * 128:(rr + 1) * 128, :]), 16)
                v_.wait_ge(sd.h, sd.n)
                v_.wait_ge(stt.h, stt.n)
                sv.inc(v_.tensor_scalar(qch[:, :], wch[:, :], swb[:, 0:1], None,
                                        op0=ALU.mult))
                sv.inc(v_.drain())
                sv.inc(v_.tensor_scalar(wch[:, :], qch[:, :], 1.0, -1.0,
                                        op0=ALU.min, op1=ALU.max))
                sv.inc(v_.drain())
                t_.wait_ge(sv.h, sv.n)
                t_.wait_ge(ss.h, ss.n)
                for jj in range(cols_n // 128):
                    b = jj % 2
                    off = jj // 2 % 2
                    tgt = PS[:, b, :].rearrange("p (o q) -> p o q", o=4)[:, off, :]
                    stt.inc(t_.transpose(tgt, wch[:, jj * 128:(jj + 1) * 128], ident_f[:, :]))
                    if jj % 4 == 3 or jj == cols_n // 128 - 1:
                        s_.wait_ge(stt.h, stt.n)
                        s_.wait_ge(sa.h, sa.n)
                        for jj2 in range(jj - jj % 4, jj + 1):
                            b2 = jj2 % 2
                            off2 = jj2 // 2 % 2
                            srcp = PS[:, b2, :].rearrange("p (o q) -> p o q", o=4)[:, off2, :]
                            ins = s_.activation(tstg[:, jj2, :], srcp, AF.Identity)
                        ss.inc(ins)
                        t_.wait_ge(ss.h, ss.n)
                sy_ps = nc.scalar
                sy_ps.wait_ge(ss.h, ss.n)
                dst = scr_dst.rearrange("(j p) f -> p j f", p=128)[:, :, rr * 128:(rr + 1) * 128]
                sa.inc(sy_ps.dma_start(dst, tstg[:, :, :]), 16)
            barrier()

    def f0_ffn():
        NG = T // 512
        scr_xres = nc.dram_tensor("scr_xres", [C, T], F32, kind="Internal").ap()
        with nc.named_scope("fqw"):
            quant_weight_prep(din["F0_w1"], FF, C, scr_qw1, 250)
            quant_weight_prep(din["F0_w2"], C, FF, scr_qw2, 251)
        barrier()
        sd.inc(sy_.dma_start(scr_xres.rearrange("(k p) t -> p k t", p=128), XT[:, :, :]), 16)
        _f0st = contextlib.ExitStack()
        xb = XT.bitcast(BF16).reshape([128, 8 * T * 2])
        nqw, nqx, nfr = 8 * FF, 32 * 512, 3 * FF
        if 8 * T * 2 >= nqw + nqx + nfr:
            qw1v = xb[:, 0:nqw].rearrange("p (a f) -> p a f", a=8)
            qx2T = xb[:, nqw:nqw + nqx].rearrange("p (a f) -> p a f", a=32)
            frow = xb[0:1, nqw + nqx:nqw + nqx + nfr].rearrange("p (a f) -> p a f", a=3)
        elif 8 * T * 2 >= nqw:
            qw1v = xb[:, 0:nqw].rearrange("p (a f) -> p a f", a=8)
            qx2T = _f0st.enter_context(sbt("qx2T_r", [128, 32, 512], BF16))[:, :, :]
            frow = _f0st.enter_context(sbt("frow_r", [1, 3, FF], BF16))[:, :, :]
        else:
            qw1v = _f0st.enter_context(sbt("qw1v_r", [128, 8, FF], BF16))[:, :, :]
            qx2T = _f0st.enter_context(sbt("qx2T_r", [128, 32, 512], BF16))[:, :, :]
            frow = _f0st.enter_context(sbt("frow_r", [1, 3, FF], BF16))[:, :, :]
        sy_.wait_ge(sd.h, sd.n)
        sd.inc(sy_.dma_start(qw1v[:, :, :],
                             scr_qw1.rearrange("(k p) f -> p k f", p=128)), 16)
        g_.wait_ge(sd.h, sd.n - 16)
        sg.inc(g_.dma_start(frow[0:1, 0, :], din["F0_b1"].rearrange("(o n) -> o n", o=1)), 16)
        sg.inc(g_.dma_start(frow[0:1, 1, :], din["F0_ln2_g"].rearrange("(o n) -> o n", o=1)), 16)
        sg.inc(g_.dma_start(frow[0:1, 2, :], din["F0_ln2_b"].rearrange("(o n) -> o n", o=1)), 16)
        barrier()
        with contextlib.ExitStack() as ph:
            qxT = ph.enter_context(sbt("qxT", [128, 8, 512], BF16))
            xls = ph.enter_context(sbt("xls_f0", [128, 8, 512], BF16))
            gb = ph.enter_context(sbt("gb", [128, FF], F32))
            t0b = ph.enter_context(sbt("t0b", [128, 512], F32))
            t0c = ph.enter_context(sbt("t0c", [128, 512], F32))
            t0x = [t0b, t0c]
            y2b = ph.enter_context(sbt("y2b", [128, 8, 512], F16))
            qib = ph.enter_context(sbt("qib", [128, 512], I32))
            qib2 = ph.enter_context(sbt("qib2", [128, 512], I32))
            cols = ph.enter_context(sbt("cols", [128, 24], F32))
            pool1 = ph.enter_context(sbt("pool1", [128, 32], F32))
            cs2 = [ph.enter_context(sbt(f"cslab{i}", [128, 8, 128], BF16))
                   for i in range(2)]
            xg = ph.enter_context(sbt("xg", [128, 4, 512], F32))
            _scC = [None]
            pstate = {"evd": {}, "bad": {}, "qtt": {}}
            csd = {0: 0, 1: 0}
            csr = {0: 0, 1: 0}

            def cs_issue(qi):
                cc2, hq2 = qi // 4, qi % 4
                q_, qs = (g_, sg) if qi % 2 == 0 else (nc.scalar, sa)
                q_.wait_ge(stt.h, csr[qi % 2])
                q_.wait_ge(qs.h, qs.n)
                qs.inc(q_.dma_start(
                    cs2[qi % 2][:, :, :],
                    scr_qw2.rearrange("(j p) c -> p j c", p=128)
                    [:, hq2 * 8:hq2 * 8 + 8, cc2 * 128:(cc2 + 1) * 128]), 16)
                csd[qi % 2] = qs.n
            for g0 in range(NG):
                gl = slice(g0 * 512, (g0 + 1) * 512)
                sy_.wait_ge(stt.h, stt.n)
                sy_.wait_ge(sv.h, sv.n)
                sy_.wait_ge(sd.h, sd.n)
                srcx = scr_xln.rearrange("(k p) t -> p k t", p=128)[:, :, gl]
                sd.inc(sy_.dma_start(xls[:, :, :], srcx), 16)
                if _scC[0] is not None:
                    _scC[0].__exit__(None, None, None)
                    _scC[0] = None
                _scA = nc.named_scope("fA"); _scA.__enter__()
                v_.wait_ge(sd.h, sd.n)
                for cc in range(8):
                    v_.wait_ge(stt.h, stt.n)
                    sv.inc(v_.tensor_tensor(t0b[:, :], xls[:, cc, :], xls[:, cc, :],
                                            op=ALU.mult))
                    t_.wait_ge(sv.h, sv.n)
                    stt.inc(t_.matmul(PS[0:1, 6, :], ones_c128b[:, :], xls[:, cc, :],
                                      start=(cc == 0), stop=(cc == 7)))
                    stt.inc(t_.matmul(PS[0:1, 7, :], ones_c128[:, :], t0b[:, :],
                                      start=(cc == 0), stop=(cc == 7)))
                v_.wait_ge(stt.h, stt.n)
                v_.tensor_scalar(rows[:, 0, :], PS[0:1, 6, :], 1.0 / C, None, op0=ALU.mult)
                sv.inc(v_.tensor_scalar(rows[:, 1, :], PS[0:1, 7, :], 1.0 / C, None,
                                        op0=ALU.mult))
                sv.inc(v_.drain())
                sv.inc(v_.tensor_tensor(rows[:, 2, :], rows[:, 0, :], rows[:, 0, :],
                                        op=ALU.mult))
                sv.inc(v_.drain())
                sv.inc(v_.tensor_tensor(rows[:, 1, :], rows[:, 1, :], rows[:, 2, :],
                                        op=ALU.subtract))
                sv.inc(v_.drain())
                sv.inc(v_.tensor_scalar(rows[:, 1, :], rows[:, 1, :], LN_EPS, None,
                                        op0=ALU.add))
                sv.inc(v_.drain())
                s_.wait_ge(sv.h, sv.n)
                ss.inc(s_.activation(rows[:, 2, :], rows[:, 1, :], AF.Ln))
                ss.inc(s_.drain())
                ss.inc(s_.activation(rows[:, 1, :], rows[:, 2, :], AF.Exp, scale=-0.5))
                ss.inc(s_.drain())
                v_.wait_ge(ss.h, ss.n)
                sv.inc(v_.tensor_tensor(rows[:, 2, :], rows[:, 0, :], rows[:, 1, :],
                                        op=ALU.mult))
                sv.inc(v_.drain())
                t_.wait_ge(sv.h, sv.n)
                stt.inc(t_.matmul(PS[:, 2, :], ones_f1[:, :], rows[:, 1, :],
                                  start=True, stop=True))
                stt.inc(t_.matmul(PS[:, 3, :], ones_f1[:, :], rows[:, 2, :],
                                  start=True, stop=True))
                v_.wait_ge(stt.h, stt.n)
                for cc in range(8):
                    sv.inc(v_.tensor_tensor(y2b[:, cc, :], xls[:, cc, :], PS[:, 2, :],
                                            op=ALU.mult))
                sv.inc(v_.drain())
                for cc in range(8):
                    sv.inc(v_.tensor_tensor(y2b[:, cc, :], y2b[:, cc, :], PS[:, 3, :],
                                            op=ALU.subtract))
                sv.inc(v_.drain())
                s_.wait_ge(sv.h, sv.n)
                for cc in range(8):
                    ins = s_.activation(y2b[:, cc, :], y2b[:, cc, :], AF.Identity,
                                        scale=bias_slice("F0_ln1_g", cc),
                                        bias=bias_slice("F0_ln1_b", cc))
                    ss.inc(ins)
                t_.wait_ge(ss.h, ss.n)
                for tc in range(4):
                    for cc in range(8):
                        b = 4 + cc % 2
                        off = cc // 2 % 4
                        tgt = PS16[:, b, :].rearrange("p (o q) -> p o q", o=8)[:, off, :]
                        stt.inc(t_.transpose(tgt, y2b[:, cc, tc * 128:(tc + 1) * 128],
                                             ident_h[:, :]))
                        if cc % 2 == 1:
                            v_.wait_ge(stt.h, stt.n)
                            for c2 in (cc - 1, cc):
                                b2 = 4 + c2 % 2
                                off2 = c2 // 2 % 4
                                sp = PS16[:, b2, :].rearrange("p (o q) -> p o q", o=8)[:, off2, :]
                                ins = v_.reduce_max(pool1[:, tc * 8 + c2:tc * 8 + c2 + 1],
                                                    sp, axis=AX.X, apply_absolute_value=True)
                            sv.inc(ins)
                            t_.wait_ge(sv.h, sv.n)
                sv.inc(v_.drain())
                for tc in range(4):
                    sv.inc(v_.reduce_max(cols[:, tc:tc + 1], pool1[:, tc * 8:(tc + 1) * 8],
                                         axis=AX.X))
                sv.inc(v_.drain())
                sv.inc(v_.tensor_scalar(cols[:, 0:4], cols[:, 0:4], 1e-5, None, op0=ALU.max))
                sv.inc(v_.drain())
                sv.inc(v_.reciprocal(cols[:, 4:8], cols[:, 0:4]))
                sv.inc(v_.drain())
                sv.inc(v_.tensor_scalar(cols[:, 4:8], cols[:, 4:8], 127.0, None, op0=ALU.mult))
                sv.inc(v_.tensor_scalar(cols[:, 8:12], cols[:, 0:4], 1.0 / 127.0, None,
                                        op0=ALU.mult))
                sv.inc(v_.drain())
                sv.inc(v_.tensor_scalar(cols[:, 8:12], cols[:, 8:12],
                                        bias_pool[:, 250:251], None, op0=ALU.mult))
                sv.inc(v_.drain())
                t_.wait_ge(sv.h, sv.n)
                for tc in range(4):
                    stt.inc(t_.transpose(PS[0:1, 6, tc * 128:(tc + 1) * 128],
                                         cols[:, 4 + tc:5 + tc], ident_f[:, :]))
                s_.wait_ge(stt.h, stt.n)
                ins = s_.copy(rows[0:1, 3, :], PS[0:1, 6, :])
                ss.inc(ins)
                t_.wait_ge(ss.h, ss.n)
                stt.inc(t_.matmul(PS[:, 7, :], ones_f1[:, :], rows[:, 3, :],
                                  start=True, stop=True))
                v_.wait_ge(stt.h, stt.n)
                for cc in range(8):
                    sv.inc(v_.tensor_tensor(y2b[:, cc, :], y2b[:, cc, :], PS[:, 7, :],
                                            op=ALU.mult))
                sv.inc(v_.drain())
                qrd = {0: 0, 1: 0}
                for cc in range(8):
                    qb = qib if cc % 2 == 0 else qib2
                    s_.wait_ge(sv.h, sv.n if cc < 2 else qrd[cc % 2])
                    ss.inc(s_.copy(qb[:, :], y2b[:, cc, :]))
                    v_.wait_ge(ss.h, ss.n)
                    sv.inc(v_.tensor_copy(qxT[:, cc, :], qb[:, :]))
                    qrd[cc % 2] = sv.n
                _scA.__exit__(None, None, None)
                _scB = nc.named_scope("fB"); _scB.__enter__()
                sy_.wait_ge(sd.h, sd.n)
                for tc in range(4):
                    tcl = slice(tc * 128, (tc + 1) * 128)
                    pevd, pbad, pqtt = pstate["evd"], pstate["bad"], pstate["qtt"]
                    mmd, evd, bad, gld, qtt = {}, {}, {}, {-1: 0, -2: 0}, {}

                    def b1_mm(f5):
                        b = f5 % 2
                        fl = slice(f5 * 512, (f5 + 1) * 512)
                        if f5 < 2:
                            if tc == 0:
                                t_.wait_ge(ss.h, ss.n)
                                t_.wait_ge(sv.h, sv.n)
                            else:
                                t_.wait_ge(ss.h, pevd.get(6 + f5, 0))
                                t_.wait_ge(sv.h, pbad.get(6 + f5, 0))
                        else:
                            t_.wait_ge(sv.h, bad.get(f5 - 2, 0))
                            t_.wait_ge(ss.h, evd.get(f5 - 2, 0))
                        stt.inc(t_.matmul(PS[:, 4 + b, :], ones_b1[:, :], frow[0:1, 0, fl],
                                          start=True, stop=True))
                        for ci in range(8):
                            ins = t_.matmul(bank(b), qxT[:, ci, tcl], qw1v[:, ci, fl],
                                            start=(ci == 0), stop=(ci == 7))
                        stt.inc(ins)
                        mmd[f5] = stt.n

                    def b1_evac(f5):
                        b = f5 % 2
                        s_.wait_ge(stt.h, mmd[f5])
                        s_.wait_ge(sv.h, bad.get(f5 - 1, 0))
                        ss.inc(s_.activation(t0x[b][:, :], bank(b), AF.Identity,
                                             scale=cols[:, 8 + tc:9 + tc]))
                        evd[f5] = ss.n

                    def b1_badd(f5):
                        b = f5 % 2
                        v_.wait_ge(ss.h, evd[f5])
                        sv.inc(v_.tensor_tensor(t0x[b][:, :], t0x[b][:, :], PS[:, 4 + b, :],
                                                op=ALU.add))
                        bad[f5] = sv.n

                    def b1_gelu(f5):
                        b = f5 % 2
                        fl = slice(f5 * 512, (f5 + 1) * 512)
                        s_.wait_ge(sv.h, bad[f5])
                        s_.wait_ge(sv.h, pqtt.get(f5, 0))
                        ss.inc(s_.activation(gb[:, fl], t0x[b][:, :], AF.Gelu))
                        gld[f5] = ss.n

                    b1_mm(0); b1_evac(0)
                    for f5 in range(1, 8):
                        b1_mm(f5); b1_badd(f5 - 1); b1_gelu(f5 - 1); b1_evac(f5)
                    b1_badd(7); b1_gelu(7)
                    v_.wait_ge(ss.h, ss.n)
                    sv.inc(v_.reduce_sum(cols[:, 16:17], gb[:, :], axis=AX.X))
                    red = {0: sv.n, 1: sv.n}
                    sqd = {0: 0, 1: 0}
                    for f5 in range(8):
                        fl = slice(f5 * 512, (f5 + 1) * 512)
                        s_.wait_ge(sv.h, red[f5 % 2])
                        ss.inc(s_.activation(t0x[f5 % 2][:, :], gb[:, fl], AF.Square))
                        sqd[f5 % 2] = ss.n
                        v_.wait_ge(ss.h, sqd[f5 % 2])
                        sv.inc(v_.reduce_sum(pool1[:, f5:f5 + 1], t0x[f5 % 2][:, :],
                                             axis=AX.X))
                        red[f5 % 2] = sv.n
                    sv.inc(v_.drain())
                    sv.inc(v_.reduce_sum(cols[:, 17:18], pool1[:, 0:8], axis=AX.X))
                    s_.wait_ge(sv.h, sv.n)
                    ss.inc(s_.activation(cols[:, 19:20], cols[:, 17:18], AF.Identity,
                                         scale=1.0 / FF, bias=cc_eps[:, 0:1]))
                    sv.inc(v_.tensor_scalar(cols[:, 18:19], cols[:, 16:17], 1.0 / FF, None,
                                            op0=ALU.mult))
                    s_.wait_ge(sv.h, sv.n)
                    ss.inc(s_.activation(cols[:, 20:21], cols[:, 18:19], AF.Square))
                    v_.wait_ge(ss.h, ss.n)
                    sv.inc(v_.tensor_tensor(cols[:, 19:20], cols[:, 19:20], cols[:, 20:21],
                                            op=ALU.subtract))
                    s_.wait_ge(sv.h, sv.n)
                    ss.inc(s_.activation(cols[:, 20:21], cols[:, 19:20], AF.Ln))
                    ss.inc(s_.drain())
                    ss.inc(s_.activation(cols[:, 20:21], cols[:, 20:21], AF.Exp, scale=-0.5))
                    v_.wait_ge(ss.h, ss.n)
                    sv.inc(v_.scalar_tensor_tensor(cols[:, 21:22], cols[:, 18:19], -1.0,
                                                   cols[:, 20:21], op0=ALU.mult,
                                                   op1=ALU.mult))
                    s_.wait_ge(sv.h, sv.n)
                    ss.inc(s_.activation(gb[:, :], gb[:, :], AF.Identity,
                                         scale=cols[:, 20:21], bias=cols[:, 21:22]))
                    apd = {0: 0, 1: 0}
                    for f5 in range(8):
                        fl = slice(f5 * 512, (f5 + 1) * 512)
                        pA = 2 if f5 % 2 == 0 else 6
                        t_.wait_ge(sv.h, apd[f5 % 2])
                        if f5 == 0:
                            t_.wait_ge(ss.h, ss.n)
                        stt.inc(t_.matmul(PS[:, pA, :], ones_b1[:, :], frow[0:1, 1, fl],
                                          start=True, stop=True))
                        stt.inc(t_.matmul(PS[:, pA + 1, :], ones_b1[:, :], frow[0:1, 2, fl],
                                          start=True, stop=True))
                        v_.wait_ge(stt.h, stt.n)
                        sv.inc(v_.tensor_tensor(gb[:, fl], gb[:, fl], PS[:, pA, :],
                                                op=ALU.mult))
                        sv.inc(v_.drain())
                        sv.inc(v_.tensor_tensor(gb[:, fl], gb[:, fl], PS[:, pA + 1, :],
                                                op=ALU.add))
                        apd[f5 % 2] = sv.n
                    sv.inc(v_.drain())
                    sv.inc(v_.reduce_max(cols[:, 13:14], gb[:, :], axis=AX.X,
                                         apply_absolute_value=True))
                    s_.wait_ge(sv.h, sv.n)
                    ss.inc(s_.activation(cols[:, 22:23], cols[:, 13:14], AF.Identity,
                                         scale=1.0, bias=0.0))
                    v_.wait_ge(ss.h, ss.n)
                    sv.inc(v_.tensor_scalar(cols[:, 13:14], cols[:, 22:23], 1e-5, None,
                                            op0=ALU.max))
                    sv.inc(v_.drain())
                    sv.inc(v_.reciprocal(cols[:, 14:15], cols[:, 13:14]))
                    s_.wait_ge(sv.h, sv.n)
                    ss.inc(s_.activation(cols[:, 14:15], cols[:, 14:15], AF.Identity,
                                         scale=127.0))
                    sv.inc(v_.tensor_scalar(cols[:, 23:24], cols[:, 13:14], 1.0 / 127.0,
                                            None, op0=ALU.mult))
                    sv.inc(v_.drain())
                    sv.inc(v_.tensor_scalar(cols[:, 4 + tc:5 + tc], cols[:, 23:24],
                                            bias_pool[:, 251:252], None, op0=ALU.mult))
                    v_.wait_ge(ss.h, ss.n)
                    qrd2 = {0: 0, 1: 0}
                    trd = {0: 0, 1: 0}
                    for f5 in range(8):
                        fl = slice(f5 * 512, (f5 + 1) * 512)
                        qb = qib if f5 % 2 == 0 else qib2
                        v_.wait_ge(ss.h, qrd2[f5 % 2])
                        sv.inc(v_.tensor_scalar(qb[:, :], gb[:, fl], cols[:, 14:15],
                                                None, op0=ALU.mult))
                        qtt[f5] = sv.n
                        s_.wait_ge(sv.h, sv.n)
                        s_.wait_ge(stt.h, trd[f5 % 2])
                        ss.inc(s_.copy(t0x[f5 % 2][:, :], qb[:, :]))
                        qrd2[f5 % 2] = ss.n
                        t_.wait_ge(ss.h, ss.n)
                        if f5 == 0:
                            t_.wait_ge(sv.h, sv.n)
                        for q4 in range(4):
                            bq = 2 + q4 % 2
                            off = q4 // 2
                            tgt = PS[:, bq, :].rearrange("p (o q) -> p o q", o=4)[:, off, :]
                            stt.inc(t_.transpose(tgt, t0x[f5 % 2][:, q4 * 128:(q4 + 1) * 128],
                                                 ident_f[:, :]))
                        trd[f5 % 2] = stt.n
                        s_.wait_ge(stt.h, stt.n)
                        for q4 in range(4):
                            bq = 2 + q4 % 2
                            off = q4 // 2
                            sp = PS[:, bq, :].rearrange("p (o q) -> p o q", o=4)[:, off, :]
                            ins = s_.activation(qx2T[:, f5 * 4 + q4, tcl], sp, AF.Identity)
                        ss.inc(ins)
                    pstate["evd"], pstate["bad"], pstate["qtt"] = evd, bad, qtt
                _scB.__exit__(None, None, None)
                _scC[0] = nc.named_scope("fC"); _scC[0].__enter__()
                t_.wait_ge(ss.h, ss.n)
                t_.wait_ge(sv.h, sv.n)
                for tc in range(4):
                    stt.inc(t_.transpose(PS[0:1, 6, tc * 128:(tc + 1) * 128],
                                         cols[:, 4 + tc:5 + tc], ident_f[:, :]))
                s_.wait_ge(stt.h, stt.n)
                ins = s_.copy(rows[0:1, 3, :], PS[0:1, 6, :])
                ss.inc(ins)
                t_.wait_ge(ss.h, ss.n)
                stt.inc(t_.matmul(PS[:, 3, :], ones_f1[:, :], rows[:, 3, :],
                                  start=True, stop=True))
                for hf in range(2):
                  sy_.wait_ge(sv.h, sv.n)
                  sy_.wait_ge(sd.h, sd.n)
                  sd.inc(sy_.dma_start(
                      xg[:, :, :],
                      scr_xres.rearrange("(k p) t -> p k t", p=128)[:, hf * 4:hf * 4 + 4, gl]),
                      16)
                  v_.wait_ge(sd.h, sd.n)
                  for cc in range(hf * 4, hf * 4 + 4):
                    b = cc % 2
                    for hq in range(4):
                        qi = cc * 4 + hq
                        if qi == 0:
                            cs_issue(0)
                        if qi + 1 < 32:
                            cs_issue(qi + 1)
                        t_.wait_ge((sg if qi % 2 == 0 else sa).h, csd[qi % 2])
                        if hq == 0:
                            t_.wait_ge(ss.h, ss.n)
                        for fc in range(8):
                            ins = t_.matmul(bank(b), cs2[qi % 2][:, fc, :],
                                            qx2T[:, hq * 8 + fc, :],
                                            start=(hq == 0 and fc == 0),
                                            stop=(hq == 3 and fc == 7))
                        stt.inc(ins)
                        csr[qi % 2] = stt.n
                    s_.wait_ge(stt.h, stt.n)
                    s_.wait_ge(sv.h, sv.n)
                    ss.inc(s_.activation(t0b[:, :], bank(b), AF.Identity))
                    v_.wait_ge(ss.h, ss.n)
                    sv.inc(v_.tensor_tensor(t0b[:, :], t0b[:, :], PS[:, 3, :], op=ALU.mult))
                    sv.inc(v_.drain())
                    sv.inc(v_.scalar_tensor_tensor(xg[:, cc - hf * 4, :], t0b[:, :],
                                                   bias_slice("F0_b2", cc), xg[:, cc - hf * 4, :],
                                                   op0=ALU.add, op1=ALU.add))
                    sv.inc(v_.drain())
                  sy_.wait_ge(sv.h, sv.n)
                  sd.inc(sy_.dma_start(
                      scr_xres.rearrange("(k p) t -> p k t", p=128)[:, hf * 4:hf * 4 + 4, gl],
                      xg[:, :, :]), 16)
            if _scC[0] is not None:
                _scC[0].__exit__(None, None, None)
                _scC[0] = None
            barrier()
            sd.inc(sy_.dma_start(XT[:, :, :],
                                 scr_xres.rearrange("(k p) t -> p k t", p=128)), 16)
            barrier()
        _f0st.close()

    def f1_ffn():
      with contextlib.ExitStack() as phx:
        b1g = phx.enter_context(sbt("b1g", [128, 24], F32))
        with contextlib.ExitStack() as ph:
            wp1 = ph.enter_context(sbt("wp1", [128, 8, NAV], F32))
            wp2 = ph.enter_context(sbt("wp2", [128, NE], F32))
            growA = ph.enter_context(sbt("growA", [1, NAV], F32))
            growB = ph.enter_context(sbt("growB", [1, NAV], F32))
            gcol = ph.enter_context(sbt("gcol", [128, 1], F32))
            lgrow = ph.enter_context(sbt("lgrow", [1, 16], F32))
            ohb = ph.enter_context(sbt("ohb", [128, NE], F32))
            de_f32 = ph.enter_context(sbt("de_f32", [128, NE, 128], F32))
            bsel = ph.enter_context(sbt("bsel", [8, FE + C], F32))
            webf = [ph.enter_context(sbt(f"webf{i}", [128, FE], F32))
                    for i in range(2)]
            wst = ph.enter_context(sbt("wst", [128, 4, 512], BF16))
            rowtmp = ph.enter_context(sbt("rowtmp", [1, 2, 512], F32))
            sc_.inc(sy_.dma_start(wp1[:, :, :],
                                  din["F1_Wp1"].rearrange("(k p) n -> p k n", p=128)), 16)
            sc_.inc(sy_.dma_start(wp2[0:NAV, :], din["F1_Wp2"]), 16)
            sc_.inc(sy_.dma_start(growB[0:1, :],
                                  din["F1_bp1"].rearrange("(o n) -> o n", o=1)), 16)
            sc_.inc(sy_.dma_start(lgrow[0:1, 0:NE],
                                  din["F1_bp2"].rearrange("(o n) -> o n", o=1)), 16)
            sc_.inc(sy_.dma_start(bsel[:, 0:FE], din["F1_b1"]), 16)
            sc_.inc(sy_.dma_start(bsel[:, FE:FE + C], din["F1_b2"]), 16)
            barrier()
            for ci in range(8):
                stt.inc(t_.matmul(PS[0:1, 0, 0:NAV], XT[:, ci, T - 1:T], wp1[:, ci, :],
                                  start=(ci == 0), stop=(ci == 7)))
                stt.inc(t_.matmul(PS[0:1, 1, 0:NAV], ones_c128[:, :], wp1[:, ci, :],
                                  start=(ci == 0), stop=(ci == 7)))
            v_.wait_ge(stt.h, stt.n)
            sv.inc(v_.tensor_tensor(consts[0:1, 5:6], consts[0:1, 2:3], consts[0:1, 3:4],
                                    op=ALU.mult))
            sv.inc(v_.drain())
            sv.inc(v_.tensor_scalar(growA[0:1, :], PS[0:1, 0, 0:NAV], consts[0:1, 3:4],
                                    None, op0=ALU.mult))
            sv.inc(v_.drain())
            sv.inc(v_.scalar_tensor_tensor(growA[0:1, :], PS[0:1, 1, 0:NAV],
                                           consts[0:1, 5:6], growA[0:1, :],
                                           op0=ALU.mult, op1=ALU.subtract))
            sv.inc(v_.drain())
            sv.inc(v_.tensor_tensor(growA[0:1, :], growB[0:1, :], growA[0:1, :],
                                    op=ALU.subtract))
            sv.inc(v_.drain())
            s_.wait_ge(sv.h, sv.n)
            ss.inc(s_.activation(growA[0:1, :], growA[0:1, :], AF.Gelu))
            ss.inc(s_.drain())
            t_.wait_ge(ss.h, ss.n)
            stt.inc(t_.transpose(PS[0:NAV, 2, 0:1], growA[0:1, :], ident_f[0:1, 0:1]))
            s_.wait_ge(stt.h, stt.n)
            ss.inc(s_.copy(gcol[0:NAV, :], PS[0:NAV, 2, 0:1]))
            t_.wait_ge(ss.h, ss.n)
            stt.inc(t_.matmul(PS[0:1, 3, 0:NE], gcol[0:NAV, :], wp2[0:NAV, :],
                              start=True, stop=True))
            v_.wait_ge(stt.h, stt.n)
            sv.inc(v_.tensor_tensor(lgrow[0:1, 8:8 + NE], PS[0:1, 3, 0:NE],
                                    lgrow[0:1, 0:NE], op=ALU.add))
            sv.inc(v_.drain())
            sv.inc(v_.reduce_max(consts[0:1, 6:7], lgrow[0:1, 8:8 + NE], axis=AX.X))
            sv.inc(v_.drain())
            sv.inc(v_.tensor_scalar(lgrow[0:1, 0:NE], lgrow[0:1, 8:8 + NE],
                                    consts[0:1, 6:7], None, op0=ALU.is_ge))
            sv.inc(v_.drain())
            t_.wait_ge(sv.h, sv.n)
            stt.inc(t_.matmul(PS[:, 4, 0:NE], ones_f1[:, :], lgrow[0:1, 0:NE],
                              start=True, stop=True))
            v_.wait_ge(stt.h, stt.n)
            sv.inc(v_.tensor_copy(ohb[:, :], PS[:, 4, 0:NE]))
            sv.inc(v_.drain())
            for e in range(NE):
                sv.inc(v_.tensor_scalar(de_f32[:, e, :], ident_f[:, :], ohb[:, e:e + 1],
                                        None, op0=ALU.mult))
            sv.inc(v_.drain())
            stt.inc(t_.transpose(PS[0:NE, 5, 0:1], lgrow[0:1, 0:NE], ident_f[0:1, 0:1]))
            s_.wait_ge(stt.h, stt.n)
            ss.inc(s_.copy(gcol[0:NE, :], PS[0:NE, 5, 0:1]))
            t_.wait_ge(ss.h, ss.n)
            t_.wait_ge(sv.h, sv.n)
            for j in range(4):
                stt.inc(t_.matmul(PS[0:1, 6, :], gcol[0:NE, :],
                                  bsel[:, j * 512:(j + 1) * 512], start=True, stop=True))
                stt.inc(t_.matmul(PS[0:1, 7, :], gcol[0:NE, :],
                                  bsel[:, FE + (j % 2) * 512:FE + (j % 2 + 1) * 512]
                                  if j < 2 else bsel[:, 0:512],
                                  start=True, stop=True))
                s_.wait_ge(stt.h, stt.n)
                for q in range(4):
                    ins = s_.copy(rowtmp[0:1, 0, q * 128:(q + 1) * 128],
                                  PS[0:1, 6, q * 128:(q + 1) * 128])
                ss.inc(ins)
                t_.wait_ge(ss.h, ss.n)
                for q in range(4):
                    stt.inc(t_.transpose(PS[0:128, 4, q:q + 1],
                                         rowtmp[0:1, 0, q * 128:(q + 1) * 128],
                                         ident_f[0:1, 0:1]))
                s_.wait_ge(stt.h, stt.n)
                for q in range(4):
                    ins = s_.copy(b1g[:, j * 4 + q:j * 4 + q + 1], PS[0:128, 4, q:q + 1])
                ss.inc(ins)
                if j < 2:
                    for q in range(4):
                        ins = s_.copy(rowtmp[0:1, 1, q * 128:(q + 1) * 128],
                                      PS[0:1, 7, q * 128:(q + 1) * 128])
                    ss.inc(ins)
                    t_.wait_ge(ss.h, ss.n)
                    for q in range(4):
                        stt.inc(t_.transpose(PS[0:128, 5, q:q + 1],
                                             rowtmp[0:1, 1, q * 128:(q + 1) * 128],
                                             ident_f[0:1, 0:1]))
                    s_.wait_ge(stt.h, stt.n)
                    for q in range(4):
                        ins = s_.copy(b1g[:, 16 + j * 4 + q:16 + j * 4 + q + 1],
                                      PS[0:128, 5, q:q + 1])
                    ss.inc(ins)
                t_.wait_ge(ss.h, ss.n)
            barrier()
            sg.inc(g_.dma_start(scr_wo, din["F1_Wo"]), 16)
            wseq = ([("w1", i, e) for i in range(8) for e in range(NE)] +
                    [("w2", i, e) for i in range(16) for e in range(NE)])
            wdma = {0: 0, 1: 0}
            wread = {0: 0, 1: 0}

            def w_issue(idx):
                k, i, e = wseq[idx]
                src = (din["F1_W1"][e, i * 128:(i + 1) * 128, :] if k == "w1"
                       else din["F1_W2"][e, i * 128:(i + 1) * 128, :])
                ncols = FE if k == "w1" else C
                q_, qs = (sy_, sd) if idx % 2 == 0 else (nc.scalar, sa)
                q_.wait_ge(stt.h, wread[idx % 2])
                q_.wait_ge(qs.h, qs.n)
                qs.inc(q_.dma_start(webf[idx % 2][:, 0:ncols], src), 16)
                wdma[idx % 2] = qs.n

            w_issue(0)
            w_issue(1)
            for dcc in range(8):
                for e in range(NE):
                    idx = dcc * NE + e
                    t_.wait_ge((sd if idx % 2 == 0 else sa).h, wdma[idx % 2])
                    if e == 0:
                        t_.wait_ge(ss.h, ss.n)
                    for fc in range(4):
                        ins = t_.matmul(PS[:, fc, :], de_f32[:, e, :],
                                        webf[idx % 2][:, fc * 512:(fc + 1) * 512],
                                        start=(e == 0), stop=(e == NE - 1))
                    stt.inc(ins)
                    wread[idx % 2] = stt.n
                    if idx + 2 < len(wseq):
                        w_issue(idx + 2)
                s_.wait_ge(stt.h, stt.n)
                s_.wait_ge(sw_.h, sw_.n)
                for fc in range(4):
                    ins = s_.activation(wst[:, fc, :], PS[:, fc, :], AF.Identity)
                ss.inc(ins)
                sy_.wait_ge(ss.h, ss.n)
                sw_.inc(sy_.dma_start(scr_w1s[dcc * 128:(dcc + 1) * 128, :],
                                      wst.rearrange("p a t -> p (a t)")), 16)
                t_.wait_ge(ss.h, ss.n)
            for fcc in range(16):
                for e in range(NE):
                    idx = 64 + fcc * NE + e
                    t_.wait_ge((sd if idx % 2 == 0 else sa).h, wdma[idx % 2])
                    if e == 0:
                        t_.wait_ge(ss.h, ss.n)
                    for fc in range(2):
                        ins = t_.matmul(PS[:, fc, :], de_f32[:, e, :],
                                        webf[idx % 2][:, fc * 512:(fc + 1) * 512],
                                        start=(e == 0), stop=(e == NE - 1))
                    stt.inc(ins)
                    wread[idx % 2] = stt.n
                    if idx + 2 < len(wseq):
                        w_issue(idx + 2)
                s_.wait_ge(stt.h, stt.n)
                s_.wait_ge(sw_.h, sw_.n)
                for fc in range(2):
                    ins = s_.activation(wst[:, fc, :], PS[:, fc, :], AF.Identity)
                ss.inc(ins)
                sy_.wait_ge(ss.h, ss.n)
                sw_.inc(sy_.dma_start(scr_w2s[fcc * 128:(fcc + 1) * 128, :],
                                      wst[:, 0:2, :].rearrange("p a t -> p (a t)")), 16)
                t_.wait_ge(ss.h, ss.n)
            barrier()
        with contextlib.ExitStack() as ph:
            w1s = [ph.enter_context(sbt(f"w1s{i}", [128, 8, 512], BF16)) for i in range(2)]
            w2h = [ph.enter_context(sbt(f"w2h{i}", [128, 8, 128], BF16))
                   for i in range(2)]
            woh = [ph.enter_context(sbt(f"woh{i}", [128, 4, 128], BF16))
                   for i in range(2)]
            hT = ph.enter_context(sbt("hT", [128, 16, 512], BF16))
            rT_ = ph.enter_context(sbt("rT_", [128, 8, 512], BF16))
            xls = ph.enter_context(sbt("xls_f1", [128, 8, 512], BF16))
            t1b = ph.enter_context(sbt("t1b", [128, 512], F32))
            t2b = ph.enter_context(sbt("t2b", [128, 512], F32))
            barrier()
            w2d = {0: 0, 1: 0}
            w2r = {0: 0, 1: 0}
            wod = {0: 0, 1: 0}
            wor = {0: 0, 1: 0}

            def w2_issue(hh):
                hm = hh % 16
                cc2, hf2 = hm // 2, hm % 2
                q_, qs = (sy_, sd) if hh % 2 == 0 else (nc.scalar, sa)
                q_.wait_ge(stt.h, w2r[hh % 2])
                q_.wait_ge(qs.h, qs.n)
                qs.inc(q_.dma_start(
                    w2h[hh % 2][:, :, :],
                    scr_w2s.rearrange("(j p) c -> p j c", p=128)
                    [:, hf2 * 8:hf2 * 8 + 8, cc2 * 128:(cc2 + 1) * 128]), 16)
                w2d[hh % 2] = qs.n

            def wo_issue(hh):
                hm = hh % 16
                jc2, hf2 = hm // 2, hm % 2
                g_.wait_ge(stt.h, wor[hh % 2])
                g_.wait_ge(sg.h, sg.n)
                sg.inc(g_.dma_start(
                    woh[hh % 2][:, :, :],
                    scr_wo.rearrange("(k p) c -> p k c", p=128)
                    [:, hf2 * 4:hf2 * 4 + 4, jc2 * 128:(jc2 + 1) * 128]), 16)
                wod[hh % 2] = sg.n

            w2_issue(0)
            w2_issue(1)
            wo_issue(0)
            wo_issue(1)
            for tsub in range(NS):
                tl = slice(tsub * 512, (tsub + 1) * 512)
                sy_.wait_ge(stt.h, stt.n)
                sy_.wait_ge(sv.h, sv.n)
                sy_.wait_ge(sd.h, sd.n)
                srcx = scr_xln.rearrange("(k p) t -> p k t", p=128)[:, :, tl]
                sd.inc(sy_.dma_start(xls[:, :, :], srcx), 16)
                t_.wait_ge(sd.h, sd.n)
                evd = [0, 0]
                w1_ready = [0, 0]
                for fq in range(4):
                    wbuf = w1s[fq % 2]
                    sy_.wait_ge(stt.h, w1_ready[fq % 2])
                    sy_.wait_ge(sd.h, sd.n)
                    sd.inc(sy_.dma_start(
                        wbuf[:, :, :],
                        scr_w1s.rearrange("(k p) f -> p k f", p=128)[:, :, fq * 512:(fq + 1) * 512]),
                        16)
                    t_.wait_ge(sd.h, sd.n)
                    for f4 in range(4):
                        fcc = fq * 4 + f4
                        b = fcc % 2
                        t_.wait_ge(ss.h, evd[b])
                        for ci in range(8):
                            ins = t_.matmul(bank(b), wbuf[:, ci, f4 * 128:(f4 + 1) * 128],
                                            xls[:, ci, :], start=(ci == 0), stop=(ci == 7))
                        stt.inc(ins)
                        s_.wait_ge(stt.h, stt.n)
                        ss.inc(s_.activation(hT[:, fcc, :], bank(b), AF.Gelu,
                                             bias=b1g[:, fcc:fcc + 1]))
                        evd[b] = ss.n
                    w1_ready[fq % 2] = stt.n
                rTev = {2: 0, 3: 0}
                for cc in range(8):
                    b = 2 + cc % 2
                    for hf2 in range(2):
                        hh = tsub * 16 + cc * 2 + hf2
                        t_.wait_ge((sd if hh % 2 == 0 else sa).h, w2d[hh % 2])
                        if hf2 == 0:
                            t_.wait_ge(ss.h, ss.n if cc == 0 else rTev[b])
                        for fc in range(8):
                            ins = t_.matmul(bank(b), w2h[hh % 2][:, fc, :],
                                            hT[:, hf2 * 8 + fc, :],
                                            start=(hf2 == 0 and fc == 0),
                                            stop=(hf2 == 1 and fc == 7))
                        stt.inc(ins)
                        w2r[hh % 2] = stt.n
                        if hh + 2 < NS * 16:
                            w2_issue(hh + 2)
                    s_.wait_ge(stt.h, stt.n)
                    ss.inc(s_.activation(rT_[:, cc, :], bank(b), AF.Identity,
                                         bias=b1g[:, 16 + cc:17 + cc]))
                    rTev[b] = ss.n
                outev = {4: 0, 5: 0}
                for jc in range(8):
                    b = 4 + jc % 2
                    for hf2 in range(2):
                        hh = tsub * 16 + jc * 2 + hf2
                        t_.wait_ge(sg.h, wod[hh % 2])
                        if hf2 == 0:
                            if jc == 0:
                                t_.wait_ge(sv.h, sv.n)
                                t_.wait_ge(ss.h, ss.n)
                            else:
                                t_.wait_ge(ss.h, outev[b])
                        for hc in range(4):
                            ins = t_.matmul(bank(b), woh[hh % 2][:, hc, :],
                                            rT_[:, hf2 * 4 + hc, :],
                                            start=(hf2 == 0 and hc == 0),
                                            stop=(hf2 == 1 and hc == 3))
                        stt.inc(ins)
                        wor[hh % 2] = stt.n
                        if hh + 2 < NS * 16:
                            wo_issue(hh + 2)
                    s_.wait_ge(stt.h, stt.n)
                    s_.wait_ge(sv.h, sv.n)
                    ss.inc(s_.activation(t1b[:, :], bank(b), AF.Identity, scale=0.3,
                                         bias=bias_slice("F1_bo_s3", jc)))
                    outev[b] = ss.n
                    v_.wait_ge(ss.h, ss.n)
                    sv.inc(v_.tensor_tensor(t2b[:, :], t1b[:, :], xls[:, jc, :],
                                            op=ALU.add))
                    sv.inc(v_.drain())
                    sv.inc(v_.tensor_tensor(XT[:, jc, tl], XT[:, jc, tl], t2b[:, :],
                                            op=ALU.add))
                    sv.inc(v_.drain())
            barrier()

    def scoped(name, fn):
        with nc.named_scope(name):
            fn()

    for ph_name in phases:
        if ph_name == "dbg_qw":
            dq1 = nc.dram_tensor("dbg_qw1", [C, FF], F32, kind="ExternalOutput").ap()
            dq2 = nc.dram_tensor("dbg_qw2", [FF, C], F32, kind="ExternalOutput").ap()
            quant_weight_prep(din["F0_w1"], FF, C, scr_qw1, 250)
            quant_weight_prep(din["F0_w2"], C, FF, scr_qw2, 251)
            barrier()
            with sbt("dbgbuf", [128, FF], BF16) as db, sbt("dbgbuff", [128, FF], F32) as dbf:
                for rr in range(8):
                    sy_.wait_ge(sd.h, sd.n)
                    sy_.wait_ge(sv.h, sv.n)
                    sd.inc(sy_.dma_start(db[:, :], scr_qw1[rr * 128:(rr + 1) * 128, :]), 16)
                    v_.wait_ge(sd.h, sd.n)
                    sv.inc(v_.tensor_copy(dbf[:, :], db[:, :]))
                    sv.inc(v_.drain())
                    sy_.wait_ge(sv.h, sv.n)
                    sd.inc(sy_.dma_start(dq1[rr * 128:(rr + 1) * 128, :], dbf[:, :]), 16)
                for rr in range(32):
                    sy_.wait_ge(sd.h, sd.n)
                    sy_.wait_ge(sv.h, sv.n)
                    sd.inc(sy_.dma_start(db[:, 0:C], scr_qw2[rr * 128:(rr + 1) * 128, :]), 16)
                    v_.wait_ge(sd.h, sd.n)
                    sv.inc(v_.tensor_copy(dbf[:, 0:C], db[:, 0:C]))
                    sv.inc(v_.drain())
                    sy_.wait_ge(sv.h, sv.n)
                    sd.inc(sy_.dma_start(dq2[rr * 128:(rr + 1) * 128, :], dbf[:, 0:C]), 16)
            sy_.wait_ge(sd.h, sd.n)
        elif ph_name == "l0a":
            scoped("ln0", ln_and_xln); scoped("mla", mla_attn)
        elif ph_name == "f0":
            scoped("ln1", ln_and_xln); scoped("f0", f0_ffn)
        elif ph_name == "l1a":
            scoped("ln2", ln_and_xln); scoped("qkv1", mha_qkv)
            scoped("mha", lambda: mha_attn_core("L1_Wo", "L1_bo", "L1_bv"))
        elif ph_name == "f1":
            scoped("ln3", ln_and_xln); scoped("f1", f1_ffn)
    scoped("epi", epilogue)

    st.close()
    return nc, din


def kernel(**inputs):
    T = inputs["L0_attn_x"].shape[1]
    nc, _ = build(T=T)
    ident = np.eye(128, dtype=np.float32)
    base = {
        "ident": ident,
        "ident_bf": ident.astype(ml_dtypes.bfloat16),
        "ident_h": ident.astype(np.float16),
        "iota8": np.arange(8, dtype=np.float32)[None],
        "L0_wq_d": inputs["L0_attn_wq_d"], "L0_bq_d": inputs["L0_attn_bq_d"],
        "L0_w_qk": inputs["L0_attn_w_qk"], "L0_b_qk": inputs["L0_attn_b_qk"],
        "L0_wkv_d": inputs["L0_attn_wkv_d"], "L0_bkv_d": inputs["L0_attn_bkv_d"],
        "L0_wv_u": inputs["L0_attn_wv_u"], "L0_bv_u": inputs["L0_attn_bv_u"],
        "L0_w_proj": inputs["L0_attn_w_proj"], "L0_b_proj": inputs["L0_attn_b_proj"],
        "F0_ln1_g": inputs["L0_ffn_ln1_g"], "F0_ln1_b": inputs["L0_ffn_ln1_b"],
        "F0_w1": inputs["L0_ffn_w1"], "F0_b1": inputs["L0_ffn_b1"],
        "F0_ln2_g": inputs["L0_ffn_ln2_g"], "F0_ln2_b": inputs["L0_ffn_ln2_b"],
        "F0_w2": inputs["L0_ffn_w2"], "F0_b2": inputs["L0_ffn_b2"],
        "L1_Wq": inputs["L1_attn_Wq"], "L1_bq": inputs["L1_attn_bq"],
        "L1_Wk": inputs["L1_attn_Wk"], "L1_bk": inputs["L1_attn_bk"],
        "L1_Wv": inputs["L1_attn_Wv"], "L1_bv": inputs["L1_attn_bv"],
        "L1_Wo": inputs["L1_attn_Wo"], "L1_bo": inputs["L1_attn_bo"],
        "F1_Wp1": inputs["L1_ffn_Wp1"], "F1_bp1": inputs["L1_ffn_bp1"],
        "F1_Wp2": inputs["L1_ffn_Wp2"], "F1_bp2": inputs["L1_ffn_bp2"],
        "F1_W1": inputs["L1_ffn_W1"], "F1_b1": inputs["L1_ffn_b1"],
        "F1_W2": inputs["L1_ffn_W2"], "F1_b2": inputs["L1_ffn_b2"],
        "F1_Wo": inputs["L1_ffn_Wo"], "F1_bo": inputs["L1_ffn_bo"],
    }
    x = np.ascontiguousarray(inputs["L0_attn_x"])
    in_maps = [dict(base, x=np.ascontiguousarray(x[b])) for b in range(x.shape[0])]
    res = run_bass_kernel_spmd(nc, in_maps, core_ids=list(range(x.shape[0])))
    return np.stack([r["out"] for r in res.results], axis=0)


# version 27
# speedup vs baseline: 1.2106x; 1.0385x over previous
import contextlib
import numpy as np
import ml_dtypes
import concourse.bass as bass
import concourse.mybir as mybir
from concourse.bass_utils import run_bass_kernel_spmd

F32 = mybir.dt.float32
BF16 = mybir.dt.bfloat16
I32 = mybir.dt.int32
F16 = mybir.dt.float16
AF = mybir.ActivationFunctionType
ALU = mybir.AluOpType
AX = mybir.AxisListType

C = 1024
NH = 16
HD = 64
CQ = 512
KV = 64
FF = 4096
NAV = 128
NE = 8
FE = 2048
LN_EPS = 1e-5
S4 = float(HD ** 0.25)
ANCHOR = 8.0

MAGIC = 2048.0


class Sem:

    def __init__(self, st, nc, name):
        self.h = st.enter_context(nc.semaphore(name))
        self.n = 0

    def inc(self, instr, k=1):
        instr.then_inc(self.h, k)
        self.n += k
        return instr


class Ctx:
    def __init__(self, nc, st, T):
        self.nc = nc
        self.st = st
        self.T = T
        self.NT = T // 128
        self.NS = T // 512


def sb(ctx, name, shape, dtype):
    return ctx.st.enter_context(ctx.nc.sbuf_tensor(name, shape, dtype))


def build(T=4096, phases=("l0a", "f0", "l1a", "f1")):
    nc = bass.Bass("TRN2", target_bir_lowering=False)
    st = contextlib.ExitStack()
    ctx = Ctx(nc, st, T)
    NT, NS = ctx.NT, ctx.NS

    din = {}

    def dram_in(name, shape, dtype=F32):
        din[name] = nc.dram_tensor(name, list(shape), dtype, kind="ExternalInput").ap()
        return din[name]

    x_in = dram_in("x", (T, C))
    out_hbm = nc.dram_tensor("out", [T, C], F32, kind="ExternalOutput").ap()

    dram_in("L0_wq_d", (CQ, C)); dram_in("L0_bq_d", (CQ,))
    dram_in("L0_w_qk", (C, CQ)); dram_in("L0_b_qk", (C,))
    dram_in("L0_wkv_d", (KV, C)); dram_in("L0_bkv_d", (KV,))
    dram_in("L0_wv_u", (C, KV)); dram_in("L0_bv_u", (C,))
    dram_in("L0_w_proj", (C, C)); dram_in("L0_b_proj", (C,))
    dram_in("F0_ln1_g", (C,)); dram_in("F0_ln1_b", (C,))
    dram_in("F0_w1", (FF, C)); dram_in("F0_b1", (FF,))
    dram_in("F0_ln2_g", (FF,)); dram_in("F0_ln2_b", (FF,))
    dram_in("F0_w2", (C, FF)); dram_in("F0_b2", (C,))
    dram_in("L1_Wq", (C, C)); dram_in("L1_bq", (C,))
    dram_in("L1_Wk", (C, C)); dram_in("L1_bk", (C,))
    dram_in("L1_Wv", (C, C)); dram_in("L1_bv", (C,))
    dram_in("L1_Wo", (C, C)); dram_in("L1_bo", (C,))
    dram_in("F1_Wp1", (C, NAV)); dram_in("F1_bp1", (NAV,))
    dram_in("F1_Wp2", (NAV, NE)); dram_in("F1_bp2", (NE,))
    dram_in("F1_W1", (NE, C, FE)); dram_in("F1_b1", (NE, FE))
    dram_in("F1_W2", (NE, FE, C)); dram_in("F1_b2", (NE, C))
    dram_in("F1_Wo", (C, C)); dram_in("F1_bo", (C,))
    dram_in("ident", (128, 128))
    dram_in("ident_bf", (128, 128), BF16)
    dram_in("ident_h", (128, 128), F16)
    dram_in("iota8", (1, 8))

    scr_xln = nc.dram_tensor("scr_xln", [C, T], BF16, kind="Internal").ap()
    scr_q = nc.dram_tensor("scr_q", [C, T], BF16, kind="Internal").ap()
    scr_k = nc.dram_tensor("scr_k", [C, T], BF16, kind="Internal").ap()
    scr_v = nc.dram_tensor("scr_v", [T, C], BF16, kind="Internal").ap()
    scr_wt = nc.dram_tensor("scr_wt", [C, C], BF16, kind="Internal").ap()
    scr_qw1 = nc.dram_tensor("scr_qw1", [C, FF], BF16, kind="Internal").ap()
    scr_qw2 = nc.dram_tensor("scr_qw2", [FF, C], BF16, kind="Internal").ap()
    scr_w1s = nc.dram_tensor("scr_w1s", [C, FE], BF16, kind="Internal").ap()
    scr_w2s = nc.dram_tensor("scr_w2s", [FE, C], BF16, kind="Internal").ap()
    scr_wo = nc.dram_tensor("scr_wo", [C, C], BF16, kind="Internal").ap()

    XT = sb(ctx, "XT", [128, 8, T], F32)
    rows = sb(ctx, "rows", [1, 5, 512], F32)
    ones_f1 = sb(ctx, "ones_f1", [1, 128], F32)
    ones_b1 = sb(ctx, "ones_b1", [1, 128], BF16)
    ones_c128 = sb(ctx, "ones_c128", [128, 1], F32)
    ones_c128b = sb(ctx, "ones_c128b", [128, 1], BF16)
    negones64 = sb(ctx, "negones64", [64, 1], BF16)
    ident_f = sb(ctx, "ident_f", [128, 128], F32)
    ident_h = sb(ctx, "ident_h_sb", [128, 128], F16)
    bias_pool = sb(ctx, "bias_pool", [128, 256], F32)
    BIAS = {}

    PS = ctx.st.enter_context(nc.psum_tensor("PS", [128, 8, 512], F32))
    PS16 = PS.bitcast(F16)

    sd = Sem(st, nc, "sd")
    sa = Sem(st, nc, "sa")
    sg = Sem(st, nc, "sg")
    stt = Sem(st, nc, "stt")
    sv = Sem(st, nc, "sv")
    ss = Sem(st, nc, "ss")
    sc_ = Sem(st, nc, "sc_")
    sq_ = Sem(st, nc, "sq_")
    sw_ = Sem(st, nc, "sw_")

    blk = st.enter_context(nc.Block())
    t_, v_, s_, sy_, g_ = nc.tensor, nc.vector, nc.scalar, nc.sync, nc.gpsimd
    _names = {}

    def sbt(name, shape, dtype):
        k = _names.get(name, 0)
        _names[name] = k + 1
        return nc.sbuf_tensor(name if k == 0 else name + "__" + str(k), shape, dtype)

    def bank(b, n=1, p0=0, p1=128):
        return PS[p0:p1, b:b + n, :]

    def wait_all(eng):
        eng.wait_ge(sc_.h, sc_.n)
        eng.wait_ge(sq_.h, sq_.n)
        eng.wait_ge(sw_.h, sw_.n)
        eng.wait_ge(stt.h, stt.n)
        eng.wait_ge(sv.h, sv.n)
        eng.wait_ge(ss.h, ss.n)
        eng.wait_ge(sd.h, sd.n)
        eng.wait_ge(sa.h, sa.n)
        eng.wait_ge(sg.h, sg.n)

    def barrier():
        for eng in (t_, v_, s_, sy_, g_):
            wait_all(eng)

    sc_.inc(sy_.dma_start(ident_f[:, :], din["ident"]), 16)
    sc_.inc(sy_.dma_start(ident_h[:, :], din["ident_h"]), 16)

    bias_col = [0]

    def load_bias(name, vec_ap, n, queue=sy_, qsem=sc_):
        c0 = bias_col[0]
        k = (n + 127) // 128
        assert n % 128 == 0 or n < 128
        ap = vec_ap.rearrange("(k p) -> p k", p=min(n, 128)) if n >= 128 else vec_ap
        if n >= 128:
            qsem.inc(queue.dma_start(bias_pool[:, c0:c0 + k], ap, allow_slow_non_contiguous=True), 16)
        else:
            qsem.inc(queue.dma_start(bias_pool[0:n, c0:c0 + 1], vec_ap.rearrange("(n o) -> n o", o=1), allow_slow_non_contiguous=True), 16)
        BIAS[name] = (c0, k)
        bias_col[0] += k

    def load_bias64(name, vec_ap, n):
        c0 = bias_col[0]
        k = n // 64
        ap = vec_ap.rearrange("(h d) -> d h", d=64)
        sc_.inc(sy_.dma_start(bias_pool[0:64, c0:c0 + k], ap,
                              allow_slow_non_contiguous=True), 16)
        BIAS[name + "_64"] = (c0, k)
        bias_col[0] += k

    def bias64_slice(name, h):
        c0, k = BIAS[name + "_64"]
        return bias_pool[0:64, c0 + h:c0 + h + 1]

    for nm in ["L0_bq_d", "L0_b_qk", "L0_bkv_d", "L0_bv_u", "L0_b_proj",
               "F0_ln1_g", "F0_ln1_b", "F0_b2",
               "L1_bq", "L1_bk", "L1_bv", "L1_bo", "F1_bo"]:
        n = din[nm].tensor.shape[0]
        load_bias(nm, din[nm], n)
    for nm in ["F0_b1", "F0_ln2_g", "F0_ln2_b"]:
        load_bias(nm, din[nm], FF)
    load_bias64("L0_b_qk", din["L0_b_qk"], C)

    def bias_slice(name, j, p0=0, p1=128):
        c0, k = BIAS[name]
        return bias_pool[p0:p1, c0 + j:c0 + j + 1]

    consts = sb(ctx, "consts", [128, 8], F32)
    cc_eps = sb(ctx, "cc_eps", [128, 1], F32)
    v_.memset(cc_eps[:, :], LN_EPS)
    v_.memset(consts[:, 0:1], -ANCHOR)
    v_.memset(consts[:, 1:2], 0.0)
    v_.memset(ones_f1[:, :], 1.0)
    v_.memset(ones_b1[:, :], 1.0)
    v_.memset(ones_c128[:, :], 1.0)
    v_.memset(ones_c128b[:, :], 1.0)
    v_.memset(negones64[:, :], -1.0)
    v_.memset(rows[:, :, :], 0.0)
    sv.inc(v_.drain())

    v_.wait_ge(sc_.h, sc_.n)
    for nm, scale in [("L1_bq", 1.0 / S4), ("L1_bk", 1.0 / S4), ("L0_b_qk", 0.125),
                      ("F1_bo", 0.3)]:
        c0, k = BIAS[nm]
        c1 = bias_col[0]
        v_.tensor_scalar(bias_pool[:, c1:c1 + k], bias_pool[:, c0:c0 + k], scale, None,
                         op0=ALU.mult)
        BIAS[nm + ("_s3" if nm == "F1_bo" else "_s")] = (c1, k)
        bias_col[0] += k
    c0, k = BIAS["L0_b_qk_64"]
    c1 = bias_col[0]
    v_.tensor_scalar(bias_pool[0:64, c1:c1 + k], bias_pool[0:64, c0:c0 + k], 0.125, None,
                     op0=ALU.mult)
    BIAS["L0_b_qk_s_64"] = (c1, k)
    bias_col[0] += k
    sv.inc(v_.drain())

    with sbt("xin_stage", [128, C], F32) as xin_stage:
      for tc in range(NT):
          sd.inc(sy_.dma_start(xin_stage[:, :], x_in[tc * 128:(tc + 1) * 128, :]), 16)
          t_.wait_ge(sd.h, sd.n)
          t_.wait_ge(sc_.h, sc_.n)
          if tc > 0:
              t_.wait_ge(ss.h, ss.n)
          for cc in range(8):
              b = cc // 4
              off = cc % 4
              tgt = PS[:, b, :].rearrange("p (o q) -> p o q", o=4)[:, off, :]
              ins = t_.transpose(tgt, xin_stage[:, cc * 128:(cc + 1) * 128], ident_f[:, :])
          stt.inc(ins)
          s_.wait_ge(stt.h, stt.n)
          for cc in range(8):
              b = cc // 4
              off = cc % 4
              src = PS[:, b, :].rearrange("p (o q) -> p o q", o=4)[:, off, :]
              ins = s_.copy(XT[:, cc, tc * 128:(tc + 1) * 128], src)
          ss.inc(ins)
          sy_.wait_ge(stt.h, stt.n)


    barrier()

    def ln_and_xln():
        with contextlib.ExitStack() as ph:
            xst = [ph.enter_context(sbt(f"xls_ln{i}", [128, 8, 512], BF16))
                   for i in range(2)]
            xl = [ph.enter_context(sbt(f"xl_ln{i}", [128, 512], F32)) for i in range(4)]
            sq2 = [ph.enter_context(sbt(f"sq_ln{i}", [128, 512], F32)) for i in range(2)]
            rows2 = ph.enter_context(sbt("rows2_ln", [1, 5, 512], F32))
            R = [rows, rows2]
            BK = {0: (6, 7, 4, 5), 1: (2, 3, 0, 1)}
            lnss = [0, 0]
            lnsv = [0, 0]
            lndma = [0, 0]
            sqrd = {0: 0, 1: 0}
            mmN, mmB = {}, {}

            def stA(t):
                tl = slice(t * 512, (t + 1) * 512)
                p = t % 2
                s6, s7, _, _ = BK[p]
                for cc in range(8):
                    v_.wait_ge(stt.h, sqrd[cc % 2])
                    sv.inc(v_.tensor_tensor(sq2[cc % 2][:, :], XT[:, cc, tl],
                                            XT[:, cc, tl], op=ALU.mult))
                    t_.wait_ge(sv.h, sv.n)
                    if cc == 0:
                        t_.wait_ge(ss.h, lnss[p])
                    stt.inc(t_.matmul(bank(s6, 1, 0, 1), ones_c128[:, :], XT[:, cc, tl],
                                      start=(cc == 0), stop=(cc == 7)))
                    stt.inc(t_.matmul(bank(s7, 1, 0, 1), ones_c128[:, :], sq2[cc % 2][:, :],
                                      start=(cc == 0), stop=(cc == 7)))
                    sqrd[cc % 2] = stt.n
                mmN[t] = stt.n

            def stB(t):
                p = t % 2
                s6, s7, b4, b5 = BK[p]
                R_ = R[p]
                v_.wait_ge(stt.h, mmN[t])
                v_.tensor_scalar(R_[:, 0, :], bank(s6, 1, 0, 1), 1.0 / C, None,
                                 op0=ALU.mult)
                sv.inc(v_.drain())
                s_.wait_ge(sv.h, sv.n)
                ss.inc(s_.activation(R_[:, 4, :], R_[:, 0, :], AF.Square))
                ss.inc(s_.activation(R_[:, 3, :], bank(s7, 1, 0, 1), AF.Identity,
                                     scale=1.0 / C, bias=cc_eps[0:1, 0:1]))
                lnss[p] = ss.n
                v_.wait_ge(ss.h, ss.n)
                sv.inc(v_.tensor_tensor(R_[:, 3, :], R_[:, 3, :], R_[:, 4, :],
                                        op=ALU.subtract))
                s_.wait_ge(sv.h, sv.n)
                ss.inc(s_.activation(R_[:, 4, :], R_[:, 3, :], AF.Ln))
                ss.inc(s_.drain())
                ss.inc(s_.activation(R_[:, 1, :], R_[:, 4, :], AF.Exp, scale=-0.5))
                v_.wait_ge(ss.h, ss.n)
                sv.inc(v_.tensor_tensor(R_[:, 2, :], R_[:, 0, :], R_[:, 1, :],
                                        op=ALU.mult))
                sv.inc(v_.drain())
                if t == NS - 1:
                    sv.inc(v_.tensor_copy(consts[0:1, 2:3], R_[0:1, 0, 511:512]))
                    sv.inc(v_.tensor_copy(consts[0:1, 3:4], R_[0:1, 1, 511:512]))
                    sv.inc(v_.drain())
                t_.wait_ge(sv.h, sv.n)
                t_.wait_ge(sv.h, lnsv[p])
                stt.inc(t_.matmul(bank(b4, 1), ones_f1[:, :], R_[:, 1, :],
                                  start=True, stop=True))
                stt.inc(t_.matmul(bank(b5, 1), ones_f1[:, :], R_[:, 2, :],
                                  start=True, stop=True))
                mmB[t] = stt.n

            def stC(t):
                tl = slice(t * 512, (t + 1) * 512)
                p = t % 2
                _, _, b4, b5 = BK[p]
                v_.wait_ge(stt.h, mmB[t])
                v_.wait_ge(sd.h, lndma[p])
                for cc in range(8):
                    sv.inc(v_.tensor_tensor(xl[cc % 4][:, :], XT[:, cc, tl], bank(b4, 1),
                                            op=ALU.mult))
                    if cc % 4 == 3:
                        sv.inc(v_.drain())
                        for cc2 in range(cc - 3, cc + 1):
                            sv.inc(v_.tensor_tensor(xst[p][:, cc2, :], xl[cc2 % 4][:, :],
                                                    bank(b5, 1), op=ALU.subtract))
                        sv.inc(v_.drain())
                lnsv[p] = sv.n
                sy_.wait_ge(sv.h, sv.n)
                sy_.wait_ge(sd.h, sd.n)
                dst = scr_xln.rearrange("(k p) t -> p k t", p=128)[:, :, tl]
                sd.inc(sy_.dma_start(dst, xst[p][:, :, :]), 16)
                lndma[p] = sd.n

            stA(0)
            stB(0)
            for t in range(NS):
                if t + 1 < NS:
                    stA(t + 1)
                stC(t)
                if t + 1 < NS:
                    stB(t + 1)
            barrier()

    _wtp_id = [0]

    def wt_prep(w_ap, rows_n, cols_n, dst_sb, scale=1.0):
        nr, nc_ = (rows_n + 127) // 128, (cols_n + 127) // 128
        _wtp_id[0] += 1
        _wst = contextlib.ExitStack()
        wnat = _wst.enter_context(
            sbt("wnat%d" % _wtp_id[0], [128, cols_n], F32))
        for rr in range(nr):
            h_ = min(128, rows_n - rr * 128)
            sy_.wait_ge(stt.h, stt.n)
            sd.inc(sy_.dma_start(wnat[0:h_, 0:cols_n], w_ap[rr * 128:rr * 128 + h_, :]), 16)
            t_.wait_ge(sd.h, sd.n)
            t_.wait_ge(ss.h, ss.n)
            for jj in range(nc_):
                b = jj % 2
                off = jj // 2 % 2
                w_ = min(128, cols_n - jj * 128)
                tgt = PS[:, b, :].rearrange("p (o q) -> p o q", o=4)[0:w_, off, 0:h_]
                stt.inc(t_.transpose(tgt, wnat[0:h_, jj * 128:jj * 128 + w_],
                                     ident_f[0:h_, 0:h_]))
                if jj % 4 == 3 or jj == nc_ - 1:
                    s_.wait_ge(stt.h, stt.n)
                    for jj2 in range(jj - jj % 4, jj + 1):
                        b2 = jj2 % 2
                        off2 = jj2 // 2 % 2
                        w2 = min(128, cols_n - jj2 * 128)
                        src = PS[:, b2, :].rearrange("p (o q) -> p o q", o=4)[0:w2, off2, 0:h_]
                        ins = s_.activation(dst_sb[0:w2, jj2, rr * 128:rr * 128 + h_], src,
                                            AF.Identity, scale=scale)
                    ss.inc(ins)
                    t_.wait_ge(ss.h, ss.n)
        barrier()
        _wst.close()

    def wt_to_scr(w_name, scale=1.0):
        with contextlib.ExitStack() as ph:
            wt_full = ph.enter_context(sbt("wtf_" + w_name, [128, 8, C], BF16))
            wt_prep(din[w_name], C, C, wt_full, scale=scale)
            barrier()
            sd.inc(sy_.dma_start(scr_wt.rearrange("(k p) c -> p k c", p=128),
                                 wt_full[:, :, :]), 16)
            barrier()

    def epilogue():
        barrier()
        xout_stage = sb(ctx, "xout_stage", [128, C], F32)
        for tc in range(NT):
            t_.wait_ge(sd.h, sd.n)
            for cc in range(8):
                b = cc // 4
                off = cc % 4
                tgt = PS[:, b, :].rearrange("p (o q) -> p o q", o=4)[:, off, :]
                ins = t_.transpose(tgt, XT[:, cc, tc * 128:(tc + 1) * 128], ident_f[:, :])
            stt.inc(ins)
            s_.wait_ge(stt.h, stt.n)
            for cc in range(8):
                b = cc // 4
                off = cc % 4
                src = PS[:, b, :].rearrange("p (o q) -> p o q", o=4)[:, off, :]
                ins = s_.copy(xout_stage[:, cc * 128:(cc + 1) * 128], src)
            ss.inc(ins)
            sy_.wait_ge(ss.h, ss.n)
            sd.inc(sy_.dma_start(out_hbm[tc * 128:(tc + 1) * 128, :], xout_stage[:, :]), 16)
            t_.wait_ge(ss.h, ss.n)
        sy_.wait_ge(sd.h, sd.n)

    xadd_state = {"done": 0}
    wtp = [sb(ctx, "wtp0", [128, C], BF16), sb(ctx, "wtp1", [128, C], BF16)]
    wtp_last = [0, 0]

    def proj_pair(bo_name, pair, first, y_pair, wn=None):
        slab = wtp[pair % 2]
        sy_.wait_ge(stt.h, wtp_last[pair % 2])
        sy_.wait_ge(sw_.h, sw_.n)
        sw_.inc(sy_.dma_start(slab[:, :], scr_wt[pair * 128:(pair + 1) * 128, :]), 16)
        t_.wait_ge(sw_.h, sw_.n)
        nts = y_pair.shape[-1] // 512
        toff = 0 if wn is None else wn * (y_pair.shape[-1])
        t_.wait_ge(sv.h, sv.n)
        t_.wait_ge(ss.h, ss.n)
        add_done = {0: 0, 1: 0}
        for cc in range(8):
            for ts2 in range(nts):
                b = (cc * nts + ts2) % 2
                t_.wait_ge(sv.h, add_done[b])
                stt.inc(t_.matmul(bank(6 + b), slab[:, cc * 128:(cc + 1) * 128],
                                  y_pair[:, ts2 * 512:(ts2 + 1) * 512],
                                  start=True, stop=True))
                v_.wait_ge(stt.h, stt.n)
                tl2 = slice(toff + ts2 * 512, toff + (ts2 + 1) * 512)
                if first:
                    ins = v_.scalar_tensor_tensor(XT[:, cc, tl2], bank(6 + b),
                                                  bias_slice(bo_name, cc), XT[:, cc, tl2],
                                                  op0=ALU.add, op1=ALU.add)
                else:
                    ins = v_.tensor_tensor(XT[:, cc, tl2], XT[:, cc, tl2], bank(6 + b),
                                           op=ALU.add)
                sv.inc(ins)
                add_done[b] = sv.n
        wtp_last[pair % 2] = stt.n
        sv.inc(v_.drain())
        xadd_state["done"] = sv.n

    def mha_qkv():
        with contextlib.ExitStack() as ph:
            wt_q = ph.enter_context(sbt("wt_q", [128, 8, C], BF16))
            wt_k = ph.enter_context(sbt("wt_k", [128, 8, C], BF16))
            stage_qk = ph.enter_context(sbt("stage_qk", [128, 4, 512], BF16))
            xls2 = [ph.enter_context(sbt(f"xls_qkv{i}", [128, 8, 512], BF16))
                    for i in range(2)]
            vstage = ph.enter_context(sbt("vstage", [128, 2, 1024], BF16))
            wt_prep(din["L1_Wq"], C, C, wt_q, scale=1.0 / S4)
            wt_prep(din["L1_Wk"], C, C, wt_k, scale=1.0 / S4)
            wt_v = wt_q
            barrier()
            xdma = [0, 0]
            xread = [0, 0]

            def x_issue(ts):
                buf = ts % 2
                sy_.wait_ge(stt.h, xread[buf])
                sy_.wait_ge(sq_.h, sq_.n)
                src = scr_xln.rearrange("(k p) t -> p k t", p=128)[
                    :, :, ts * 512:(ts + 1) * 512]
                sq_.inc(sy_.dma_start(xls2[buf][:, :, :], src), 16)
                xdma[buf] = sq_.n

            evac_done = [0, 0, 0, 0]
            x_issue(0)
            for tsub in range(NS):
                tl = slice(tsub * 512, (tsub + 1) * 512)
                xls = xls2[tsub % 2]
                if tsub + 1 < NS:
                    x_issue(tsub + 1)
                t_.wait_ge(sq_.h, xdma[tsub % 2])
                s_.wait_ge(sd.h, sd.n)
                for mat, wt, bname, scr in ((0, wt_q, "L1_bq_s", scr_q),
                                            (1, wt_k, "L1_bk_s", scr_k)):
                    s_.wait_ge(sd.h, sd.n)
                    for co in range(8):
                        b = co % 2
                        if co == 4:
                            s_.wait_ge(sd.h, sd.n)
                        t_.wait_ge(ss.h, evac_done[b])
                        for ci in range(8):
                            ins = t_.matmul(bank(b), wt[:, ci, co * 128:(co + 1) * 128],
                                            xls[:, ci, :], start=(ci == 0), stop=(ci == 7))
                        stt.inc(ins)
                        s_.wait_ge(stt.h, stt.n)
                        ss.inc(s_.activation(stage_qk[:, co % 4, :], bank(b), AF.Identity,
                                             bias=bias_slice(bname, co)))
                        evac_done[b] = ss.n
                        if co % 4 == 3:
                            sy_.wait_ge(ss.h, ss.n)
                            h4 = co // 4
                            dstm = scr.rearrange("(k p) t -> p k t",
                                                 p=128)[:, h4 * 4:h4 * 4 + 4, tl]
                            sd.inc(sy_.dma_start(dstm, stage_qk[:, :, :]), 16)
                xread[tsub % 2] = stt.n
            barrier()
            wt_prep(din["L1_Wv"], C, C, wt_v, scale=1.0)
            barrier()
            xdma = [0, 0]
            xread = [0, 0]
            x_issue2 = x_issue
            x_issue2(0)
            for tsub in range(NS):
                tl = slice(tsub * 512, (tsub + 1) * 512)
                xls = xls2[tsub % 2]
                if tsub + 1 < NS:
                    x_issue2(tsub + 1)
                t_.wait_ge(sq_.h, xdma[tsub % 2])
                s_.wait_ge(sd.h, sd.n)
                for tci in range(4):
                    if tci == 2:
                        s_.wait_ge(sd.h, sd.n)
                    for ch in range(2):
                        b = 2 + ch
                        t_.wait_ge(ss.h, evac_done[b])
                        for ci in range(8):
                            ins = t_.matmul(bank(b), xls[:, ci, tci * 128:(tci + 1) * 128],
                                            wt_v[:, ci, ch * 512:(ch + 1) * 512],
                                            start=(ci == 0), stop=(ci == 7))
                        stt.inc(ins)
                        s_.wait_ge(stt.h, stt.n)
                        ss.inc(s_.activation(vstage[:, tci % 2, ch * 512:(ch + 1) * 512],
                                             bank(b), AF.Identity))
                        evac_done[b] = ss.n
                    if tci % 2 == 1:
                        sy_.wait_ge(ss.h, ss.n)
                        t0 = tsub * 512 + (tci - 1) * 128
                        sd.inc(sy_.dma_start(
                            scr_v[t0:t0 + 256, :].rearrange("(a p) c -> p a c", p=128),
                            vstage[:, :, :]), 16)
                xread[tsub % 2] = stt.n
            barrier()

    def mha_attn_core(wo_name, bo_name, bv_name):
        NC_ = ctx.NT
        NW = T // 1024
        wt_to_scr(wo_name)
        with contextlib.ExitStack() as ph:
            kaug = ph.enter_context(sbt("kaug", [65, T], BF16))
            qaug2 = [ph.enter_context(sbt(f"qaug_{i}", [65, T], BF16))
                     for i in range(2)]
            v4 = ph.enter_context(sbt("v4", [128, NC_, 2, 66], BF16))
            e_sb = [ph.enter_context(sbt(f"e_sb{i}", [128, 1024], BF16))
                    for i in range(4)]
            qk_tmp = ph.enter_context(sbt("qk_tmp", [64, 2048], BF16))
            den_row = ph.enter_context(sbt("den_row", [1, 1024], F32))
            u_sb = ph.enter_context(sbt("u_sb", [64, 1024], F32))
            y_pair = ph.enter_context(sbt("y_pair", [128, T], BF16))
            v_.memset(v4[:, :, :, 64:65], 1.0)
            v_.memset(kaug[64:65, :], 1.0)
            sv.inc(v_.drain())
            barrier()
            exp_done = {0: 0, 2: 0}
            pvrd = {0: 0, 1: 0, 2: 0, 3: 0}
            v4_ready = [0]
            ep_state = {0: 0, 1: 0}
            qread = [0, 0]
            anch_mm = [0]
            bdma = [0]
            for h in range(NH):
                qaug = qaug2[h % 2]
                if h % 2 == 0:
                    srcv = scr_v.rearrange("(c p) (g d) -> p c g d", p=128, d=64)
                    sy_.wait_ge(stt.h, stt.n)
                    sy_.wait_ge(sd.h, sd.n)
                    for g2 in range(2):
                        sd.inc(sy_.dma_start(v4[:, :, g2, 0:64],
                                             srcv[:, :, h + g2, :]), 16)
                    v4_ready[0] = sd.n
                sy_.wait_ge(stt.h, qread[(h + 1) % 2])
                sy_.wait_ge(sq_.h, sq_.n)
                sq_.inc(sy_.dma_start(kaug[0:64, :],
                                      scr_k[h * 64:(h + 1) * 64, :]), 16)
                if h == 0:
                    sq_.inc(sy_.dma_start(qaug[0:64, :], scr_q[0:64, :]), 16)
                if h + 1 < NH:
                    sq_.inc(sy_.dma_start(qaug2[(h + 1) % 2][0:64, :],
                                          scr_q[(h + 1) * 64:(h + 2) * 64, :]), 16)
                bdma[0] = sq_.n
                t_.wait_ge(sd.h, v4_ready[0])
                for hf2 in range(max(1, T // 2048)):
                    hl = slice(hf2 * 2048, min((hf2 + 1) * 2048, T))
                    v_.wait_ge(sq_.h, bdma[0])
                    v_.wait_ge(stt.h, anch_mm[0])
                    sv.inc(v_.tensor_tensor(qk_tmp[:, :], qaug[0:64, hl], kaug[0:64, hl],
                                            op=ALU.mult))
                    t_.wait_ge(sv.h, sv.n)
                    if hf2 == 0:
                        t_.wait_ge(ss.h, ss.n)
                    for j4 in range(4):
                        stt.inc(t_.matmul(PS[0:1, 4 + (j4 % 2) * 2 + j4 // 2, :],
                                          negones64[:, :],
                                          qk_tmp[:, j4 * 512:(j4 + 1) * 512],
                                          start=True, stop=True))
                    anch_mm[0] = stt.n
                    s_.wait_ge(stt.h, stt.n)
                    for j4 in range(4):
                        ins = s_.activation(qaug[64:65, hf2 * 2048 + j4 * 512:
                                                 hf2 * 2048 + (j4 + 1) * 512],
                                            PS[0:1, 4 + (j4 % 2) * 2 + j4 // 2, :],
                                            AF.Identity, bias=consts[0:1, 0:1])
                    ss.inc(ins)
                    t_.wait_ge(ss.h, ss.n)
                for w in range(NW):
                    wl = slice(w * 1024, (w + 1) * 1024)

                    def emit_qk(sc):
                        sb_ = (sc % 2) * 2
                        t_.wait_ge(ss.h, exp_done[sb_])
                        if sc < 2 and w == 0:
                            t_.wait_ge(sv.h, xadd_state["done"])
                        for half in range(2):
                            ins = t_.matmul(PS[:, sb_ + half, :],
                                            kaug[:, sc * 128:(sc + 1) * 128],
                                            qaug[:, w * 1024 + half * 512:
                                                 w * 1024 + (half + 1) * 512],
                                            start=True, stop=True)
                        stt.inc(ins)
                        qk_done[sc] = stt.n

                    def emit_exp(sc):
                        sb_ = (sc % 2) * 2
                        s_.wait_ge(stt.h, qk_done[sc])
                        s_.wait_ge(stt.h, pvrd[sc % 4])
                        ss.inc(s_.activation(e_sb[sc % 4][:, :],
                                             bank(sb_, 2).rearrange("p a t -> p (a t)"),
                                             AF.Exp))
                        exp_done[sb_] = ss.n
                        ex_done[sc] = ss.n

                    pb = 4 + (w % 2) * 2

                    def emit_pv(sc):
                        t_.wait_ge(ss.h, ex_done[sc])
                        if sc == 0:
                            t_.wait_ge(sv.h, ep_state[w % 2])
                        for half in range(2):
                            ins = t_.matmul(PS[0:65, pb + half, :],
                                            v4[:, sc, h % 2, 0:65],
                                            e_sb[sc % 4][:, half * 512:(half + 1) * 512],
                                            start=(sc == 0), stop=(sc == NC_ - 1))
                        stt.inc(ins)
                        pvrd[sc % 4] = stt.n

                    qk_done, ex_done = {}, {}
                    emit_qk(0)
                    emit_exp(0)
                    emit_qk(1)
                    emit_exp(1)
                    for sc in range(2, NC_):
                        emit_qk(sc)
                        emit_exp(sc)
                        emit_pv(sc - 2)
                    emit_pv(NC_ - 2)
                    emit_pv(NC_ - 1)
                    s_.wait_ge(stt.h, stt.n)
                    ss.inc(s_.copy(den_row[0:1, :],
                                   PS[64:65, pb:pb + 2, :].rearrange("p a t -> p (a t)")))
                    v_.wait_ge(ss.h, ss.n)
                    sv.inc(v_.reciprocal(den_row[0:1, :], den_row[0:1, :]))
                    s_.wait_ge(sv.h, sv.n)
                    ss.inc(s_.copy(u_sb[:, :],
                                   PS[0:64, pb:pb + 2, :].rearrange("p a t -> p (a t)")))
                    t_.wait_ge(ss.h, ss.n)
                    stt.inc(t_.matmul(PS[0:64, pb, :], ones_f1[0:1, 0:64],
                                      den_row[0:1, 0:512], start=True, stop=True))
                    stt.inc(t_.matmul(PS[0:64, pb + 1, :], ones_f1[0:1, 0:64],
                                      den_row[0:1, 512:1024], start=True, stop=True))
                    v_.wait_ge(stt.h, stt.n)
                    sv.inc(v_.tensor_tensor(u_sb[:, :], u_sb[:, :],
                                            PS[0:64, pb:pb + 2, :].rearrange("p a t -> p (a t)"),
                                            op=ALU.mult))
                    ep_state[w % 2] = sv.n
                    s_.wait_ge(sv.h, sv.n)
                    p0 = (h % 2) * 64
                    ss.inc(s_.activation(y_pair[p0:p0 + 64, wl], u_sb[:, :], AF.Identity,
                                         bias=bias_slice(bv_name, h // 2, p0, p0 + 64)))
                qread[h % 2] = stt.n
                if h % 2 == 1:
                    proj_pair(bo_name, h // 2, h == 1, y_pair)
            barrier()

    def mla_attn():
        NC_ = ctx.NT
        NW = T // 1024
        wt_to_scr("L0_w_proj")
        with contextlib.ExitStack() as ph:
            ckvT = ph.enter_context(sbt("ckvT", [65, T], BF16))
            ckv_tok = ph.enter_context(sbt("ckv_tok", [128, NC_, 66], BF16))
            wvuT = ph.enter_context(sbt("wvuT", [65, C], BF16))
            bkv_row = ph.enter_context(sbt("bkv_row", [1, 64], F32))
            tok_tmp = ph.enter_context(sbt("tok_tmp", [128, 64], F32))
            with contextlib.ExitStack() as phw:
                wvuT_tmp = phw.enter_context(sbt("wvuT_tmp", [128, 1, C], BF16))
                wt_prep(din["L0_wv_u"], C, KV, wvuT_tmp, scale=1.0)
                ss.inc(s_.drain())
                ss.inc(s_.copy(wvuT[0:64, :], wvuT_tmp[0:64, 0, :]))
                barrier()
            sg.inc(g_.dma_start(wvuT[64:65, :],
                                din["L0_bv_u"].rearrange("(o n) -> o n", o=1)), 16)
            sc_.inc(sy_.dma_start(bkv_row[0:1, :],
                                  din["L0_bkv_d"].rearrange("(o n) -> o n", o=1)), 16)
            v_.memset(ckv_tok[:, :, 64:65], 1.0)
            v_.memset(ckvT[64:65, :], 1.0)
            sv.inc(v_.drain())
            barrier()
            with contextlib.ExitStack() as ph2:
                wqdT = ph2.enter_context(sbt("wqdT", [128, 8, CQ], BF16))
                wqkT = ph2.enter_context(sbt("wqkT", [128, 4, C], BF16))
                wkvdT = ph2.enter_context(sbt("wkvdT", [128, 8, KV], BF16))
                xls = ph2.enter_context(sbt("xls_mla", [128, 8, 512], BF16))
                cqc = ph2.enter_context(sbt("cqc", [128, 4, 512], BF16))
                stq = ph2.enter_context(sbt("stq", [128, 8, 512], BF16))
                wt_prep(din["L0_wq_d"], CQ, C, wqdT, scale=1.0)
                wt_prep(din["L0_w_qk"], C, CQ, wqkT, scale=0.125)
                wt_prep(din["L0_wkv_d"], KV, C, wkvdT, scale=1.0)
                barrier()
                evac_done = [0, 0, 0, 0]
                t_.wait_ge(sc_.h, sc_.n)
                stt.inc(t_.matmul(PS[:, 5, 0:64], ones_f1[0:1, :], bkv_row[0:1, :],
                                  start=True, stop=True))
                for tsub in range(NS):
                    tl = slice(tsub * 512, (tsub + 1) * 512)
                    sy_.wait_ge(stt.h, stt.n)
                    src = scr_xln.rearrange("(k p) t -> p k t", p=128)[:, :, tl]
                    sd.inc(sy_.dma_start(xls[:, :, :], src), 16)
                    t_.wait_ge(sd.h, sd.n)
                    s_.wait_ge(sd.h, sd.n)
                    for co in range(4):
                        b = co % 2
                        t_.wait_ge(ss.h, evac_done[b])
                        for ci in range(8):
                            ins = t_.matmul(bank(b), wqdT[:, ci, co * 128:(co + 1) * 128],
                                            xls[:, ci, :], start=(ci == 0), stop=(ci == 7))
                        stt.inc(ins)
                        s_.wait_ge(stt.h, stt.n)
                        ss.inc(s_.activation(cqc[:, co, :], bank(b), AF.Identity,
                                             bias=bias_slice("L0_bq_d", co)))
                        evac_done[b] = ss.n
                    t_.wait_ge(ss.h, ss.n)
                    for co in range(8):
                        b = co % 2
                        t_.wait_ge(ss.h, evac_done[b])
                        for ci in range(4):
                            ins = t_.matmul(bank(b), wqkT[:, ci, co * 128:(co + 1) * 128],
                                            cqc[:, ci, :], start=(ci == 0), stop=(ci == 3))
                        stt.inc(ins)
                        s_.wait_ge(stt.h, stt.n)
                        ss.inc(s_.activation(stq[:, co, :], bank(b), AF.Identity,
                                             bias=bias_slice("L0_b_qk_s", co)))
                        evac_done[b] = ss.n
                    sy_.wait_ge(ss.h, ss.n)
                    dstq = scr_q.rearrange("(k p) t -> p k t", p=128)[:, :, tl]
                    sd.inc(sy_.dma_start(dstq, stq[:, :, :]), 16)
                    b = 2
                    t_.wait_ge(ss.h, evac_done[b])
                    for ci in range(8):
                        ins = t_.matmul(bank(b, 1, 0, 64), wkvdT[:, ci, 0:64],
                                        xls[:, ci, :], start=(ci == 0), stop=(ci == 7))
                    stt.inc(ins)
                    s_.wait_ge(stt.h, stt.n)
                    ss.inc(s_.activation(ckvT[0:64, tl], bank(b, 1, 0, 64), AF.Identity,
                                         bias=bias_slice("L0_bkv_d", 0, 0, 64)))
                    evac_done[b] = ss.n
                    for tci in range(4):
                        sc = tsub * 4 + tci
                        b = 3
                        t_.wait_ge(sv.h, sv.n)
                        t_.wait_ge(ss.h, evac_done[b])
                        for ci in range(8):
                            ins = t_.matmul(PS[:, b, 0:64],
                                            xls[:, ci, tci * 128:(tci + 1) * 128],
                                            wkvdT[:, ci, 0:64],
                                            start=(ci == 0), stop=(ci == 7))
                        stt.inc(ins)
                        s_.wait_ge(stt.h, stt.n)
                        s_.wait_ge(sv.h, sv.n)
                        ss.inc(s_.copy(tok_tmp[:, :], PS[:, b, 0:64]))
                        evac_done[b] = ss.n
                        v_.wait_ge(ss.h, ss.n)
                        sv.inc(v_.tensor_tensor(ckv_tok[:, sc, 0:64], tok_tmp[:, :],
                                                PS[:, 5, 0:64], op=ALU.add))
                barrier()
            ep_state = {0: 0, 1: 0}
            ep_ss = {0: 0, 1: 0}
            ph3 = contextlib.ExitStack()
            qaug2 = [ph3.enter_context(sbt(f"qaug0_{i}", [65, T], BF16))
                     for i in range(2)]
            e_sb = [ph3.enter_context(sbt(f"e0_sb{i}", [128, 1024], BF16))
                    for i in range(4)]
            qk_tmp = ph3.enter_context(sbt("qk_tmp0", [64, 2048], BF16))
            den_row = ph3.enter_context(sbt("den_row0", [1, 1024], F32))
            u_sb = ph3.enter_context(sbt("u_sb0", [65, 1024], F32))
            z_bf = ph3.enter_context(sbt("z_bf", [65, 1024], BF16))
            y_pair = ph3.enter_context(sbt("y_pair0", [128, T], BF16))
            exp_done = {0: 0, 2: 0}
            pvrd = {0: 0, 1: 0, 2: 0, 3: 0}
            qread = [0, 0]
            qdma = [0, 0]
            anch_mm = [0]
            for h in range(NH):
                qaug = qaug2[h % 2]
                if h == 0:
                    sy_.wait_ge(sq_.h, sq_.n)
                    sq_.inc(sy_.dma_start(qaug[0:64, :], scr_q[0:64, :]), 16)
                    qdma[0] = sq_.n
                if h + 1 < NH:
                    sy_.wait_ge(stt.h, qread[(h + 1) % 2])
                    sy_.wait_ge(sq_.h, sq_.n)
                    sq_.inc(sy_.dma_start(qaug2[(h + 1) % 2][0:64, :],
                                          scr_q[(h + 1) * 64:(h + 2) * 64, :]), 16)
                    qdma[(h + 1) % 2] = sq_.n
                for hf2 in range(max(1, T // 2048)):
                    hl = slice(hf2 * 2048, min((hf2 + 1) * 2048, T))
                    v_.wait_ge(sq_.h, qdma[h % 2])
                    v_.wait_ge(stt.h, anch_mm[0])
                    sv.inc(v_.tensor_tensor(qk_tmp[:, :], qaug[0:64, hl], ckvT[0:64, hl],
                                            op=ALU.mult))
                    t_.wait_ge(sv.h, sv.n)
                    if hf2 == 0:
                        t_.wait_ge(ss.h, ss.n)
                    for j4 in range(4):
                        stt.inc(t_.matmul(PS[0:1, 4 + (j4 % 2) * 2 + j4 // 2, :],
                                          negones64[:, :],
                                          qk_tmp[:, j4 * 512:(j4 + 1) * 512],
                                          start=True, stop=True))
                    anch_mm[0] = stt.n
                    s_.wait_ge(stt.h, stt.n)
                    for j4 in range(4):
                        ins = s_.activation(qaug[64:65, hf2 * 2048 + j4 * 512:
                                                 hf2 * 2048 + (j4 + 1) * 512],
                                            PS[0:1, 4 + (j4 % 2) * 2 + j4 // 2, :],
                                            AF.Identity, bias=consts[0:1, 0:1])
                    ss.inc(ins)
                    t_.wait_ge(ss.h, ss.n)
                for w in range(NW):
                    wl = slice(w * 1024, (w + 1) * 1024)

                    def emit_qk(sc):
                        sb_ = (sc % 2) * 2
                        t_.wait_ge(ss.h, exp_done[sb_])
                        if sc < 2 and w == 0:
                            t_.wait_ge(sv.h, xadd_state["done"])
                        for half in range(2):
                            ins = t_.matmul(PS[:, sb_ + half, :],
                                            ckvT[:, sc * 128:(sc + 1) * 128],
                                            qaug[:, w * 1024 + half * 512:
                                                 w * 1024 + (half + 1) * 512],
                                            start=True, stop=True)
                        stt.inc(ins)
                        qk_done[sc] = stt.n

                    def emit_exp(sc):
                        sb_ = (sc % 2) * 2
                        s_.wait_ge(stt.h, qk_done[sc])
                        s_.wait_ge(stt.h, pvrd[sc % 4])
                        ss.inc(s_.activation(e_sb[sc % 4][:, :],
                                             bank(sb_, 2).rearrange("p a t -> p (a t)"),
                                             AF.Exp))
                        exp_done[sb_] = ss.n
                        ex_done[sc] = ss.n

                    pb = 4 + (w % 2) * 2

                    def emit_pv(sc):
                        t_.wait_ge(ss.h, ex_done[sc])
                        if sc == 0:
                            t_.wait_ge(sv.h, ep_state[w % 2])
                            t_.wait_ge(ss.h, ep_ss[w % 2])
                        for half in range(2):
                            ins = t_.matmul(PS[0:65, pb + half, :],
                                            ckv_tok[:, sc, 0:65],
                                            e_sb[sc % 4][:, half * 512:(half + 1) * 512],
                                            start=(sc == 0), stop=(sc == NC_ - 1))
                        stt.inc(ins)
                        pvrd[sc % 4] = stt.n

                    qk_done, ex_done = {}, {}
                    emit_qk(0)
                    emit_exp(0)
                    emit_qk(1)
                    emit_exp(1)
                    for sc in range(2, NC_):
                        emit_qk(sc)
                        emit_exp(sc)
                        emit_pv(sc - 2)
                    emit_pv(NC_ - 2)
                    emit_pv(NC_ - 1)
                    s_.wait_ge(stt.h, stt.n)
                    ss.inc(s_.copy(den_row[0:1, :],
                                   PS[64:65, pb:pb + 2, :].rearrange("p a t -> p (a t)")))
                    v_.wait_ge(ss.h, ss.n)
                    sv.inc(v_.reciprocal(den_row[0:1, :], den_row[0:1, :]))
                    s_.wait_ge(sv.h, sv.n)
                    ss.inc(s_.copy(u_sb[:, :],
                                   PS[0:65, pb:pb + 2, :].rearrange("p a t -> p (a t)")))
                    t_.wait_ge(ss.h, ss.n)
                    stt.inc(t_.matmul(PS[0:65, pb, :], ones_f1[0:1, 0:65],
                                      den_row[0:1, 0:512], start=True, stop=True))
                    stt.inc(t_.matmul(PS[0:65, pb + 1, :], ones_f1[0:1, 0:65],
                                      den_row[0:1, 512:1024], start=True, stop=True))
                    v_.wait_ge(stt.h, stt.n)
                    sv.inc(v_.tensor_tensor(z_bf[:, :], u_sb[:, :],
                                            PS[0:65, pb:pb + 2, :].rearrange("p a t -> p (a t)"),
                                            op=ALU.mult))
                    ep_state[w % 2] = sv.n
                    t_.wait_ge(sv.h, sv.n)
                    for half in range(2):
                        stt.inc(t_.matmul(PS[0:64, pb + half, :],
                                          wvuT[:, h * 64:(h + 1) * 64],
                                          z_bf[:, half * 512:(half + 1) * 512],
                                          start=True, stop=True))
                    s_.wait_ge(stt.h, stt.n)
                    p0 = (h % 2) * 64
                    ss.inc(s_.copy(y_pair[p0:p0 + 64, wl],
                                   PS[0:64, pb:pb + 2, :].rearrange("p a t -> p (a t)")))
                    ep_ss[w % 2] = ss.n
                qread[h % 2] = stt.n
                if h % 2 == 1:
                    proj_pair("L0_b_proj", h // 2, h == 1, y_pair)
            barrier()
            ph3.close()


    _qwp_id = [0]

    def quant_weight_prep(w_ap, rows_n, cols_n, scr_dst, isw_col):
        nr = rows_n // 128
        _qwp_id[0] += 1
        u = str(_qwp_id[0])
        with contextlib.ExitStack() as ph:
            wch = ph.enter_context(sbt("wch" + u, [128, cols_n], F32))
            qch = ph.enter_context(sbt("qch" + u, [128, cols_n], I32))
            parts = ph.enter_context(sbt("parts" + u, [128, nr + 1], F32))
            sca = ph.enter_context(sbt("sca" + u, [1, 4], F32))
            swb = ph.enter_context(sbt("swb" + u, [128, 2], F32))
            tstg = ph.enter_context(sbt("tstg" + u, [128, cols_n // 128, 128], BF16))
            for rr in range(nr):
                sy_.wait_ge(sv.h, sv.n)
                sd.inc(sy_.dma_start(wch[:, :], w_ap[rr * 128:(rr + 1) * 128, :]), 16)
                v_.wait_ge(sd.h, sd.n)
                sv.inc(v_.reduce_sum(parts[:, rr:rr + 1], wch[:, :], axis=AX.X,
                                     apply_absolute_value=True))
            sv.inc(v_.drain())
            sv.inc(v_.reduce_sum(parts[:, nr:nr + 1], parts[:, 0:nr], axis=AX.X))
            sv.inc(v_.drain())
            sv.inc(v_.tensor_copy(wch[:, 0:1], parts[:, nr:nr + 1]))
            sv.inc(v_.drain())
            t_.wait_ge(sv.h, sv.n)
            stt.inc(t_.matmul(PS[0:1, 6, 0:1], ones_c128[:, :], wch[:, 0:1],
                              start=True, stop=True))
            v_.wait_ge(stt.h, stt.n)
            sv.inc(v_.tensor_scalar(sca[0:1, 0:1], PS[0:1, 6, 0:1],
                                    1.0 / (rows_n * cols_n), 1e-5,
                                    op0=ALU.mult, op1=ALU.max))
            sv.inc(v_.drain())
            sv.inc(v_.reciprocal(sca[0:1, 1:2], sca[0:1, 0:1]))
            sv.inc(v_.drain())
            t_.wait_ge(sv.h, sv.n)
            stt.inc(t_.matmul(PS[0:128, 6, 0:1], ones_f1[0:1, :], sca[0:1, 1:2],
                              start=True, stop=True))
            stt.inc(t_.matmul(PS[0:128, 6, 1:2], ones_f1[0:1, :], sca[0:1, 0:1],
                              start=True, stop=True))
            v_.wait_ge(stt.h, stt.n)
            sv.inc(v_.tensor_copy(swb[:, 0:2], PS[0:128, 6, 0:2]))
            sv.inc(v_.drain())
            sv.inc(v_.tensor_copy(bias_pool[:, isw_col:isw_col + 1], swb[:, 1:2]))
            sv.inc(v_.drain())
            for rr in range(nr):
                sy_.wait_ge(sv.h, sv.n)
                sy_.wait_ge(stt.h, stt.n)
                sd.inc(sy_.dma_start(wch[:, :], w_ap[rr * 128:(rr + 1) * 128, :]), 16)
                v_.wait_ge(sd.h, sd.n)
                v_.wait_ge(stt.h, stt.n)
                sv.inc(v_.tensor_scalar(qch[:, :], wch[:, :], swb[:, 0:1], None,
                                        op0=ALU.mult))
                sv.inc(v_.drain())
                sv.inc(v_.tensor_scalar(wch[:, :], qch[:, :], 1.0, -1.0,
                                        op0=ALU.min, op1=ALU.max))
                sv.inc(v_.drain())
                t_.wait_ge(sv.h, sv.n)
                t_.wait_ge(ss.h, ss.n)
                for jj in range(cols_n // 128):
                    b = jj % 2
                    off = jj // 2 % 2
                    tgt = PS[:, b, :].rearrange("p (o q) -> p o q", o=4)[:, off, :]
                    stt.inc(t_.transpose(tgt, wch[:, jj * 128:(jj + 1) * 128], ident_f[:, :]))
                    if jj % 4 == 3 or jj == cols_n // 128 - 1:
                        s_.wait_ge(stt.h, stt.n)
                        s_.wait_ge(sa.h, sa.n)
                        for jj2 in range(jj - jj % 4, jj + 1):
                            b2 = jj2 % 2
                            off2 = jj2 // 2 % 2
                            srcp = PS[:, b2, :].rearrange("p (o q) -> p o q", o=4)[:, off2, :]
                            ins = s_.activation(tstg[:, jj2, :], srcp, AF.Identity)
                        ss.inc(ins)
                        t_.wait_ge(ss.h, ss.n)
                sy_ps = nc.scalar
                sy_ps.wait_ge(ss.h, ss.n)
                dst = scr_dst.rearrange("(j p) f -> p j f", p=128)[:, :, rr * 128:(rr + 1) * 128]
                sa.inc(sy_ps.dma_start(dst, tstg[:, :, :]), 16)
            barrier()

    def f0_ffn():
        NG = T // 512
        scr_xres = nc.dram_tensor("scr_xres", [C, T], F32, kind="Internal").ap()
        with nc.named_scope("fqw"):
            quant_weight_prep(din["F0_w1"], FF, C, scr_qw1, 250)
            quant_weight_prep(din["F0_w2"], C, FF, scr_qw2, 251)
        barrier()
        sd.inc(sy_.dma_start(scr_xres.rearrange("(k p) t -> p k t", p=128), XT[:, :, :]), 16)
        _f0st = contextlib.ExitStack()
        xb = XT.bitcast(BF16).reshape([128, 8 * T * 2])
        nqw, nqx, nfr = 8 * FF, 32 * 512, 3 * FF
        if 8 * T * 2 >= nqw + nqx + nfr:
            qw1v = xb[:, 0:nqw].rearrange("p (a f) -> p a f", a=8)
            qx2T = xb[:, nqw:nqw + nqx].rearrange("p (a f) -> p a f", a=32)
            frow = xb[0:1, nqw + nqx:nqw + nqx + nfr].rearrange("p (a f) -> p a f", a=3)
        elif 8 * T * 2 >= nqw:
            qw1v = xb[:, 0:nqw].rearrange("p (a f) -> p a f", a=8)
            qx2T = _f0st.enter_context(sbt("qx2T_r", [128, 32, 512], BF16))[:, :, :]
            frow = _f0st.enter_context(sbt("frow_r", [1, 3, FF], BF16))[:, :, :]
        else:
            qw1v = _f0st.enter_context(sbt("qw1v_r", [128, 8, FF], BF16))[:, :, :]
            qx2T = _f0st.enter_context(sbt("qx2T_r", [128, 32, 512], BF16))[:, :, :]
            frow = _f0st.enter_context(sbt("frow_r", [1, 3, FF], BF16))[:, :, :]
        sy_.wait_ge(sd.h, sd.n)
        sd.inc(sy_.dma_start(qw1v[:, :, :],
                             scr_qw1.rearrange("(k p) f -> p k f", p=128)), 16)
        g_.wait_ge(sd.h, sd.n - 16)
        sg.inc(g_.dma_start(frow[0:1, 0, :], din["F0_b1"].rearrange("(o n) -> o n", o=1)), 16)
        sg.inc(g_.dma_start(frow[0:1, 1, :], din["F0_ln2_g"].rearrange("(o n) -> o n", o=1)), 16)
        sg.inc(g_.dma_start(frow[0:1, 2, :], din["F0_ln2_b"].rearrange("(o n) -> o n", o=1)), 16)
        barrier()
        with contextlib.ExitStack() as ph:
            qxT = ph.enter_context(sbt("qxT", [128, 8, 512], BF16))
            xls = ph.enter_context(sbt("xls_f0", [128, 8, 512], BF16))
            gb = ph.enter_context(sbt("gb", [128, FF], F32))
            t0b = ph.enter_context(sbt("t0b", [128, 512], F32))
            t0c = ph.enter_context(sbt("t0c", [128, 512], F32))
            t0x = [t0b, t0c]
            y2b = ph.enter_context(sbt("y2b", [128, 8, 512], F16))
            qib = ph.enter_context(sbt("qib", [128, 512], I32))
            qib2 = ph.enter_context(sbt("qib2", [128, 512], I32))
            cols = ph.enter_context(sbt("cols", [128, 24], F32))
            pool1 = ph.enter_context(sbt("pool1", [128, 32], F32))
            cs2 = [ph.enter_context(sbt(f"cslab{i}", [128, 8, 128], BF16))
                   for i in range(2)]
            xg = ph.enter_context(sbt("xg", [128, 4, 512], F32))
            _scC = [None]
            pstate = {"evd": {}, "bad": {}, "qtt": {}}
            csd = {0: 0, 1: 0}
            csr = {0: 0, 1: 0}

            def cs_issue(qi):
                cc2, hq2 = qi // 4, qi % 4
                q_, qs = (g_, sg) if qi % 2 == 0 else (nc.scalar, sa)
                q_.wait_ge(stt.h, csr[qi % 2])
                q_.wait_ge(qs.h, qs.n)
                qs.inc(q_.dma_start(
                    cs2[qi % 2][:, :, :],
                    scr_qw2.rearrange("(j p) c -> p j c", p=128)
                    [:, hq2 * 8:hq2 * 8 + 8, cc2 * 128:(cc2 + 1) * 128]), 16)
                csd[qi % 2] = qs.n
            for g0 in range(NG):
                gl = slice(g0 * 512, (g0 + 1) * 512)
                sy_.wait_ge(stt.h, stt.n)
                sy_.wait_ge(sv.h, sv.n)
                sy_.wait_ge(sd.h, sd.n)
                srcx = scr_xln.rearrange("(k p) t -> p k t", p=128)[:, :, gl]
                sd.inc(sy_.dma_start(xls[:, :, :], srcx), 16)
                if _scC[0] is not None:
                    _scC[0].__exit__(None, None, None)
                    _scC[0] = None
                _scA = nc.named_scope("fA"); _scA.__enter__()
                v_.wait_ge(sd.h, sd.n)
                for cc in range(8):
                    v_.wait_ge(stt.h, stt.n)
                    sv.inc(v_.tensor_tensor(t0b[:, :], xls[:, cc, :], xls[:, cc, :],
                                            op=ALU.mult))
                    t_.wait_ge(sv.h, sv.n)
                    stt.inc(t_.matmul(PS[0:1, 6, :], ones_c128b[:, :], xls[:, cc, :],
                                      start=(cc == 0), stop=(cc == 7)))
                    stt.inc(t_.matmul(PS[0:1, 7, :], ones_c128[:, :], t0b[:, :],
                                      start=(cc == 0), stop=(cc == 7)))
                v_.wait_ge(stt.h, stt.n)
                v_.tensor_scalar(rows[:, 0, :], PS[0:1, 6, :], 1.0 / C, None, op0=ALU.mult)
                sv.inc(v_.tensor_scalar(rows[:, 1, :], PS[0:1, 7, :], 1.0 / C, None,
                                        op0=ALU.mult))
                sv.inc(v_.drain())
                sv.inc(v_.tensor_tensor(rows[:, 2, :], rows[:, 0, :], rows[:, 0, :],
                                        op=ALU.mult))
                sv.inc(v_.drain())
                sv.inc(v_.tensor_tensor(rows[:, 1, :], rows[:, 1, :], rows[:, 2, :],
                                        op=ALU.subtract))
                sv.inc(v_.drain())
                sv.inc(v_.tensor_scalar(rows[:, 1, :], rows[:, 1, :], LN_EPS, None,
                                        op0=ALU.add))
                sv.inc(v_.drain())
                s_.wait_ge(sv.h, sv.n)
                ss.inc(s_.activation(rows[:, 2, :], rows[:, 1, :], AF.Ln))
                ss.inc(s_.drain())
                ss.inc(s_.activation(rows[:, 1, :], rows[:, 2, :], AF.Exp, scale=-0.5))
                ss.inc(s_.drain())
                v_.wait_ge(ss.h, ss.n)
                sv.inc(v_.tensor_tensor(rows[:, 2, :], rows[:, 0, :], rows[:, 1, :],
                                        op=ALU.mult))
                sv.inc(v_.drain())
                t_.wait_ge(sv.h, sv.n)
                stt.inc(t_.matmul(PS[:, 2, :], ones_f1[:, :], rows[:, 1, :],
                                  start=True, stop=True))
                stt.inc(t_.matmul(PS[:, 3, :], ones_f1[:, :], rows[:, 2, :],
                                  start=True, stop=True))
                v_.wait_ge(stt.h, stt.n)
                for cc in range(8):
                    sv.inc(v_.tensor_tensor(y2b[:, cc, :], xls[:, cc, :], PS[:, 2, :],
                                            op=ALU.mult))
                sv.inc(v_.drain())
                for cc in range(8):
                    sv.inc(v_.tensor_tensor(y2b[:, cc, :], y2b[:, cc, :], PS[:, 3, :],
                                            op=ALU.subtract))
                sv.inc(v_.drain())
                s_.wait_ge(sv.h, sv.n)
                for cc in range(8):
                    ins = s_.activation(y2b[:, cc, :], y2b[:, cc, :], AF.Identity,
                                        scale=bias_slice("F0_ln1_g", cc),
                                        bias=bias_slice("F0_ln1_b", cc))
                    ss.inc(ins)
                t_.wait_ge(ss.h, ss.n)
                for tc in range(4):
                    for cc in range(8):
                        b = 4 + cc % 2
                        off = cc // 2 % 4
                        tgt = PS16[:, b, :].rearrange("p (o q) -> p o q", o=8)[:, off, :]
                        stt.inc(t_.transpose(tgt, y2b[:, cc, tc * 128:(tc + 1) * 128],
                                             ident_h[:, :]))
                        if cc % 2 == 1:
                            v_.wait_ge(stt.h, stt.n)
                            for c2 in (cc - 1, cc):
                                b2 = 4 + c2 % 2
                                off2 = c2 // 2 % 4
                                sp = PS16[:, b2, :].rearrange("p (o q) -> p o q", o=8)[:, off2, :]
                                ins = v_.reduce_max(pool1[:, tc * 8 + c2:tc * 8 + c2 + 1],
                                                    sp, axis=AX.X, apply_absolute_value=True)
                            sv.inc(ins)
                            t_.wait_ge(sv.h, sv.n)
                sv.inc(v_.drain())
                for tc in range(4):
                    sv.inc(v_.reduce_max(cols[:, tc:tc + 1], pool1[:, tc * 8:(tc + 1) * 8],
                                         axis=AX.X))
                sv.inc(v_.drain())
                sv.inc(v_.tensor_scalar(cols[:, 0:4], cols[:, 0:4], 1e-5, None, op0=ALU.max))
                sv.inc(v_.drain())
                sv.inc(v_.reciprocal(cols[:, 4:8], cols[:, 0:4]))
                sv.inc(v_.drain())
                sv.inc(v_.tensor_scalar(cols[:, 4:8], cols[:, 4:8], 127.0, None, op0=ALU.mult))
                sv.inc(v_.tensor_scalar(cols[:, 8:12], cols[:, 0:4], 1.0 / 127.0, None,
                                        op0=ALU.mult))
                sv.inc(v_.drain())
                sv.inc(v_.tensor_scalar(cols[:, 8:12], cols[:, 8:12],
                                        bias_pool[:, 250:251], None, op0=ALU.mult))
                sv.inc(v_.drain())
                t_.wait_ge(sv.h, sv.n)
                for tc in range(4):
                    stt.inc(t_.transpose(PS[0:1, 6, tc * 128:(tc + 1) * 128],
                                         cols[:, 4 + tc:5 + tc], ident_f[:, :]))
                s_.wait_ge(stt.h, stt.n)
                ins = s_.copy(rows[0:1, 3, :], PS[0:1, 6, :])
                ss.inc(ins)
                t_.wait_ge(ss.h, ss.n)
                stt.inc(t_.matmul(PS[:, 7, :], ones_f1[:, :], rows[:, 3, :],
                                  start=True, stop=True))
                v_.wait_ge(stt.h, stt.n)
                for cc in range(8):
                    sv.inc(v_.tensor_tensor(y2b[:, cc, :], y2b[:, cc, :], PS[:, 7, :],
                                            op=ALU.mult))
                sv.inc(v_.drain())
                qrd = {0: 0, 1: 0}
                for cc in range(8):
                    qb = qib if cc % 2 == 0 else qib2
                    s_.wait_ge(sv.h, sv.n if cc < 2 else qrd[cc % 2])
                    ss.inc(s_.copy(qb[:, :], y2b[:, cc, :]))
                    v_.wait_ge(ss.h, ss.n)
                    sv.inc(v_.tensor_copy(qxT[:, cc, :], qb[:, :]))
                    qrd[cc % 2] = sv.n
                _scA.__exit__(None, None, None)
                _scB = nc.named_scope("fB"); _scB.__enter__()
                sy_.wait_ge(sd.h, sd.n)
                for tc in range(4):
                    tcl = slice(tc * 128, (tc + 1) * 128)
                    pevd, pbad, pqtt = pstate["evd"], pstate["bad"], pstate["qtt"]
                    mmd, evd, bad, gld, qtt = {}, {}, {}, {-1: 0, -2: 0}, {}

                    def b1_mm(f5):
                        b = f5 % 2
                        fl = slice(f5 * 512, (f5 + 1) * 512)
                        if f5 < 2:
                            if tc == 0:
                                t_.wait_ge(ss.h, ss.n)
                                t_.wait_ge(sv.h, sv.n)
                            else:
                                t_.wait_ge(ss.h, pevd.get(6 + f5, 0))
                                t_.wait_ge(sv.h, pbad.get(6 + f5, 0))
                        else:
                            t_.wait_ge(sv.h, bad.get(f5 - 2, 0))
                            t_.wait_ge(ss.h, evd.get(f5 - 2, 0))
                        stt.inc(t_.matmul(PS[:, 4 + b, :], ones_b1[:, :], frow[0:1, 0, fl],
                                          start=True, stop=True))
                        for ci in range(8):
                            ins = t_.matmul(bank(b), qxT[:, ci, tcl], qw1v[:, ci, fl],
                                            start=(ci == 0), stop=(ci == 7))
                        stt.inc(ins)
                        mmd[f5] = stt.n

                    def b1_evac(f5):
                        b = f5 % 2
                        s_.wait_ge(stt.h, mmd[f5])
                        s_.wait_ge(sv.h, bad.get(f5 - 1, 0))
                        ss.inc(s_.activation(t0x[b][:, :], bank(b), AF.Identity,
                                             scale=cols[:, 8 + tc:9 + tc]))
                        evd[f5] = ss.n

                    def b1_badd(f5):
                        b = f5 % 2
                        v_.wait_ge(ss.h, evd[f5])
                        sv.inc(v_.tensor_tensor(t0x[b][:, :], t0x[b][:, :], PS[:, 4 + b, :],
                                                op=ALU.add))
                        bad[f5] = sv.n

                    def b1_gelu(f5):
                        b = f5 % 2
                        fl = slice(f5 * 512, (f5 + 1) * 512)
                        s_.wait_ge(sv.h, bad[f5])
                        s_.wait_ge(sv.h, pqtt.get(f5, 0))
                        ss.inc(s_.activation(gb[:, fl], t0x[b][:, :], AF.Gelu))
                        gld[f5] = ss.n

                    b1_mm(0); b1_evac(0)
                    for f5 in range(1, 8):
                        b1_mm(f5); b1_badd(f5 - 1); b1_gelu(f5 - 1); b1_evac(f5)
                    b1_badd(7); b1_gelu(7)
                    v_.wait_ge(ss.h, ss.n)
                    sv.inc(v_.reduce_sum(cols[:, 16:17], gb[:, :], axis=AX.X))
                    red = {0: sv.n, 1: sv.n}
                    sqd = {0: 0, 1: 0}
                    for f5 in range(8):
                        fl = slice(f5 * 512, (f5 + 1) * 512)
                        s_.wait_ge(sv.h, red[f5 % 2])
                        ss.inc(s_.activation(t0x[f5 % 2][:, :], gb[:, fl], AF.Square))
                        sqd[f5 % 2] = ss.n
                        v_.wait_ge(ss.h, sqd[f5 % 2])
                        sv.inc(v_.reduce_sum(pool1[:, f5:f5 + 1], t0x[f5 % 2][:, :],
                                             axis=AX.X))
                        red[f5 % 2] = sv.n
                    sv.inc(v_.drain())
                    sv.inc(v_.reduce_sum(cols[:, 17:18], pool1[:, 0:8], axis=AX.X))
                    s_.wait_ge(sv.h, sv.n)
                    ss.inc(s_.activation(cols[:, 19:20], cols[:, 17:18], AF.Identity,
                                         scale=1.0 / FF, bias=cc_eps[:, 0:1]))
                    sv.inc(v_.tensor_scalar(cols[:, 18:19], cols[:, 16:17], 1.0 / FF, None,
                                            op0=ALU.mult))
                    s_.wait_ge(sv.h, sv.n)
                    ss.inc(s_.activation(cols[:, 20:21], cols[:, 18:19], AF.Square))
                    v_.wait_ge(ss.h, ss.n)
                    sv.inc(v_.tensor_tensor(cols[:, 19:20], cols[:, 19:20], cols[:, 20:21],
                                            op=ALU.subtract))
                    s_.wait_ge(sv.h, sv.n)
                    ss.inc(s_.activation(cols[:, 20:21], cols[:, 19:20], AF.Ln))
                    ss.inc(s_.drain())
                    ss.inc(s_.activation(cols[:, 20:21], cols[:, 20:21], AF.Exp, scale=-0.5))
                    v_.wait_ge(ss.h, ss.n)
                    sv.inc(v_.scalar_tensor_tensor(cols[:, 21:22], cols[:, 18:19], -1.0,
                                                   cols[:, 20:21], op0=ALU.mult,
                                                   op1=ALU.mult))
                    s_.wait_ge(sv.h, sv.n)
                    ss.inc(s_.activation(gb[:, :], gb[:, :], AF.Identity,
                                         scale=cols[:, 20:21], bias=cols[:, 21:22]))
                    apd = {0: 0, 1: 0}
                    for f5 in range(8):
                        fl = slice(f5 * 512, (f5 + 1) * 512)
                        pA = 2 if f5 % 2 == 0 else 6
                        t_.wait_ge(sv.h, apd[f5 % 2])
                        if f5 == 0:
                            t_.wait_ge(ss.h, ss.n)
                        stt.inc(t_.matmul(PS[:, pA, :], ones_b1[:, :], frow[0:1, 1, fl],
                                          start=True, stop=True))
                        stt.inc(t_.matmul(PS[:, pA + 1, :], ones_b1[:, :], frow[0:1, 2, fl],
                                          start=True, stop=True))
                        v_.wait_ge(stt.h, stt.n)
                        sv.inc(v_.tensor_tensor(gb[:, fl], gb[:, fl], PS[:, pA, :],
                                                op=ALU.mult))
                        sv.inc(v_.drain())
                        sv.inc(v_.tensor_tensor(gb[:, fl], gb[:, fl], PS[:, pA + 1, :],
                                                op=ALU.add))
                        apd[f5 % 2] = sv.n
                    sv.inc(v_.drain())
                    sv.inc(v_.reduce_max(cols[:, 13:14], gb[:, :], axis=AX.X,
                                         apply_absolute_value=True))
                    s_.wait_ge(sv.h, sv.n)
                    ss.inc(s_.activation(cols[:, 22:23], cols[:, 13:14], AF.Identity,
                                         scale=1.0, bias=0.0))
                    v_.wait_ge(ss.h, ss.n)
                    sv.inc(v_.tensor_scalar(cols[:, 13:14], cols[:, 22:23], 1e-5, None,
                                            op0=ALU.max))
                    sv.inc(v_.drain())
                    sv.inc(v_.reciprocal(cols[:, 14:15], cols[:, 13:14]))
                    s_.wait_ge(sv.h, sv.n)
                    ss.inc(s_.activation(cols[:, 14:15], cols[:, 14:15], AF.Identity,
                                         scale=127.0))
                    sv.inc(v_.tensor_scalar(cols[:, 23:24], cols[:, 13:14], 1.0 / 127.0,
                                            None, op0=ALU.mult))
                    sv.inc(v_.drain())
                    sv.inc(v_.tensor_scalar(cols[:, 4 + tc:5 + tc], cols[:, 23:24],
                                            bias_pool[:, 251:252], None, op0=ALU.mult))
                    v_.wait_ge(ss.h, ss.n)
                    qrd2 = {0: 0, 1: 0}
                    trd = {0: 0, 1: 0}
                    for f5 in range(8):
                        fl = slice(f5 * 512, (f5 + 1) * 512)
                        qb = qib if f5 % 2 == 0 else qib2
                        v_.wait_ge(ss.h, qrd2[f5 % 2])
                        sv.inc(v_.tensor_scalar(qb[:, :], gb[:, fl], cols[:, 14:15],
                                                None, op0=ALU.mult))
                        qtt[f5] = sv.n
                        s_.wait_ge(sv.h, sv.n)
                        s_.wait_ge(stt.h, trd[f5 % 2])
                        ss.inc(s_.copy(t0x[f5 % 2][:, :], qb[:, :]))
                        qrd2[f5 % 2] = ss.n
                        t_.wait_ge(ss.h, ss.n)
                        if f5 == 0:
                            t_.wait_ge(sv.h, sv.n)
                        for q4 in range(4):
                            bq = 2 + q4 % 2
                            off = q4 // 2
                            tgt = PS[:, bq, :].rearrange("p (o q) -> p o q", o=4)[:, off, :]
                            stt.inc(t_.transpose(tgt, t0x[f5 % 2][:, q4 * 128:(q4 + 1) * 128],
                                                 ident_f[:, :]))
                        trd[f5 % 2] = stt.n
                        s_.wait_ge(stt.h, stt.n)
                        for q4 in range(4):
                            bq = 2 + q4 % 2
                            off = q4 // 2
                            sp = PS[:, bq, :].rearrange("p (o q) -> p o q", o=4)[:, off, :]
                            ins = s_.activation(qx2T[:, f5 * 4 + q4, tcl], sp, AF.Identity)
                        ss.inc(ins)
                    pstate["evd"], pstate["bad"], pstate["qtt"] = evd, bad, qtt
                _scB.__exit__(None, None, None)
                _scC[0] = nc.named_scope("fC"); _scC[0].__enter__()
                t_.wait_ge(ss.h, ss.n)
                t_.wait_ge(sv.h, sv.n)
                for tc in range(4):
                    stt.inc(t_.transpose(PS[0:1, 6, tc * 128:(tc + 1) * 128],
                                         cols[:, 4 + tc:5 + tc], ident_f[:, :]))
                s_.wait_ge(stt.h, stt.n)
                ins = s_.copy(rows[0:1, 3, :], PS[0:1, 6, :])
                ss.inc(ins)
                t_.wait_ge(ss.h, ss.n)
                stt.inc(t_.matmul(PS[:, 3, :], ones_f1[:, :], rows[:, 3, :],
                                  start=True, stop=True))
                for hf in range(2):
                  sy_.wait_ge(sv.h, sv.n)
                  sy_.wait_ge(sd.h, sd.n)
                  sd.inc(sy_.dma_start(
                      xg[:, :, :],
                      scr_xres.rearrange("(k p) t -> p k t", p=128)[:, hf * 4:hf * 4 + 4, gl]),
                      16)
                  v_.wait_ge(sd.h, sd.n)
                  for cc in range(hf * 4, hf * 4 + 4):
                    b = cc % 2
                    for hq in range(4):
                        qi = cc * 4 + hq
                        if qi == 0:
                            cs_issue(0)
                        if qi + 1 < 32:
                            cs_issue(qi + 1)
                        t_.wait_ge((sg if qi % 2 == 0 else sa).h, csd[qi % 2])
                        if hq == 0:
                            t_.wait_ge(ss.h, ss.n)
                        for fc in range(8):
                            ins = t_.matmul(bank(b), cs2[qi % 2][:, fc, :],
                                            qx2T[:, hq * 8 + fc, :],
                                            start=(hq == 0 and fc == 0),
                                            stop=(hq == 3 and fc == 7))
                        stt.inc(ins)
                        csr[qi % 2] = stt.n
                    s_.wait_ge(stt.h, stt.n)
                    s_.wait_ge(sv.h, sv.n)
                    ss.inc(s_.activation(t0b[:, :], bank(b), AF.Identity))
                    v_.wait_ge(ss.h, ss.n)
                    sv.inc(v_.tensor_tensor(t0b[:, :], t0b[:, :], PS[:, 3, :], op=ALU.mult))
                    sv.inc(v_.drain())
                    sv.inc(v_.scalar_tensor_tensor(xg[:, cc - hf * 4, :], t0b[:, :],
                                                   bias_slice("F0_b2", cc), xg[:, cc - hf * 4, :],
                                                   op0=ALU.add, op1=ALU.add))
                    sv.inc(v_.drain())
                  sy_.wait_ge(sv.h, sv.n)
                  sd.inc(sy_.dma_start(
                      scr_xres.rearrange("(k p) t -> p k t", p=128)[:, hf * 4:hf * 4 + 4, gl],
                      xg[:, :, :]), 16)
            if _scC[0] is not None:
                _scC[0].__exit__(None, None, None)
                _scC[0] = None
            barrier()
            sd.inc(sy_.dma_start(XT[:, :, :],
                                 scr_xres.rearrange("(k p) t -> p k t", p=128)), 16)
            barrier()
        _f0st.close()

    def f1_ffn():
      with contextlib.ExitStack() as phx:
        b1g = phx.enter_context(sbt("b1g", [128, 24], F32))
        with contextlib.ExitStack() as ph:
            wp1 = ph.enter_context(sbt("wp1", [128, 8, NAV], F32))
            wp2 = ph.enter_context(sbt("wp2", [128, NE], F32))
            growA = ph.enter_context(sbt("growA", [1, NAV], F32))
            growB = ph.enter_context(sbt("growB", [1, NAV], F32))
            gcol = ph.enter_context(sbt("gcol", [128, 1], F32))
            lgrow = ph.enter_context(sbt("lgrow", [1, 16], F32))
            ohb = ph.enter_context(sbt("ohb", [128, NE], F32))
            de_f32 = ph.enter_context(sbt("de_f32", [128, NE, 128], F32))
            bsel = ph.enter_context(sbt("bsel", [8, FE + C], F32))
            webf = [ph.enter_context(sbt(f"webf{i}", [128, FE], F32))
                    for i in range(2)]
            wst = ph.enter_context(sbt("wst", [128, 4, 512], BF16))
            rowtmp = ph.enter_context(sbt("rowtmp", [1, 2, 512], F32))
            sc_.inc(sy_.dma_start(wp1[:, :, :],
                                  din["F1_Wp1"].rearrange("(k p) n -> p k n", p=128)), 16)
            sc_.inc(sy_.dma_start(wp2[0:NAV, :], din["F1_Wp2"]), 16)
            sc_.inc(sy_.dma_start(growB[0:1, :],
                                  din["F1_bp1"].rearrange("(o n) -> o n", o=1)), 16)
            sc_.inc(sy_.dma_start(lgrow[0:1, 0:NE],
                                  din["F1_bp2"].rearrange("(o n) -> o n", o=1)), 16)
            sc_.inc(sy_.dma_start(bsel[:, 0:FE], din["F1_b1"]), 16)
            sc_.inc(sy_.dma_start(bsel[:, FE:FE + C], din["F1_b2"]), 16)
            barrier()
            for ci in range(8):
                stt.inc(t_.matmul(PS[0:1, 0, 0:NAV], XT[:, ci, T - 1:T], wp1[:, ci, :],
                                  start=(ci == 0), stop=(ci == 7)))
                stt.inc(t_.matmul(PS[0:1, 1, 0:NAV], ones_c128[:, :], wp1[:, ci, :],
                                  start=(ci == 0), stop=(ci == 7)))
            v_.wait_ge(stt.h, stt.n)
            sv.inc(v_.tensor_tensor(consts[0:1, 5:6], consts[0:1, 2:3], consts[0:1, 3:4],
                                    op=ALU.mult))
            sv.inc(v_.drain())
            sv.inc(v_.tensor_scalar(growA[0:1, :], PS[0:1, 0, 0:NAV], consts[0:1, 3:4],
                                    None, op0=ALU.mult))
            sv.inc(v_.drain())
            sv.inc(v_.scalar_tensor_tensor(growA[0:1, :], PS[0:1, 1, 0:NAV],
                                           consts[0:1, 5:6], growA[0:1, :],
                                           op0=ALU.mult, op1=ALU.subtract))
            sv.inc(v_.drain())
            sv.inc(v_.tensor_tensor(growA[0:1, :], growB[0:1, :], growA[0:1, :],
                                    op=ALU.subtract))
            sv.inc(v_.drain())
            s_.wait_ge(sv.h, sv.n)
            ss.inc(s_.activation(growA[0:1, :], growA[0:1, :], AF.Gelu))
            ss.inc(s_.drain())
            t_.wait_ge(ss.h, ss.n)
            stt.inc(t_.transpose(PS[0:NAV, 2, 0:1], growA[0:1, :], ident_f[0:1, 0:1]))
            s_.wait_ge(stt.h, stt.n)
            ss.inc(s_.copy(gcol[0:NAV, :], PS[0:NAV, 2, 0:1]))
            t_.wait_ge(ss.h, ss.n)
            stt.inc(t_.matmul(PS[0:1, 3, 0:NE], gcol[0:NAV, :], wp2[0:NAV, :],
                              start=True, stop=True))
            v_.wait_ge(stt.h, stt.n)
            sv.inc(v_.tensor_tensor(lgrow[0:1, 8:8 + NE], PS[0:1, 3, 0:NE],
                                    lgrow[0:1, 0:NE], op=ALU.add))
            sv.inc(v_.drain())
            sv.inc(v_.reduce_max(consts[0:1, 6:7], lgrow[0:1, 8:8 + NE], axis=AX.X))
            sv.inc(v_.drain())
            sv.inc(v_.tensor_scalar(lgrow[0:1, 0:NE], lgrow[0:1, 8:8 + NE],
                                    consts[0:1, 6:7], None, op0=ALU.is_ge))
            sv.inc(v_.drain())
            t_.wait_ge(sv.h, sv.n)
            stt.inc(t_.matmul(PS[:, 4, 0:NE], ones_f1[:, :], lgrow[0:1, 0:NE],
                              start=True, stop=True))
            v_.wait_ge(stt.h, stt.n)
            sv.inc(v_.tensor_copy(ohb[:, :], PS[:, 4, 0:NE]))
            sv.inc(v_.drain())
            for e in range(NE):
                sv.inc(v_.tensor_scalar(de_f32[:, e, :], ident_f[:, :], ohb[:, e:e + 1],
                                        None, op0=ALU.mult))
            sv.inc(v_.drain())
            stt.inc(t_.transpose(PS[0:NE, 5, 0:1], lgrow[0:1, 0:NE], ident_f[0:1, 0:1]))
            s_.wait_ge(stt.h, stt.n)
            ss.inc(s_.copy(gcol[0:NE, :], PS[0:NE, 5, 0:1]))
            t_.wait_ge(ss.h, ss.n)
            t_.wait_ge(sv.h, sv.n)
            for j in range(4):
                stt.inc(t_.matmul(PS[0:1, 6, :], gcol[0:NE, :],
                                  bsel[:, j * 512:(j + 1) * 512], start=True, stop=True))
                stt.inc(t_.matmul(PS[0:1, 7, :], gcol[0:NE, :],
                                  bsel[:, FE + (j % 2) * 512:FE + (j % 2 + 1) * 512]
                                  if j < 2 else bsel[:, 0:512],
                                  start=True, stop=True))
                s_.wait_ge(stt.h, stt.n)
                for q in range(4):
                    ins = s_.copy(rowtmp[0:1, 0, q * 128:(q + 1) * 128],
                                  PS[0:1, 6, q * 128:(q + 1) * 128])
                ss.inc(ins)
                t_.wait_ge(ss.h, ss.n)
                for q in range(4):
                    stt.inc(t_.transpose(PS[0:128, 4, q:q + 1],
                                         rowtmp[0:1, 0, q * 128:(q + 1) * 128],
                                         ident_f[0:1, 0:1]))
                s_.wait_ge(stt.h, stt.n)
                for q in range(4):
                    ins = s_.copy(b1g[:, j * 4 + q:j * 4 + q + 1], PS[0:128, 4, q:q + 1])
                ss.inc(ins)
                if j < 2:
                    for q in range(4):
                        ins = s_.copy(rowtmp[0:1, 1, q * 128:(q + 1) * 128],
                                      PS[0:1, 7, q * 128:(q + 1) * 128])
                    ss.inc(ins)
                    t_.wait_ge(ss.h, ss.n)
                    for q in range(4):
                        stt.inc(t_.transpose(PS[0:128, 5, q:q + 1],
                                             rowtmp[0:1, 1, q * 128:(q + 1) * 128],
                                             ident_f[0:1, 0:1]))
                    s_.wait_ge(stt.h, stt.n)
                    for q in range(4):
                        ins = s_.copy(b1g[:, 16 + j * 4 + q:16 + j * 4 + q + 1],
                                      PS[0:128, 5, q:q + 1])
                    ss.inc(ins)
                t_.wait_ge(ss.h, ss.n)
            barrier()
            sg.inc(g_.dma_start(scr_wo, din["F1_Wo"]), 16)
            wseq = ([("w1", i, e) for i in range(8) for e in range(NE)] +
                    [("w2", i, e) for i in range(16) for e in range(NE)])
            wdma = {0: 0, 1: 0}
            wread = {0: 0, 1: 0}

            def w_issue(idx):
                k, i, e = wseq[idx]
                src = (din["F1_W1"][e, i * 128:(i + 1) * 128, :] if k == "w1"
                       else din["F1_W2"][e, i * 128:(i + 1) * 128, :])
                ncols = FE if k == "w1" else C
                q_, qs = (sy_, sd) if idx % 2 == 0 else (nc.scalar, sa)
                q_.wait_ge(stt.h, wread[idx % 2])
                q_.wait_ge(qs.h, qs.n)
                qs.inc(q_.dma_start(webf[idx % 2][:, 0:ncols], src), 16)
                wdma[idx % 2] = qs.n

            w_issue(0)
            w_issue(1)
            for dcc in range(8):
                for e in range(NE):
                    idx = dcc * NE + e
                    t_.wait_ge((sd if idx % 2 == 0 else sa).h, wdma[idx % 2])
                    if e == 0:
                        t_.wait_ge(ss.h, ss.n)
                    for fc in range(4):
                        ins = t_.matmul(PS[:, fc, :], de_f32[:, e, :],
                                        webf[idx % 2][:, fc * 512:(fc + 1) * 512],
                                        start=(e == 0), stop=(e == NE - 1))
                    stt.inc(ins)
                    wread[idx % 2] = stt.n
                    if idx + 2 < len(wseq):
                        w_issue(idx + 2)
                s_.wait_ge(stt.h, stt.n)
                s_.wait_ge(sw_.h, sw_.n)
                for fc in range(4):
                    ins = s_.activation(wst[:, fc, :], PS[:, fc, :], AF.Identity)
                ss.inc(ins)
                sy_.wait_ge(ss.h, ss.n)
                sw_.inc(sy_.dma_start(scr_w1s[dcc * 128:(dcc + 1) * 128, :],
                                      wst.rearrange("p a t -> p (a t)")), 16)
                t_.wait_ge(ss.h, ss.n)
            for fcc in range(16):
                for e in range(NE):
                    idx = 64 + fcc * NE + e
                    t_.wait_ge((sd if idx % 2 == 0 else sa).h, wdma[idx % 2])
                    if e == 0:
                        t_.wait_ge(ss.h, ss.n)
                    for fc in range(2):
                        ins = t_.matmul(PS[:, fc, :], de_f32[:, e, :],
                                        webf[idx % 2][:, fc * 512:(fc + 1) * 512],
                                        start=(e == 0), stop=(e == NE - 1))
                    stt.inc(ins)
                    wread[idx % 2] = stt.n
                    if idx + 2 < len(wseq):
                        w_issue(idx + 2)
                s_.wait_ge(stt.h, stt.n)
                s_.wait_ge(sw_.h, sw_.n)
                for fc in range(2):
                    ins = s_.activation(wst[:, fc, :], PS[:, fc, :], AF.Identity)
                ss.inc(ins)
                sy_.wait_ge(ss.h, ss.n)
                sw_.inc(sy_.dma_start(scr_w2s[fcc * 128:(fcc + 1) * 128, :],
                                      wst[:, 0:2, :].rearrange("p a t -> p (a t)")), 16)
                t_.wait_ge(ss.h, ss.n)
            barrier()
        with contextlib.ExitStack() as ph:
            w1s = [ph.enter_context(sbt(f"w1s{i}", [128, 8, 512], BF16)) for i in range(2)]
            w2h = [ph.enter_context(sbt(f"w2h{i}", [128, 8, 128], BF16))
                   for i in range(2)]
            woh = [ph.enter_context(sbt(f"woh{i}", [128, 4, 128], BF16))
                   for i in range(2)]
            hT = ph.enter_context(sbt("hT", [128, 16, 512], BF16))
            rT_ = ph.enter_context(sbt("rT_", [128, 8, 512], BF16))
            xls = ph.enter_context(sbt("xls_f1", [128, 8, 512], BF16))
            t1b = ph.enter_context(sbt("t1b", [128, 512], F32))
            t2b = ph.enter_context(sbt("t2b", [128, 512], F32))
            barrier()
            w2d = {0: 0, 1: 0}
            w2r = {0: 0, 1: 0}
            wod = {0: 0, 1: 0}
            wor = {0: 0, 1: 0}

            def w2_issue(hh):
                hm = hh % 16
                cc2, hf2 = hm // 2, hm % 2
                q_, qs = (sy_, sd) if hh % 2 == 0 else (nc.scalar, sa)
                q_.wait_ge(stt.h, w2r[hh % 2])
                q_.wait_ge(qs.h, qs.n)
                qs.inc(q_.dma_start(
                    w2h[hh % 2][:, :, :],
                    scr_w2s.rearrange("(j p) c -> p j c", p=128)
                    [:, hf2 * 8:hf2 * 8 + 8, cc2 * 128:(cc2 + 1) * 128]), 16)
                w2d[hh % 2] = qs.n

            def wo_issue(hh):
                hm = hh % 16
                jc2, hf2 = hm // 2, hm % 2
                g_.wait_ge(stt.h, wor[hh % 2])
                g_.wait_ge(sg.h, sg.n)
                sg.inc(g_.dma_start(
                    woh[hh % 2][:, :, :],
                    scr_wo.rearrange("(k p) c -> p k c", p=128)
                    [:, hf2 * 4:hf2 * 4 + 4, jc2 * 128:(jc2 + 1) * 128]), 16)
                wod[hh % 2] = sg.n

            w2_issue(0)
            w2_issue(1)
            wo_issue(0)
            wo_issue(1)
            for tsub in range(NS):
                tl = slice(tsub * 512, (tsub + 1) * 512)
                sy_.wait_ge(stt.h, stt.n)
                sy_.wait_ge(sv.h, sv.n)
                sy_.wait_ge(sd.h, sd.n)
                srcx = scr_xln.rearrange("(k p) t -> p k t", p=128)[:, :, tl]
                sd.inc(sy_.dma_start(xls[:, :, :], srcx), 16)
                t_.wait_ge(sd.h, sd.n)
                evd = [0, 0]
                w1_ready = [0, 0]
                for fq in range(4):
                    wbuf = w1s[fq % 2]
                    sy_.wait_ge(stt.h, w1_ready[fq % 2])
                    sy_.wait_ge(sd.h, sd.n)
                    sd.inc(sy_.dma_start(
                        wbuf[:, :, :],
                        scr_w1s.rearrange("(k p) f -> p k f", p=128)[:, :, fq * 512:(fq + 1) * 512]),
                        16)
                    t_.wait_ge(sd.h, sd.n)
                    for f4 in range(4):
                        fcc = fq * 4 + f4
                        b = fcc % 2
                        t_.wait_ge(ss.h, evd[b])
                        for ci in range(8):
                            ins = t_.matmul(bank(b), wbuf[:, ci, f4 * 128:(f4 + 1) * 128],
                                            xls[:, ci, :], start=(ci == 0), stop=(ci == 7))
                        stt.inc(ins)
                        s_.wait_ge(stt.h, stt.n)
                        ss.inc(s_.activation(hT[:, fcc, :], bank(b), AF.Gelu,
                                             bias=b1g[:, fcc:fcc + 1]))
                        evd[b] = ss.n
                    w1_ready[fq % 2] = stt.n
                rTev = {2: 0, 3: 0}
                for cc in range(8):
                    b = 2 + cc % 2
                    for hf2 in range(2):
                        hh = tsub * 16 + cc * 2 + hf2
                        t_.wait_ge((sd if hh % 2 == 0 else sa).h, w2d[hh % 2])
                        if hf2 == 0:
                            t_.wait_ge(ss.h, ss.n if cc == 0 else rTev[b])
                        for fc in range(8):
                            ins = t_.matmul(bank(b), w2h[hh % 2][:, fc, :],
                                            hT[:, hf2 * 8 + fc, :],
                                            start=(hf2 == 0 and fc == 0),
                                            stop=(hf2 == 1 and fc == 7))
                        stt.inc(ins)
                        w2r[hh % 2] = stt.n
                        if hh + 2 < NS * 16:
                            w2_issue(hh + 2)
                    s_.wait_ge(stt.h, stt.n)
                    ss.inc(s_.activation(rT_[:, cc, :], bank(b), AF.Identity,
                                         bias=b1g[:, 16 + cc:17 + cc]))
                    rTev[b] = ss.n
                outev = {4: 0, 5: 0}
                for jc in range(8):
                    b = 4 + jc % 2
                    for hf2 in range(2):
                        hh = tsub * 16 + jc * 2 + hf2
                        t_.wait_ge(sg.h, wod[hh % 2])
                        if hf2 == 0:
                            if jc == 0:
                                t_.wait_ge(sv.h, sv.n)
                                t_.wait_ge(ss.h, ss.n)
                            else:
                                t_.wait_ge(ss.h, outev[b])
                        for hc in range(4):
                            ins = t_.matmul(bank(b), woh[hh % 2][:, hc, :],
                                            rT_[:, hf2 * 4 + hc, :],
                                            start=(hf2 == 0 and hc == 0),
                                            stop=(hf2 == 1 and hc == 3))
                        stt.inc(ins)
                        wor[hh % 2] = stt.n
                        if hh + 2 < NS * 16:
                            wo_issue(hh + 2)
                    s_.wait_ge(stt.h, stt.n)
                    s_.wait_ge(sv.h, sv.n)
                    ss.inc(s_.activation(t1b[:, :], bank(b), AF.Identity, scale=0.3,
                                         bias=bias_slice("F1_bo_s3", jc)))
                    outev[b] = ss.n
                    v_.wait_ge(ss.h, ss.n)
                    sv.inc(v_.tensor_tensor(t2b[:, :], t1b[:, :], xls[:, jc, :],
                                            op=ALU.add))
                    sv.inc(v_.drain())
                    sv.inc(v_.tensor_tensor(XT[:, jc, tl], XT[:, jc, tl], t2b[:, :],
                                            op=ALU.add))
                    sv.inc(v_.drain())
            barrier()

    def scoped(name, fn):
        with nc.named_scope(name):
            fn()

    for ph_name in phases:
        if ph_name == "dbg_qw":
            dq1 = nc.dram_tensor("dbg_qw1", [C, FF], F32, kind="ExternalOutput").ap()
            dq2 = nc.dram_tensor("dbg_qw2", [FF, C], F32, kind="ExternalOutput").ap()
            quant_weight_prep(din["F0_w1"], FF, C, scr_qw1, 250)
            quant_weight_prep(din["F0_w2"], C, FF, scr_qw2, 251)
            barrier()
            with sbt("dbgbuf", [128, FF], BF16) as db, sbt("dbgbuff", [128, FF], F32) as dbf:
                for rr in range(8):
                    sy_.wait_ge(sd.h, sd.n)
                    sy_.wait_ge(sv.h, sv.n)
                    sd.inc(sy_.dma_start(db[:, :], scr_qw1[rr * 128:(rr + 1) * 128, :]), 16)
                    v_.wait_ge(sd.h, sd.n)
                    sv.inc(v_.tensor_copy(dbf[:, :], db[:, :]))
                    sv.inc(v_.drain())
                    sy_.wait_ge(sv.h, sv.n)
                    sd.inc(sy_.dma_start(dq1[rr * 128:(rr + 1) * 128, :], dbf[:, :]), 16)
                for rr in range(32):
                    sy_.wait_ge(sd.h, sd.n)
                    sy_.wait_ge(sv.h, sv.n)
                    sd.inc(sy_.dma_start(db[:, 0:C], scr_qw2[rr * 128:(rr + 1) * 128, :]), 16)
                    v_.wait_ge(sd.h, sd.n)
                    sv.inc(v_.tensor_copy(dbf[:, 0:C], db[:, 0:C]))
                    sv.inc(v_.drain())
                    sy_.wait_ge(sv.h, sv.n)
                    sd.inc(sy_.dma_start(dq2[rr * 128:(rr + 1) * 128, :], dbf[:, 0:C]), 16)
            sy_.wait_ge(sd.h, sd.n)
        elif ph_name == "l0a":
            scoped("ln0", ln_and_xln); scoped("mla", mla_attn)
        elif ph_name == "f0":
            scoped("ln1", ln_and_xln); scoped("f0", f0_ffn)
        elif ph_name == "l1a":
            scoped("ln2", ln_and_xln); scoped("qkv1", mha_qkv)
            scoped("mha", lambda: mha_attn_core("L1_Wo", "L1_bo", "L1_bv"))
        elif ph_name == "f1":
            scoped("ln3", ln_and_xln); scoped("f1", f1_ffn)
    scoped("epi", epilogue)

    st.close()
    return nc, din


def kernel(**inputs):
    T = inputs["L0_attn_x"].shape[1]
    nc, _ = build(T=T)
    ident = np.eye(128, dtype=np.float32)
    base = {
        "ident": ident,
        "ident_bf": ident.astype(ml_dtypes.bfloat16),
        "ident_h": ident.astype(np.float16),
        "iota8": np.arange(8, dtype=np.float32)[None],
        "L0_wq_d": inputs["L0_attn_wq_d"], "L0_bq_d": inputs["L0_attn_bq_d"],
        "L0_w_qk": inputs["L0_attn_w_qk"], "L0_b_qk": inputs["L0_attn_b_qk"],
        "L0_wkv_d": inputs["L0_attn_wkv_d"], "L0_bkv_d": inputs["L0_attn_bkv_d"],
        "L0_wv_u": inputs["L0_attn_wv_u"], "L0_bv_u": inputs["L0_attn_bv_u"],
        "L0_w_proj": inputs["L0_attn_w_proj"], "L0_b_proj": inputs["L0_attn_b_proj"],
        "F0_ln1_g": inputs["L0_ffn_ln1_g"], "F0_ln1_b": inputs["L0_ffn_ln1_b"],
        "F0_w1": inputs["L0_ffn_w1"], "F0_b1": inputs["L0_ffn_b1"],
        "F0_ln2_g": inputs["L0_ffn_ln2_g"], "F0_ln2_b": inputs["L0_ffn_ln2_b"],
        "F0_w2": inputs["L0_ffn_w2"], "F0_b2": inputs["L0_ffn_b2"],
        "L1_Wq": inputs["L1_attn_Wq"], "L1_bq": inputs["L1_attn_bq"],
        "L1_Wk": inputs["L1_attn_Wk"], "L1_bk": inputs["L1_attn_bk"],
        "L1_Wv": inputs["L1_attn_Wv"], "L1_bv": inputs["L1_attn_bv"],
        "L1_Wo": inputs["L1_attn_Wo"], "L1_bo": inputs["L1_attn_bo"],
        "F1_Wp1": inputs["L1_ffn_Wp1"], "F1_bp1": inputs["L1_ffn_bp1"],
        "F1_Wp2": inputs["L1_ffn_Wp2"], "F1_bp2": inputs["L1_ffn_bp2"],
        "F1_W1": inputs["L1_ffn_W1"], "F1_b1": inputs["L1_ffn_b1"],
        "F1_W2": inputs["L1_ffn_W2"], "F1_b2": inputs["L1_ffn_b2"],
        "F1_Wo": inputs["L1_ffn_Wo"], "F1_bo": inputs["L1_ffn_bo"],
    }
    x = np.ascontiguousarray(inputs["L0_attn_x"])
    in_maps = [dict(base, x=np.ascontiguousarray(x[b])) for b in range(x.shape[0])]
    res = run_bass_kernel_spmd(nc, in_maps, core_ids=list(range(x.shape[0])))
    return np.stack([r["out"] for r in res.results], axis=0)
